# version 03
# baseline (speedup 1.0000x reference)
import contextlib
import numpy as np
import concourse.bass as bass
import concourse.mybir as mybir
from concourse.bass_utils import run_bass_kernel_spmd

F32 = mybir.dt.float32
BF16 = mybir.dt.bfloat16
AF = mybir.ActivationFunctionType
ALU = mybir.AluOpType
AX = mybir.AxisListType.X

B, N, DIM = 2, 8192, 1024
H, HD, R, QK = 16, 64, 8, 32
HID = 2816
NE = 8
NT = 64
OWN_NT = 16
EPS1 = 1e-5
EPS2 = 1e-6
RS32 = float(1.0 / np.sqrt(32.0))

PHASES = 4
P0_STEPS = 99
P1_TILES = NT
DEBUG = {}

ENG = ("gpsimd", "vector", "scalar", "tensor", "sync")


CHAINED_OPS = frozenset([
    "memset", "memzero", "tensor_copy", "tensor_tensor", "tensor_add",
    "tensor_sub", "tensor_mul", "tensor_max", "tensor_scalar",
    "tensor_single_scalar", "tensor_scalar_add", "tensor_scalar_sub",
    "tensor_scalar_mul", "tensor_scalar_max", "tensor_scalar_min",
    "tensor_relu", "tensor_reduce", "reduce_max", "reduce_sum",
    "scalar_tensor_tensor", "reciprocal", "select", "copy_predicated",
    "bn_stats", "bn_aggr", "activation", "add", "mul", "copy", "square",
    "sqrt", "sign", "affine_select", "iota", "stream_shuffle",
])
_IN_KWS = ("bias", "scale", "alpha", "scalar1", "scalar2", "in_", "in0", "in1", "mask")


def _ap_name(a):
    try:
        return a.tensor.name
    except AttributeError:
        return None


class AutoChain:

    def __init__(self, eng, state, chain_sem):
        self._eng = eng
        self._st = state
        self._sem = chain_sem

    def note_update(self, inst, sem, val):
        rec = self._st["insts"].get(id(inst.ins))
        if rec is not None and rec[1] is None:
            self._st["insts"][id(inst.ins)] = (rec[0], (sem, val))

    def _edge(self, key):
        st = self._st
        rec = st["insts"].get(key)
        if rec is None:
            return
        idx, upd = rec
        if idx <= st["synced"]:
            return
        if upd is None:
            st["chain_n"] = st.get("chain_n", 0) + 1
            st["hinst"][key].then_inc(self._sem, 1)
            upd = (self._sem, st["chain_n"])
            st["insts"][key] = (idx, upd)
        self._eng.wait_ge(upd[0], upd[1])
        st["synced"] = max(st["synced"], idx)

    def __getattr__(self, name):
        fn = getattr(self._eng, name)
        if name not in CHAINED_OPS:
            return fn
        st = self._st

        def wrapped(*args, **kw):
            outs, ins = [], []
            seen_out = False
            for a in args:
                nm = _ap_name(a)
                if nm is None:
                    continue
                if not seen_out:
                    outs.append(nm)
                    seen_out = True
                else:
                    ins.append(nm)
            for k, v in kw.items():
                nm = _ap_name(v)
                if nm is None:
                    continue
                if k == "accum_out" or k == "out":
                    outs.append(nm)
                elif k in _IN_KWS:
                    ins.append(nm)
            for nm in ins:
                w = st["writer"].get(nm)
                if w is not None:
                    self._edge(w)
            for nm in outs:
                w = st["writer"].get(nm)
                if w is not None:
                    self._edge(w)
                for r in st["readers"].get(nm, ()):
                    self._edge(r)
            inst = fn(*args, **kw)
            st["idx"] = st.get("idx", 0) + 1
            key = id(inst.ins)
            st["insts"][key] = (st["idx"], None)
            st["hinst"][key] = inst
            for nm in outs:
                st["writer"][nm] = key
                st["readers"][nm] = []
            for nm in ins:
                st["readers"].setdefault(nm, []).append(key)
            return inst

        return wrapped


class Ctx:
    def __init__(self, eng, which, counts):
        self.eng = eng
        self.w = which
        self.n = counts

    def on(self, name):
        return self.w == name

    def bump(self, key, amt=1):
        self.n[key] = self.n.get(key, 0) + amt
        return self.n[key]

    def cnt(self, key):
        return self.n.get(key, 0)


def build_program():
    nc = bass.Bass("TRN2", target_bir_lowering=False)

    xb = nc.dram_tensor("x_b", [N, DIM], F32, kind="ExternalInput").ap()
    wqT = nc.dram_tensor("wqT", [DIM, DIM], F32, kind="ExternalInput").ap()
    wkT = nc.dram_tensor("wkT", [DIM, DIM], F32, kind="ExternalInput").ap()
    wvT = nc.dram_tensor("wvT", [DIM, R * HD], F32, kind="ExternalInput").ap()
    wqv = nc.dram_tensor("wqv", [QK * H, DIM], F32, kind="ExternalInput").ap()
    wsT = nc.dram_tensor("wsT", [HD + QK, 1], F32, kind="ExternalInput").ap()
    bqvT = nc.dram_tensor("bqvT", [QK, H], F32, kind="ExternalInput").ap()
    wfT = nc.dram_tensor("wfT", [DIM, DIM], F32, kind="ExternalInput").ap()
    wr1T = nc.dram_tensor("wr1T", [DIM, 2 * DIM], F32, kind="ExternalInput").ap()
    wr2T = nc.dram_tensor("wr2T", [2 * DIM, DIM], F32, kind="ExternalInput").ap()
    routerT = nc.dram_tensor("routerT", [DIM, NE], F32, kind="ExternalInput").ap()
    guT = nc.dram_tensor("guT", [NE, DIM, 2 * HID], F32, kind="ExternalInput").ap()
    dnT = nc.dram_tensor("dnT", [NE, HID, DIM], F32, kind="ExternalInput").ap()
    bq_d = nc.dram_tensor("bq", [DIM], F32, kind="ExternalInput").ap()
    bk_d = nc.dram_tensor("bk", [DIM], F32, kind="ExternalInput").ap()
    bv_d = nc.dram_tensor("bv", [R * HD], F32, kind="ExternalInput").ap()
    bs_d = nc.dram_tensor("bs", [1], F32, kind="ExternalInput").ap()
    bf_d = nc.dram_tensor("bf", [DIM], F32, kind="ExternalInput").ap()
    br1_d = nc.dram_tensor("br1", [2 * DIM], F32, kind="ExternalInput").ap()
    br2_d = nc.dram_tensor("br2", [DIM], F32, kind="ExternalInput").ap()
    g1_d = nc.dram_tensor("g1", [DIM], F32, kind="ExternalInput").ap()
    b1_d = nc.dram_tensor("b1", [DIM], F32, kind="ExternalInput").ap()
    g2_d = nc.dram_tensor("g2", [DIM], F32, kind="ExternalInput").ap()
    b2_d = nc.dram_tensor("b2", [DIM], F32, kind="ExternalInput").ap()
    nw_d = nc.dram_tensor("nw", [DIM], F32, kind="ExternalInput").ap()
    alpha_d = nc.dram_tensor("alpha", [1], F32, kind="ExternalInput").ap()

    y = nc.dram_tensor("y", [OWN_NT * 128, DIM], F32, kind="ExternalOutput").ap()
    kt_hbm = nc.dram_tensor("kt_scratch", [DIM, N], BF16, kind="Internal").ap()
    x1_hbm = nc.dram_tensor("x1_scratch", [OWN_NT * 128, DIM], F32, kind="Internal").ap()

    dbg = {}
    dbg_specs = {"kt": ([DIM, N], BF16), "v": ([128, NT * 512], BF16),
                 "qt": ([DIM, 2048], BF16), "qvd": ([128, OWN_NT * 16], F32),
                 "oft": ([DIM, 2048], BF16), "x1": ([OWN_NT * 128, DIM], F32),
                 "nrm": ([DIM, 2048], BF16), "cmb": ([128, OWN_NT * 8], F32)}
    for k_, (shape, dt) in dbg_specs.items():
        if DEBUG.get(k_):
            dbg[k_] = nc.dram_tensor("dbg_" + k_, shape, dt, kind="ExternalOutput").ap()

    CNTS = {w: {} for w in ENG}
    MS = {}

    ES = contextlib.ExitStack()
    with ES:
        def sbuf(name, shape, dt, side="left"):
            return ES.enter_context(nc.sbuf_tensor(name, shape, dt, side=side))

        s_pe = ES.enter_context(nc.semaphore("s_pe"))
        s_ve = ES.enter_context(nc.semaphore("s_ve"))
        s_sc = ES.enter_context(nc.semaphore("s_sc"))
        s_gp = ES.enter_context(nc.semaphore("s_gp"))
        ph = ES.enter_context(nc.semaphore("ph"))
        stOF = contextlib.ExitStack()
        PH = [0]
        SEM = {"pe": s_pe, "ve": s_ve, "sc": s_sc, "gp": s_gp}
        KEY = {"tensor": "pe", "vector": "ve", "scalar": "sc", "gpsimd": "gp"}
        DSTREAMS = ["dx0", "dx1", "dk0", "dk1", "dk2", "dk3", "kt0", "kt1",
                    "dx30", "dx31", "x1s0", "x1s1", "x1l0", "x1l1",
                    "yst0", "yst1", "dw", "dq0", "dq1", "dc",
                    "dgu0", "dgu1", "dn", "dbg"]
        for _s in DSTREAMS:
            SEM[_s] = ES.enter_context(nc.semaphore("sd_" + _s))

        CH_SEM = {w: ES.enter_context(nc.semaphore("ch_" + w))
                  for w in ("gpsimd", "vector", "scalar")}
        CH_ST = {w: {"idx": 0, "chain_n": 0, "synced": 0, "writer": {},
                     "readers": {}, "insts": {}, "hinst": {}}
                 for w in ("gpsimd", "vector", "scalar")}

        def mkeng(e, w):
            if w in CH_SEM:
                return AutoChain(e, CH_ST[w], CH_SEM[w])
            return e

        def run_block(name, emit):
            with nc.Block(name) as blk:
                blk.gpsimd(lambda e: emit(Ctx(mkeng(e, "gpsimd"), "gpsimd", CNTS["gpsimd"])))
                blk.vector(lambda e: emit(Ctx(mkeng(e, "vector"), "vector", CNTS["vector"])))
                blk.scalar(lambda e: emit(Ctx(mkeng(e, "scalar"), "scalar", CNTS["scalar"])))
                blk.tensor(lambda e: emit(Ctx(e, "tensor", CNTS["tensor"])))
                blk.sync(lambda e: emit(Ctx(e, "sync", CNTS["sync"])))
            PH[0] += 5

        def producer(P, engname, last_inst):
            k = KEY[engname]
            v = P.bump(k)
            if P.on(engname) and last_inst is not None:
                last_inst.then_inc(SEM[k], 1)
                if isinstance(P.eng, AutoChain):
                    P.eng.note_update(last_inst, SEM[k], v)
            return v

        def dma(P, qname, stream, dst, src):
            v = P.bump(stream, 16)
            if P.on(qname):
                P.eng.dma_start(dst, src).then_inc(SEM[stream], 16)
            return v

        def wait(P, engname, semkey, val):
            if P.on(engname) and val > 0:
                assert val <= P.cnt(semkey), f"future wait {semkey} {val}>{P.cnt(semkey)}"
                P.eng.wait_ge(SEM[semkey], val)

        def phase_begin(P):
            if PH[0] > 0:
                P.eng.wait_ge(ph, PH[0])

        def phase_end(P, extra=()):
            for semkey, v in extra:
                if v > 0:
                    P.eng.wait_ge(SEM[semkey], v)
            P.eng.sem_inc(ph, 1)

        ident = sbuf("ident", [128, 128], BF16, side="right")
        ident_f = sbuf("ident_f", [16, 16], F32, side="right")
        ones_row = sbuf("ones_row", [1, 128], BF16, side="right")
        ones_tok = sbuf("ones_tok", [1, 512], BF16, side="right")
        ones_col = sbuf("ones_col", [128, 1], BF16, side="right")
        ones_f = sbuf("ones_f", [1, 128], F32, side="right")
        wsv_bc = sbuf("wsv_bc", [128, 512], BF16, side="right")
        bs128 = sbuf("bs128", [128, 1], F32, side="right")
        alpha128 = sbuf("alpha128", [128, 1], F32, side="right")
        qvd_sb = sbuf("qvd", [128, OWN_NT * 16], F32, side="right")
        of_T = None

        ACT_CONSTS = [float(EPS1), 1.0, 0.0, -0.5, 0.125, RS32,
                      float(EPS2 * DIM), float(0.5 * np.log(DIM)), -1e9]
        consts_sb = sbuf("consts_sb", [128, len(ACT_CONSTS)], F32, side="right")
        for i_, v_ in enumerate(ACT_CONSTS):
            nc.const_aps.register(F32, v_, consts_sb[:, i_:i_ + 1])

        st12 = contextlib.ExitStack()
        with st12:
            def sb12(name, shape, dt):
                return st12.enter_context(nc.sbuf_tensor(name, shape, dt))

            v_all = sb12("v_all", [128, NT * 512], BF16)
            qt_sb = [sb12(f"qt{i}", [128, 2048], BF16) for i in range(8)]

            st01 = contextlib.ExitStack()
            with st01:
                def sb01(name, shape, dt):
                    return st01.enter_context(nc.sbuf_tensor(name, shape, dt))

                def ps01(name, shape, dt):
                    return st01.enter_context(nc.psum_tensor(name, shape, dt))

                wq_sb = sb01("wq_sb", [128, 8 * DIM], BF16)
                wk_sb = sb01("wk_sb", [128, 8 * DIM], BF16)
                wv_sb = sb01("wv_sb", [128, 8 * 512], BF16)
                wcT = sb01("wcT", [128, 8 * 16], BF16)
                bkR = sb01("bkR", [1, DIM], BF16)
                bqR = sb01("bqR", [1, DIM], BF16)
                bvR = sb01("bvR", [1, 512], BF16)
                g1bc = sb01("g1bc", [128, DIM], F32)
                b1bc = sb01("b1bc", [128, DIM], F32)
                bqvd = sb01("bqvd", [16, 1], F32)

                st00 = contextlib.ExitStack()
                st00.__enter__()

                def sb00(name, shape, dt):
                    return st00.enter_context(nc.sbuf_tensor(name, shape, dt))

                wsvrow = sb00("wsvrow", [1, 512], F32)
                wqv_blk = [sb00(f"wqv_blk{i}", [32, DIM], BF16) for i in range(2)]
                wsq_dup = sb00("wsq_dup", [64, 1], BF16)
                g1row = sb00("g1row", [1, DIM], F32)
                b1row = sb00("b1row", [1, DIM], F32)
                bsrow = sb00("bsrow", [1, 1], F32)
                alrow = sb00("alrow", [1, 1], F32)
                alclip = sb00("alclip", [1, 1], F32)
                wsvr64 = sb00("wsvr64", [1, 64], F32)
                wc_row = sb00("wc_row", [1, DIM], F32)
                wcomb16 = sb00("wcomb16", [16, DIM], BF16)
                bqvT_bf = sb00("bqvT_bf", [QK, H], BF16)
                bqv_ev = sb00("bqv_ev", [1, 16], F32)

                ps_a = st00.enter_context(nc.psum_tensor("ps_a", [128, 512], F32))
                ps_b = st00.enter_context(nc.psum_tensor("ps_b", [128, 512], F32))
                ps_c = st00.enter_context(nc.psum_tensor("ps_c", [128, 512], BF16))

                def emit_p0(P):
                    e = P.eng
                    phase_begin(P)
                    if P.on("gpsimd"):
                        e.memset(ident[:], 0.0)
                        e.memset(ident_f[:], 0.0)
                        e.affine_select(out=ident[:], in_=ident[:], compare_op=ALU.not_equal,
                                        fill=1.0, base=0, pattern=[[-1, 128]],
                                        channel_multiplier=1)
                        e.affine_select(out=ident_f[:], in_=ident_f[:], compare_op=ALU.not_equal,
                                        fill=1.0, base=0, pattern=[[-1, 16]],
                                        channel_multiplier=1)
                    if P.on("gpsimd"):
                        e.memset(ones_row[:], 1.0)
                        e.memset(ones_tok[:], 1.0)
                        e.memset(ones_col[:], 1.0)
                        for i_, v_ in enumerate(ACT_CONSTS):
                            e.memset(consts_sb[:, i_:i_ + 1], v_)
                    last = None
                    if P.on("gpsimd"):
                        last = e.memset(ones_f[:], 1.0)
                    MS["p0_const"] = producer(P, "gpsimd", last)

                    if P0_STEPS < 2:
                        P.eng.sem_inc(ph, 1)
                        return
                    dmas = []
                    for kk in range(8):
                        dmas.append((wk_sb[:, kk * DIM:(kk + 1) * DIM], wkT[kk * 128:(kk + 1) * 128, :]))
                        dmas.append((wq_sb[:, kk * DIM:(kk + 1) * DIM], wqT[kk * 128:(kk + 1) * 128, :]))
                        dmas.append((wv_sb[:, kk * 512:(kk + 1) * 512], wvT[kk * 128:(kk + 1) * 128, :]))
                    dmas += [(wsq_dup[0:32, :], wsT[0:32, :]),
                             (wsq_dup[32:64, :], wsT[0:32, :]),
                             (wsvr64[:], wsT[32:96, 0][None, :]),
                             (bqvT_bf[:], bqvT),
                             (bkR[:], bk_d[None, :]),
                             (bqR[:], bq_d[None, :]),
                             (bvR[:], bv_d[None, :]),
                             (g1row[:], g1_d[None, :]),
                             (b1row[:], b1_d[None, :]),
                             (bsrow[:], bs_d[None, :]),
                             (alrow[:], alpha_d[None, :])]
                    for dst, src in dmas:
                        MS["p0_dma"] = dma(P, "gpsimd", "dw", dst, src)

                    if P0_STEPS < 3:
                        if P.on("sync"):
                            P.eng.wait_ge(SEM["dw"], P.cnt("dw"))
                            P.eng.wait_ge(s_gp, P.cnt("gp"))
                        P.eng.sem_inc(ph, 1)
                        return
                    wait(P, "vector", "dw", MS["p0_dma"])
                    last = None
                    if P.on("vector"):
                        e.tensor_scalar(alclip[:], alrow[:], -0.5, 2.0, ALU.max, ALU.min)
                        for r_ in range(8):
                            e.tensor_copy(wsvrow[0:1, r_ * 64:(r_ + 1) * 64], wsvr64[:])
                        last = e.memset(wc_row[:], 0.0)
                    MS["p0_vsm"] = producer(P, "vector", last)

                    if P0_STEPS < 4:
                        if P.on("sync"):
                            P.eng.wait_ge(SEM["dw"], P.cnt("dw"))
                            P.eng.wait_ge(s_ve, P.cnt("ve"))
                        P.eng.sem_inc(ph, 1)
                        return
                    for half in range(2):
                        wait(P, "tensor", "dw", MS["p0_dma"])
                        wait(P, "tensor", "ve", MS["p0_vsm"] if half == 0 else MS["p0_gb"])
                        last = None
                        if P.on("tensor"):
                            e.matmul(ps_a[:], ones_f[:], g1row[0:1, half * 512:(half + 1) * 512],
                                     start=True, stop=True)
                            last = e.matmul(ps_b[:], ones_f[:], b1row[0:1, half * 512:(half + 1) * 512],
                                            start=True, stop=True)
                        vpe = producer(P, "tensor", last)
                        wait(P, "vector", "pe", vpe)
                        last = None
                        if P.on("vector"):
                            e.tensor_copy(g1bc[:, half * 512:(half + 1) * 512], ps_a[:])
                            last = e.tensor_copy(b1bc[:, half * 512:(half + 1) * 512], ps_b[:])
                        MS["p0_gb"] = producer(P, "vector", last)
                    wait(P, "tensor", "ve", MS["p0_gb"])
                    last = None
                    if P.on("tensor"):
                        e.matmul(ps_a[:], ones_f[:], wsvrow[:], start=True, stop=True)
                        e.matmul(ps_b[:, 0:1], ones_f[:], bsrow[:], start=True, stop=True)
                        last = e.matmul(ps_b[:, 2:3], ones_f[:], alclip[:], start=True, stop=True)
                    vpe = producer(P, "tensor", last)
                    wait(P, "vector", "pe", vpe)
                    last = None
                    if P.on("vector"):
                        e.tensor_copy(wsv_bc[:], ps_a[:])
                        e.tensor_copy(bs128[:], ps_b[:, 0:1])
                        last = e.tensor_copy(alpha128[:], ps_b[:, 2:3])
                    MS["p0_bc2"] = producer(P, "vector", last)

                    if P0_STEPS < 5:
                        if P.on("sync"):
                            P.eng.wait_ge(SEM["dw"], P.cnt("dw"))
                            P.eng.wait_ge(s_ve, P.cnt("ve"))
                        P.eng.sem_inc(ph, 1)
                        return
                    vWQM = {}
                    for h in range(16):
                        hb = h % 2
                        if P.on("gpsimd") and h >= 2:
                            e.wait_ge(s_pe, vWQM[h - 2])
                        vWQL = dma(P, "gpsimd", f"dq{hb}", wqv_blk[hb][:], wqv[h * 32:(h + 1) * 32, :])
                        wait(P, "tensor", "ve", P.cnt("ve"))
                        wait(P, "tensor", f"dq{hb}", vWQL)
                        last = None
                        if P.on("tensor"):
                            e.matmul(ps_b[0:1, 0:512], wsq_dup[0:32, :],
                                     wqv_blk[hb][:, 0:512], start=True, stop=True)
                            last = e.matmul(ps_a[0:1, 0:512], wsq_dup[0:32, :],
                                            wqv_blk[hb][:, 512:DIM], start=True, stop=True)
                        vpe = producer(P, "tensor", last)
                        vWQM[h] = vpe
                        wait(P, "vector", "pe", vpe)
                        if P.on("vector") and h >= 1:
                            e.wait_ge(SEM["dc"], P.cnt("dc"))
                        last = None
                        if P.on("vector"):
                            e.tensor_copy(wc_row[0:1, 0:512], ps_b[0:1, 0:512])
                            last = e.tensor_copy(wc_row[0:1, 512:DIM], ps_a[0:1, 0:512])
                        vve = producer(P, "vector", last)
                        wait(P, "gpsimd", "ve", vve)
                        MS["p0_wc"] = dma(P, "gpsimd", "dc", wcomb16[h:h + 1, :], wc_row[:])
                    if P0_STEPS < 6:
                        if P.on("sync"):
                            P.eng.wait_ge(SEM["dc"], P.cnt("dc"))
                            P.eng.wait_ge(s_ve, P.cnt("ve"))
                        P.eng.sem_inc(ph, 1)
                        return
                    for kk in range(8):
                        if kk == 0:
                            wait(P, "tensor", "dc", MS["p0_wc"])
                        else:
                            wait(P, "tensor", "ve", P.cnt("ve"))
                        last = None
                        if P.on("tensor"):
                            last = e.transpose(ps_c[:, 0:16], wcomb16[:, kk * 128:(kk + 1) * 128],
                                               ident[0:16, 0:16])
                        vpe = producer(P, "tensor", last)
                        wait(P, "vector", "pe", vpe)
                        last = None
                        if P.on("vector"):
                            last = e.tensor_copy(wcT[:, kk * 16:(kk + 1) * 16], ps_c[:, 0:16])
                        producer(P, "vector", last)
                    wait(P, "tensor", "ve", P.cnt("ve"))
                    last = None
                    if P.on("tensor"):
                        last = e.matmul(ps_a[0:1, 0:16], wsq_dup[0:32, :], bqvT_bf[0:32, :],
                                        start=True, stop=True)
                    vpe = producer(P, "tensor", last)
                    wait(P, "vector", "pe", vpe)
                    last = None
                    if P.on("vector"):
                        last = e.tensor_copy(bqv_ev[:], ps_a[0:1, 0:16])
                    vve = producer(P, "vector", last)
                    for h in range(16):
                        wait(P, "gpsimd", "ve", vve)
                        MS["p0_bqv"] = dma(P, "gpsimd", "dc", bqvd[h:h + 1, :], bqv_ev[0:1, h:h + 1])
                    wait(P, "vector", "dc", MS["p0_bqv"])
                    last = None
                    if P.on("vector"):
                        last = e.tensor_scalar(bqvd[:], bqvd[:], RS32, None, ALU.mult)
                    producer(P, "vector", last)
                    if P.on("sync"):
                        P.eng.wait_ge(s_ve, P.cnt("ve"))
                        for _s in ("dw", "dq", "dc"):
                            if P.cnt(_s) > 0:
                                P.eng.wait_ge(SEM[_s], P.cnt(_s))
                        P.eng.wait_ge(s_gp, P.cnt("gp"))
                    P.eng.sem_inc(ph, 1)

                run_block("p0", emit_p0)
                st00.__exit__(None, None, None)

                xbuf = [sb01(f"xbuf{i}", [128, DIM], F32) for i in range(2)]
                xnT = [sb01(f"xnT{k}", [128, 512], BF16) for k in range(8)]
                sq_t = sb01("sq_t", [128, DIM], F32)
                xc_t = [sb01(f"xc{i}", [128, DIM], F32) for i in range(2)]
                xg_t = sb01("xg_t", [128, DIM], F32)
                xn_bf = [sb01(f"xnbf{i}", [128, DIM], BF16) for i in range(2)]
                sm1 = [sb01(f"sm1{i}", [128, 8], F32) for i in range(2)]
                kst = [sb01(f"kst{i}", [128, 512], BF16) for i in range(4)]
                qvdT = sb01("qvdT", [16, 2048], F32)

                ps_tr = [ps01(f"ps_tr{i}", [128, 1024], BF16) for i in range(2)]
                ps_k = [ps01(f"ps_k{i}", [128, 512], F32) for i in range(2)]
                ps_v = [ps01(f"ps_v{i}", [128, 512], F32) for i in range(2)]
                ps_qv = ps01("ps_qv", [128, 512], F32)
                ps_sm = ps01("ps_sm", [128, 64], F32)

                def emit_p1(P):
                    e = P.eng
                    phase_begin(P)
                    vXN, vXC, vTR, vXNE, vVEV = {}, {}, {}, {}, {}
                    vKEV, vKSO, vQVE3, vQVS, vQVT = {}, {}, {}, {}, {}
                    pjg = {}

                    for kt in range(P1_TILES):
                        g, slot = kt // 4, kt % 4
                        own = kt < OWN_NT
                        pp = kt % 2

                        if P.on("sync") and kt >= 2:
                            e.wait_ge(s_ve, vXN[kt - 2])
                            e.wait_ge(s_sc, vXC[kt - 2])
                        vX = dma(P, "sync", f"dx{pp}", xbuf[pp][:], xb[kt * 128:(kt + 1) * 128, :])

                        wait(P, "vector", f"dx{pp}", vX)
                        last = None
                        if P.on("vector"):
                            s = sm1[pp]
                            e.reduce_sum(s[:, 0:1], xbuf[pp][:], AX)
                            e.tensor_mul(sq_t[:], xbuf[pp][:], xbuf[pp][:])
                            e.reduce_sum(s[:, 1:2], sq_t[:], AX)
                            e.tensor_scalar(s[:, 2:3], s[:, 0:1], float(-1.0 / DIM), None, ALU.mult)
                            e.tensor_mul(s[:, 4:5], s[:, 2:3], s[:, 2:3])
                            last = e.scalar_tensor_tensor(s[:, 3:4], s[:, 1:2], float(1.0 / DIM),
                                                          s[:, 4:5], ALU.mult, ALU.subtract)
                        vST = producer(P, "vector", last)
                        wait(P, "scalar", "ve", vST)
                        last = None
                        if P.on("scalar"):
                            s = sm1[pp]
                            e.activation(s[:, 7:8], s[:, 3:4], AF.Ln, bias=float(EPS1), scale=1.0)
                            last = e.activation(s[:, 5:6], s[:, 7:8], AF.Exp, bias=0.0, scale=-0.5)
                        vRS = producer(P, "scalar", last)
                        wait(P, "vector", "sc", vRS)
                        last = None
                        if P.on("vector"):
                            last = e.tensor_mul(sm1[pp][:, 6:7], sm1[pp][:, 2:3], sm1[pp][:, 5:6])
                        vNM = producer(P, "vector", last)
                        wait(P, "scalar", "ve", vNM)
                        if P.on("scalar") and kt >= 2:
                            e.wait_ge(s_ve, vXN[kt - 2])
                        last = None
                        if P.on("scalar"):
                            last = e.activation(xc_t[pp][:], xbuf[pp][:], AF.Identity,
                                                bias=sm1[pp][:, 6:7], scale=sm1[pp][:, 5:6])
                        vXC[kt] = producer(P, "scalar", last)
                        wait(P, "vector", "sc", vXC[kt])
                        if P.on("vector") and kt >= 2:
                            e.wait_ge(s_pe, vTR[kt - 2])
                        last = None
                        if P.on("vector"):
                            e.tensor_mul(xg_t[:], xc_t[pp][:], g1bc[:])
                            last = e.tensor_add(xn_bf[pp][:], xg_t[:], b1bc[:])
                        vXN[kt] = producer(P, "vector", last)

                        wait(P, "tensor", "ve", vXN[kt])
                        if P.on("tensor") and kt >= 2:
                            e.wait_ge(s_ve, vXNE[kt - 2])
                        last = None
                        if P.on("tensor"):
                            for kd in range(8):
                                last = e.transpose(ps_tr[pp][:, kd * 128:(kd + 1) * 128],
                                                   xn_bf[pp][:, kd * 128:(kd + 1) * 128], ident[:])
                        vTR[kt] = producer(P, "tensor", last)
                        wait(P, "vector", "pe", vTR[kt])
                        if P.on("vector") and g > 0 and slot == 0:
                            e.wait_ge(s_pe, pjg[g - 1])
                        last = None
                        if P.on("vector"):
                            for kd in range(8):
                                last = e.tensor_copy(xnT[kd][:, slot * 128:(slot + 1) * 128],
                                                     ps_tr[pp][:, kd * 128:(kd + 1) * 128])
                        vXNE[kt] = producer(P, "vector", last)

                        wait(P, "tensor", "ve", vXNE[kt])
                        if P.on("tensor") and kt >= 2:
                            e.wait_ge(s_ve, vVEV[kt - 2])
                        last = None
                        if P.on("tensor"):
                            for kk in range(8):
                                e.matmul(ps_v[pp][:], xnT[kk][:, slot * 128:(slot + 1) * 128],
                                         wv_sb[:, kk * 512:(kk + 1) * 512], start=(kk == 0), stop=False)
                            last = e.matmul(ps_v[pp][:], ones_row[:], bvR[:], start=False, stop=True)
                        vVP = producer(P, "tensor", last)
                        wait(P, "vector", "pe", vVP)
                        last = None
                        if P.on("vector"):
                            last = e.tensor_copy(v_all[:, kt * 512:(kt + 1) * 512], ps_v[pp][:])
                        vVEV[kt] = producer(P, "vector", last)

                        if own:
                            oi = kt
                            if P.on("tensor") and oi >= 1:
                                e.wait_ge(s_sc, vQVS[oi - 1])
                            last = None
                            if P.on("tensor"):
                                for kk in range(8):
                                    last = e.matmul(ps_qv[0:16, 0:128], wcT[:, kk * 16:(kk + 1) * 16],
                                                    xnT[kk][:, slot * 128:(slot + 1) * 128],
                                                    start=(kk == 0), stop=(kk == 7))
                            vQP = producer(P, "tensor", last)
                            wait(P, "scalar", "pe", vQP)
                            last = None
                            if P.on("scalar"):
                                last = e.activation(qvdT[:, oi * 128:(oi + 1) * 128], ps_qv[0:16, 0:128],
                                                    AF.Identity, bias=bqvd[:], scale=RS32)
                            vQVS[oi] = producer(P, "scalar", last)
                            wait(P, "tensor", "sc", vQVS[oi])
                            if P.on("tensor") and oi >= 1:
                                e.wait_ge(s_ve, vQVE3[oi - 1])
                            last = None
                            if P.on("tensor"):
                                last = e.transpose(ps_sm[:, 0:16], qvdT[0:16, oi * 128:(oi + 1) * 128],
                                                   ident_f[:])
                            vQVT[oi] = producer(P, "tensor", last)
                            wait(P, "vector", "pe", vQVT[oi])
                            last = None
                            if P.on("vector"):
                                last = e.tensor_copy(qvd_sb[:, oi * 16:(oi + 1) * 16], ps_sm[:, 0:16])
                            vQVE3[oi] = producer(P, "vector", last)

                        if slot == 3:
                            own_g = g < OWN_NT // 4
                            nchunks = 16 if own_g else 8
                            for ci in range(nchunks):
                                is_q = ci >= 8
                                mc = ci % 8
                                ck = P.bump("ckq")
                                kb = ck % 2
                                if P.on("tensor"):
                                    if ci == 0:
                                        e.wait_ge(s_ve, vXNE[kt])
                                    if ck >= 3:
                                        e.wait_ge(s_sc, vKEV[ck - 2])
                                last = None
                                if P.on("tensor"):
                                    wsrc = wq_sb if is_q else wk_sb
                                    bR = bqR if is_q else bkR
                                    for kk in range(8):
                                        e.matmul(ps_k[kb][:],
                                                 wsrc[:, kk * DIM + mc * 128:kk * DIM + (mc + 1) * 128],
                                                 xnT[kk][:], start=(kk == 0), stop=False)
                                    last = e.matmul(ps_k[kb][:], bR[0:1, mc * 128:(mc + 1) * 128],
                                                    ones_tok[:], start=False, stop=True)
                                vKP = producer(P, "tensor", last)
                                wait(P, "scalar", "pe", vKP)
                                last = None
                                if is_q:
                                    if P.on("scalar"):
                                        last = e.activation(qt_sb[mc][:, g * 512:(g + 1) * 512], ps_k[kb][:],
                                                            AF.Identity, bias=0.0, scale=0.125)
                                    vKEV[ck] = producer(P, "scalar", last)
                                else:
                                    ki = P.bump("ki") - 1
                                    if P.on("scalar") and ki >= 4:
                                        e.wait_ge(SEM[f"dk{ki % 4}"], vKSO[ki - 4])
                                    if P.on("scalar"):
                                        last = e.activation(kst[ki % 4][:], ps_k[kb][:], AF.Identity,
                                                            bias=0.0, scale=1.0)
                                    vKEV[ck] = producer(P, "scalar", last)
                                    wait(P, "sync", "sc", vKEV[ck])
                                    vKSO[ki] = dma(P, "sync", f"dk{ki % 4}",
                                                   kt_hbm[mc * 128:(mc + 1) * 128, g * 512:(g + 1) * 512],
                                                   kst[ki % 4][:])
                            pjg[g] = P.cnt("pe")

                    if P.on("sync"):
                        for _s in ("dx", "dk0", "dk1", "dk2", "dk3"):
                            if P.cnt(_s) > 0:
                                e.wait_ge(SEM[_s], P.cnt(_s))
                        e.wait_ge(s_ve, P.cnt("ve"))
                        e.wait_ge(s_sc, P.cnt("sc"))
                    P.eng.sem_inc(ph, 1)

                if PHASES >= 1:
                    run_block("p1", emit_p1)

                if dbg:
                    def emit_dbg1(P):
                        e = P.eng
                        phase_begin(P)
                        if "kt" in dbg:
                            dma(P, "sync", "dbg", dbg["kt"], kt_hbm)
                        if "v" in dbg:
                            dma(P, "sync", "dbg", dbg["v"], v_all[:])
                        if "qt" in dbg:
                            for i in range(8):
                                dma(P, "sync", "dbg", dbg["qt"][i * 128:(i + 1) * 128, :], qt_sb[i][:])
                        if "qvd" in dbg:
                            dma(P, "sync", "dbg", dbg["qvd"], qvd_sb[:])
                        if P.on("sync"):
                            e.wait_ge(SEM["dbg"], P.cnt("dbg"))
                        P.eng.sem_inc(ph, 1)

                    run_block("dbg1", emit_dbg1)

            of_T = [stOF.enter_context(nc.sbuf_tensor(f"ofT{i}", [128, 2048], BF16, side="right"))
                    for i in range(8)]

            if PHASES >= 2:
                st2 = contextlib.ExitStack()
                with st2:
                    def sb2(name, shape, dt):
                        return st2.enter_context(nc.sbuf_tensor(name, shape, dt))

                    def ps2(name, shape, dt):
                        return st2.enter_context(nc.psum_tensor(name, shape, dt))

                    ktb = [sb2(f"ktb{i}", [128, N], BF16) for i in range(2)]
                    PT = sb2("PT", [128, 2 * N], BF16)
                    on_bf = [sb2(f"onbf{i}", [128, 512], BF16) for i in range(2)]
                    tmp5 = [sb2(f"tmp5{i}", [128, 512], F32) for i in range(2)]
                    of_acc = [sb2(f"ofac{i}", [128, 64], F32) for i in range(2)]
                    of_bf = [sb2(f"ofbf{i}", [128, 64], BF16) for i in range(2)]
                    sm2 = [sb2(f"sm2{i}", [128, 32], F32) for i in range(2)]

                    spp = [ps2("spX", [128, 1024], F32), ps2("spY", [128, 1024], F32)]
                    av_ps = [ps2(f"av{i}", [128, 512], F32) for i in range(2)]
                    l_ps = [ps2(f"l{i}", [128, 8], F32) for i in range(2)]

                    NSTEP = 8 * 16

                    def emit_p2(P):
                        e = P.eng
                        phase_begin(P)
                        vKT, vSP, vEX, vAV = {}, {}, {}, {}
                        vCP, vCE, vCB, vOT, vOE = {}, {}, {}, {}, {}
                        sp_end = {}

                        def dve_tail(t):
                            for hh in range(2):
                                wait(P, "vector", "sc", vCE[(t, hh)])
                                if P.on("vector") and t >= 1:
                                    e.wait_ge(s_pe, vOT[(t - 1, hh)])
                                last = None
                                if P.on("vector"):
                                    s = sm2[hh]
                                    e.reciprocal(s[:, 3:4], s[:, 2:3])
                                    e.tensor_scalar(s[:, 24:32], s[:, 16:24], s[:, 3:4], None, ALU.mult)
                                    e.tensor_scalar(of_acc[hh][:], on_bf[hh][:, 0:64], s[:, 24:25],
                                                    None, ALU.mult)
                                    for r_ in range(1, 7):
                                        e.scalar_tensor_tensor(of_acc[hh][:],
                                                               on_bf[hh][:, r_ * 64:(r_ + 1) * 64],
                                                               s[:, 24 + r_:25 + r_], of_acc[hh][:],
                                                               ALU.mult, ALU.add)
                                    last = e.scalar_tensor_tensor(of_bf[hh][:], on_bf[hh][:, 448:512],
                                                                  s[:, 31:32], of_acc[hh][:],
                                                                  ALU.mult, ALU.add)
                                vCB[(t, hh)] = producer(P, "vector", last)

                        def act_comp(t):
                            for hh in range(2):
                                wait(P, "scalar", "ve", vCP[(t, hh)])
                                last = None
                                if P.on("scalar"):
                                    last = e.activation(sm2[hh][:, 16:24], sm2[hh][:, 8:16], AF.Exp,
                                                        bias=sm2[hh][:, 1:2], scale=1.0,
                                                        accum_out=sm2[hh][:, 2:3])
                                vCE[(t, hh)] = producer(P, "scalar", last)

                        def pe_oftr(t, hh):
                            wait(P, "tensor", "ve", vCB[(t, hh)])
                            last = None
                            if P.on("tensor"):
                                last = e.transpose(av_ps[hh][:].bitcast(BF16)[0:64, 0:128],
                                                   of_bf[hh][:], ident[:])
                            vOT[(t, hh)] = producer(P, "tensor", last)

                        def dve_ofevac(t):
                            c, qi = t // 16, t % 16
                            for hh in range(2):
                                wait(P, "vector", "pe", vOT[(t, hh)])
                                last = None
                                if P.on("vector"):
                                    last = e.tensor_copy(
                                        of_T[c][hh * 64:(hh + 1) * 64, qi * 128:(qi + 1) * 128],
                                        av_ps[hh][:].bitcast(BF16)[0:64, 0:128])
                                vOE[(t, hh)] = producer(P, "vector", last)

                        for t in range(NSTEP):
                            c, qi = t // 16, t % 16
                            if qi == 0:
                                if c == 0:
                                    vKT[0] = dma(P, "sync", "kt0", ktb[0][:], kt_hbm[0:128, :])
                                if c < 7:
                                    if P.on("sync") and c >= 1:
                                        e.wait_ge(s_pe, sp_end[c - 1])
                                    vKT[c + 1] = dma(P, "sync", f"kt{(c + 1) % 2}", ktb[(c + 1) % 2][:],
                                                     kt_hbm[(c + 1) * 128:(c + 2) * 128, :])

                            if t >= 1:
                                act_comp(t - 1)
                                dve_tail(t - 1)
                                pe_oftr(t - 1, 0)

                            kb = c % 2
                            for st in range(16):
                                if t >= 1 and st == 12:
                                    pe_oftr(t - 1, 1)
                                sp = spp[st % 2]
                                if P.on("tensor"):
                                    if st == 0 and qi == 0:
                                        e.wait_ge(SEM[f"kt{c % 2}"], vKT[c])
                                    if (t, st - 2) in vEX or (t - 1, 15) in vEX:
                                        key = (t, st - 2) if st >= 2 else (t - 1, 14 + st)
                                        if key in vEX:
                                            e.wait_ge(s_sc, vEX[key])
                                last = None
                                if P.on("tensor"):
                                    for jj in range(4):
                                        jc = st * 4 + jj
                                        e.matmul(sp[:, jj * 128:(jj + 1) * 128],
                                                 ktb[kb][0:64, jc * 128:(jc + 1) * 128],
                                                 qt_sb[c][0:64, qi * 128:(qi + 1) * 128],
                                                 start=True, stop=True, tile_position=(0, 0))
                                        last = e.matmul(sp[:, 512 + jj * 128:512 + (jj + 1) * 128],
                                                        ktb[kb][64:128, jc * 128:(jc + 1) * 128],
                                                        qt_sb[c][64:128, qi * 128:(qi + 1) * 128],
                                                        start=True, stop=True, tile_position=(64, 0))
                                vSP[(t, st)] = producer(P, "tensor", last)
                                for hh in range(2):
                                    if P.on("scalar"):
                                        if st == 0 and hh == 0 and t >= 1:
                                            e.wait_ge(s_pe, vAV[(t - 1, 1)])
                                        e.wait_ge(s_pe, vSP[(t, st)])
                                    last = None
                                    if P.on("scalar"):
                                        last = e.activation(
                                            PT[:, hh * N + st * 512:hh * N + (st + 1) * 512],
                                            sp[:, hh * 512:(hh + 1) * 512], AF.Exp,
                                            bias=0.0, scale=1.0)
                                    vEX[(t, st)] = producer(P, "scalar", last)

                            if t >= 1:
                                dve_ofevac(t - 1)

                            if P.on("tensor") and t >= 1:
                                e.wait_ge(s_ve, vOE[(t - 1, 0)])
                                e.wait_ge(s_ve, vOE[(t - 1, 1)])
                            lasts = [None, None]
                            for gg in range(16):
                                for hh in range(2):
                                    if P.on("tensor"):
                                        e.wait_ge(s_sc, vEX[(t, gg)])
                                        for jj in range(4):
                                            jc = gg * 4 + jj
                                            lhs = PT[:, hh * N + jc * 128:hh * N + (jc + 1) * 128]
                                            e.matmul(av_ps[hh][:], lhs, v_all[:, jc * 512:(jc + 1) * 512],
                                                     start=(jc == 0), stop=(jc == 63))
                                            lasts[hh] = e.matmul(l_ps[hh][:, 0:1], lhs, ones_col[:],
                                                                 start=(jc == 0), stop=(jc == 63))
                            for hh in range(2):
                                vAV[(t, hh)] = producer(P, "tensor", lasts[hh])

                            for hh in range(2):
                                h = 2 * c + hh
                                wait(P, "vector", "pe", vAV[(t, hh)])
                                last = None
                                if P.on("vector"):
                                    s = sm2[hh]
                                    e.reciprocal(s[:, 0:1], l_ps[hh][:, 0:1])
                                    e.tensor_scalar(on_bf[hh][:], av_ps[hh][:], s[:, 0:1], None, ALU.mult)
                                    e.tensor_mul(tmp5[hh][:], on_bf[hh][:], wsv_bc[:])
                                    for r_ in range(8):
                                        e.reduce_sum(s[:, 8 + r_:9 + r_],
                                                     tmp5[hh][:, r_ * 64:(r_ + 1) * 64], AX)
                                    e.tensor_scalar(s[:, 8:16], s[:, 8:16],
                                                    qvd_sb[:, qi * 16 + h:qi * 16 + h + 1],
                                                    bs128[:, 0:1], ALU.add, ALU.add)
                                    last = e.reduce_max(s[:, 1:2], s[:, 8:16], AX, negate=True)
                                vCP[(t, hh)] = producer(P, "vector", last)

                            if qi == 15:
                                sp_end[c] = P.cnt("pe")

                        act_comp(NSTEP - 1)
                        dve_tail(NSTEP - 1)
                        pe_oftr(NSTEP - 1, 0)
                        pe_oftr(NSTEP - 1, 1)
                        dve_ofevac(NSTEP - 1)

                        if P.on("sync"):
                            e.wait_ge(s_ve, P.cnt("ve"))
                        P.eng.sem_inc(ph, 1)

                    run_block("p2", emit_p2)

                    if dbg and "oft" in dbg:
                        def emit_dbg2(P):
                            e = P.eng
                            phase_begin(P)
                            for i in range(8):
                                dma(P, "sync", "dbg", dbg["oft"][i * 128:(i + 1) * 128, :], of_T[i][:])
                            if P.on("sync"):
                                e.wait_ge(SEM["dbg"], P.cnt("dbg"))
                            P.eng.sem_inc(ph, 1)

                        run_block("dbg2", emit_dbg2)
        if PHASES >= 3:
            st3 = contextlib.ExitStack()
            with st3:
                def sb3(name, shape, dt):
                    return st3.enter_context(nc.sbuf_tensor(name, shape, dt))

                st3a = contextlib.ExitStack()
                with st3a:
                    def sb3a(name, shape, dt):
                        return st3a.enter_context(nc.sbuf_tensor(name, shape, dt))

                    def ps3a(name, shape, dt):
                        return st3a.enter_context(nc.psum_tensor(name, shape, dt))

                    wf_sb = sb3a("wf_sb", [128, 8 * DIM], BF16)
                    wr1_sb = sb3a("wr1_sb", [128, 8 * 2048], BF16)
                    wr2_sb = sb3a("wr2_sb", [128, 16 * DIM], BF16)
                    g2bc = sb3a("g2bc", [128, DIM], F32)
                    b2bc = sb3a("b2bc", [128, DIM], F32)
                    bfR = sb3a("bfR", [1, DIM], BF16)
                    br1R = sb3a("br1R", [1, 2 * DIM], BF16)
                    br2R = sb3a("br2R", [1, DIM], BF16)
                    x3buf = [sb3a(f"x3b{i}", [128, DIM], F32) for i in range(2)]
                    row3 = x3buf[0][0:1, :]
                    out2b = [sb3a(f"out2b{i}", [128, DIM], F32) for i in range(2)]
                    xpo2b = [sb3a(f"xpo2b{i}", [128, DIM], F32) for i in range(4)]
                    on2bf = [sb3a(f"on2bf{i}", [128, DIM], BF16) for i in range(2)]
                    onT = [sb3a(f"onT{k}", [128, 512], BF16) for k in range(8)]
                    h1T = [sb3a(f"h1T{k}", [128, 512], BF16) for k in range(16)]
                    x1b = [sb3a(f"x1b{i}", [128, DIM], F32) for i in range(2)]
                    x1t = out2b
                    sq3 = sb3a("sq3", [128, DIM], F32)
                    sm3 = [sb3a(f"sm3{i}", [128, 16], F32) for i in range(2)]

                    ps_m = [ps3a(f"ps_m{i}", [128, 1024], F32) for i in range(2)]
                    ps_f = [ps3a(f"ps_f{i}", [128, 512], F32) for i in range(2)]
                    ps_t3 = [ps3a(f"ps_t3{i}", [128, 1024], BF16) for i in range(2)]

                    def emit_p3a1(P):
                        e = P.eng
                        phase_begin(P)
                        dmas = []
                        for kk in range(8):
                            dmas.append((wf_sb[:, kk * DIM:(kk + 1) * DIM], wfT[kk * 128:(kk + 1) * 128, :]))
                            dmas.append((wr1_sb[:, kk * 2048:(kk + 1) * 2048], wr1T[kk * 128:(kk + 1) * 128, :]))
                        for kk in range(16):
                            dmas.append((wr2_sb[:, kk * DIM:(kk + 1) * DIM], wr2T[kk * 128:(kk + 1) * 128, :]))
                        dmas += [(bfR[:], bf_d[None, :]), (br1R[:], br1_d[None, :]), (br2R[:], br2_d[None, :])]
                        vW3 = 0
                        for dst, src in dmas:
                            vW3 = dma(P, "gpsimd", "dw", dst, src)
                        vBCe = 0
                        for dsrc, bc in ((g2_d, g2bc), (b2_d, b2bc)):
                            vRL = dma(P, "gpsimd", "dc", row3, dsrc[None, :])
                            for half in range(2):
                                wait(P, "tensor", "dc", vRL)
                                wait(P, "tensor", "ve", vBCe)
                                last = None
                                if P.on("tensor"):
                                    last = e.matmul(ps_m[0][:, 0:512], ones_f[:],
                                                    x3buf[0][0:1, half * 512:(half + 1) * 512],
                                                    start=True, stop=True)
                                vBC = producer(P, "tensor", last)
                                wait(P, "vector", "pe", vBC)
                                last = None
                                if P.on("vector"):
                                    last = e.tensor_copy(bc[:, half * 512:(half + 1) * 512], ps_m[0][:, 0:512])
                                vBCe = producer(P, "vector", last)
                            if P.on("gpsimd"):
                                e.wait_ge(s_pe, P.cnt("pe"))

                        vO2, vLN2F, vTRA, vTEV = {}, {}, {}, {}
                        vX1, vX1S = {}, {}
                        pmEv = {0: vBCe}
                        pmU = [0]
                        pfEv = {}
                        pfU = [0]
                        ptRel = {}
                        ptU = [0]
                        f1g = {}
                        ffn2g = {}
                        h1ready = {}

                        for g3 in range(4):
                            for tl in range(4):
                                qi = g3 * 4 + tl
                                pp = qi % 2

                                if P.on("sync"):
                                    if qi >= 2:
                                        e.wait_ge(s_ve, vO2[qi - 2])
                                    else:
                                        e.wait_ge(s_ve, vBCe)
                                vX3 = dma(P, "sync", f"dx3{pp}", x3buf[pp][:], xb[qi * 128:(qi + 1) * 128, :])

                                pmU[0] += 1
                                u = pmU[0]
                                wait(P, "tensor", "dw", vW3)
                                if P.on("tensor"):
                                    e.wait_ge(s_ve, pmEv.get(u - 2, vBCe))
                                last = None
                                if P.on("tensor"):
                                    for half in range(2):
                                        for kk in range(8):
                                            e.matmul(ps_m[(u - 1) % 2][:, half * 512:(half + 1) * 512],
                                                     of_T[kk][:, qi * 128:(qi + 1) * 128],
                                                     wf_sb[:, kk * DIM + half * 512:kk * DIM + (half + 1) * 512],
                                                     start=(kk == 0), stop=False)
                                        last = e.matmul(ps_m[(u - 1) % 2][:, half * 512:(half + 1) * 512],
                                                        ones_row[:], bfR[0:1, half * 512:(half + 1) * 512],
                                                        start=False, stop=True)
                                vWF = producer(P, "tensor", last)
                                wait(P, "vector", "pe", vWF)
                                wait(P, "vector", f"dx3{pp}", vX3)
                                last = None
                                if P.on("vector"):
                                    e.tensor_add(out2b[pp][:], ps_m[(u - 1) % 2][:], x3buf[pp][:])
                                    last = e.tensor_add(xpo2b[tl][:], out2b[pp][:], x3buf[pp][:])
                                vO2[qi] = producer(P, "vector", last)
                                pmEv[u] = vO2[qi]

                                last = None
                                if P.on("vector"):
                                    s = sm3[pp]
                                    e.reduce_sum(s[:, 0:1], out2b[pp][:], AX)
                                    e.tensor_mul(sq3[:], out2b[pp][:], out2b[pp][:])
                                    e.reduce_sum(s[:, 1:2], sq3[:], AX)
                                    e.tensor_scalar(s[:, 2:3], s[:, 0:1], float(-1.0 / DIM), None, ALU.mult)
                                    e.tensor_mul(s[:, 4:5], s[:, 2:3], s[:, 2:3])
                                    last = e.scalar_tensor_tensor(s[:, 3:4], s[:, 1:2], float(1.0 / DIM),
                                                                  s[:, 4:5], ALU.mult, ALU.subtract)
                                vst = producer(P, "vector", last)
                                wait(P, "scalar", "ve", vst)
                                last = None
                                if P.on("scalar"):
                                    s = sm3[pp]
                                    e.activation(s[:, 7:8], s[:, 3:4], AF.Ln, bias=float(EPS1), scale=1.0)
                                    last = e.activation(s[:, 5:6], s[:, 7:8], AF.Exp, bias=0.0, scale=-0.5)
                                vrs = producer(P, "scalar", last)
                                wait(P, "vector", "sc", vrs)
                                if P.on("vector") and qi >= 2:
                                    e.wait_ge(s_pe, vTRA[qi - 2])
                                last = None
                                if P.on("vector"):
                                    e.tensor_scalar(sq3[:], out2b[pp][:], sm3[pp][:, 2:3],
                                                    sm3[pp][:, 5:6], ALU.add, ALU.mult)
                                    e.tensor_mul(sq3[:], sq3[:], g2bc[:])
                                    last = e.tensor_add(on2bf[pp][:], sq3[:], b2bc[:])
                                vLN2F[qi] = producer(P, "vector", last)

                                ptU[0] += 1
                                up = ptU[0]
                                wait(P, "tensor", "ve", vLN2F[qi])
                                if P.on("tensor") and up >= 3:
                                    e.wait_ge(s_ve, ptRel[up - 2])
                                last = None
                                if P.on("tensor"):
                                    for kd in range(8):
                                        last = e.transpose(ps_t3[(up - 1) % 2][:, kd * 128:(kd + 1) * 128],
                                                           on2bf[pp][:, kd * 128:(kd + 1) * 128], ident[:])
                                vTRA[qi] = producer(P, "tensor", last)
                                wait(P, "vector", "pe", vTRA[qi])
                                if P.on("vector") and g3 > 0:
                                    e.wait_ge(s_pe, f1g[g3 - 1])
                                last = None
                                if P.on("vector"):
                                    for kd in range(8):
                                        last = e.tensor_copy(onT[kd][:, tl * 128:(tl + 1) * 128],
                                                             ps_t3[(up - 1) % 2][:, kd * 128:(kd + 1) * 128])
                                vTEV[qi] = producer(P, "vector", last)
                                ptRel[up] = vTEV[qi]

                            for mc in range(16):
                                pfU[0] += 1
                                uf = pfU[0]
                                if P.on("tensor"):
                                    if mc == 0:
                                        e.wait_ge(s_ve, vTEV[g3 * 4 + 3])
                                    if uf >= 3:
                                        e.wait_ge(s_ve, pfEv[uf - 2])
                                last = None
                                if P.on("tensor"):
                                    for kk in range(8):
                                        e.matmul(ps_f[(uf - 1) % 2][:],
                                                 wr1_sb[:, kk * 2048 + mc * 128:kk * 2048 + (mc + 1) * 128],
                                                 onT[kk][:], start=(kk == 0), stop=False)
                                    last = e.matmul(ps_f[(uf - 1) % 2][:], br1R[0:1, mc * 128:(mc + 1) * 128],
                                                    ones_tok[:], start=False, stop=True)
                                vF1 = producer(P, "tensor", last)
                                wait(P, "vector", "pe", vF1)
                                if P.on("vector") and g3 > 0:
                                    e.wait_ge(s_pe, ffn2g[g3 - 1])
                                last = None
                                if P.on("vector"):
                                    last = e.tensor_scalar(h1T[mc][:], ps_f[(uf - 1) % 2][:],
                                                           0.0, None, ALU.max)
                                vF1E = producer(P, "vector", last)
                                pfEv[uf] = vF1E
                            h1ready[g3] = pfEv[pfU[0]]
                            f1g[g3] = P.cnt("pe")

                            for tl in range(4):
                                qi = g3 * 4 + tl
                                pp = qi % 2
                                pmU[0] += 1
                                u = pmU[0]
                                if P.on("tensor"):
                                    e.wait_ge(s_ve, h1ready[g3])
                                    if u >= 3:
                                        e.wait_ge(s_ve, pmEv[u - 2])
                                last = None
                                if P.on("tensor"):
                                    for half in range(2):
                                        for mc in range(16):
                                            e.matmul(ps_m[(u - 1) % 2][:, half * 512:(half + 1) * 512],
                                                     h1T[mc][:, tl * 128:(tl + 1) * 128],
                                                     wr2_sb[:, mc * DIM + half * 512:mc * DIM + (half + 1) * 512],
                                                     start=(mc == 0), stop=False)
                                        last = e.matmul(ps_m[(u - 1) % 2][:, half * 512:(half + 1) * 512],
                                                        ones_row[:], br2R[0:1, half * 512:(half + 1) * 512],
                                                        start=False, stop=True)
                                vFF2 = producer(P, "tensor", last)
                                if tl == 3:
                                    ffn2g[g3] = P.cnt("pe")

                                wait(P, "vector", "pe", vFF2)
                                last = None
                                if P.on("vector"):
                                    s = sm3[pp]
                                    e.tensor_add(x1t[pp][:], ps_m[(u - 1) % 2][:], xpo2b[tl][:])
                                    e.reduce_sum(s[:, 8:9], x1t[pp][:], AX)
                                    e.tensor_mul(sq3[:], x1t[pp][:], x1t[pp][:])
                                    e.reduce_sum(s[:, 9:10], sq3[:], AX)
                                    e.tensor_scalar(s[:, 10:11], s[:, 8:9], float(-1.0 / DIM), None, ALU.mult)
                                    e.tensor_mul(s[:, 12:13], s[:, 10:11], s[:, 10:11])
                                    last = e.scalar_tensor_tensor(s[:, 11:12], s[:, 9:10], float(1.0 / DIM),
                                                                  s[:, 12:13], ALU.mult, ALU.subtract)
                                vst2 = producer(P, "vector", last)
                                pmEv[u] = vst2
                                wait(P, "scalar", "ve", vst2)
                                last = None
                                if P.on("scalar"):
                                    s = sm3[pp]
                                    e.activation(s[:, 15:16], s[:, 11:12], AF.Ln, bias=float(EPS1), scale=1.0)
                                    last = e.activation(s[:, 13:14], s[:, 15:16], AF.Exp, bias=0.0, scale=-0.5)
                                vrs2 = producer(P, "scalar", last)
                                wait(P, "vector", "sc", vrs2)
                                if P.on("vector") and qi >= 2:
                                    e.wait_ge(SEM[f"x1s{qi % 2}"], vX1S[qi - 2])
                                last = None
                                if P.on("vector"):
                                    last = e.tensor_scalar(x1b[pp][:], x1t[pp][:], sm3[pp][:, 10:11],
                                                           sm3[pp][:, 13:14], ALU.add, ALU.mult)
                                vX1[qi] = producer(P, "vector", last)
                                wait(P, "sync", "ve", vX1[qi])
                                vX1S[qi] = dma(P, "sync", f"x1s{qi % 2}", x1_hbm[qi * 128:(qi + 1) * 128, :], x1b[pp][:])

                        if P.on("sync"):
                            for _s in ("dx30", "dx31", "x1s0", "x1s1"):
                                if P.cnt(_s) > 0:
                                    e.wait_ge(SEM[_s], P.cnt(_s))
                            e.wait_ge(s_ve, P.cnt("ve"))
                        P.eng.sem_inc(ph, 1)

                    run_block("p3a1", emit_p3a1)
                stOF.close()

                nrmT = [sb3(f"nrmT{k}", [128, 2048], BF16) for k in range(8)]
                cmb_sb = sb3("cmb_sb", [128, OWN_NT * 8], F32)

                st3r = contextlib.ExitStack()
                with st3r:
                    def sb3r(name, shape, dt):
                        return st3r.enter_context(nc.sbuf_tensor(name, shape, dt))

                    rt_sb = sb3r("rt_sb", [128, 64], BF16)
                    nwbc = sb3r("nwbc", [128, DIM], F32)
                    rowr = sb3r("rowr", [1, DIM], F32)
                    x1r = [sb3r(f"x1r{i}", [128, DIM], F32) for i in range(2)]
                    nbf = [sb3r(f"nbf{i}", [128, DIM], BF16) for i in range(2)]
                    sqr = sb3r("sqr", [128, DIM], F32)
                    rsm = [sb3r(f"rsm{i}", [128, 40], F32) for i in range(2)]

                    ps_rt = [st3r.enter_context(nc.psum_tensor(f"ps_rt{i}", [128, 512], F32)) for i in range(2)]
                    ps_tr2 = [st3r.enter_context(nc.psum_tensor(f"ps_tr2{i}", [128, 1024], BF16)) for i in range(2)]

                    def emit_p3a2(P):
                        e = P.eng
                        phase_begin(P)
                        vRL = 0
                        for kk in range(8):
                            vRL = dma(P, "gpsimd", "dw", rt_sb[:, kk * 8:(kk + 1) * 8],
                                      routerT[kk * 128:(kk + 1) * 128, :])
                        vRL2 = dma(P, "gpsimd", "dw", rowr[:], nw_d[None, :])
                        vBCe = 0
                        for half in range(2):
                            wait(P, "tensor", "dw", vRL2)
                            wait(P, "tensor", "ve", vBCe)
                            last = None
                            if P.on("tensor"):
                                last = e.matmul(ps_rt[0][:], ones_f[:], rowr[0:1, half * 512:(half + 1) * 512],
                                                start=True, stop=True)
                            vBC = producer(P, "tensor", last)
                            wait(P, "vector", "pe", vBC)
                            last = None
                            if P.on("vector"):
                                last = e.tensor_copy(nwbc[:, half * 512:(half + 1) * 512], ps_rt[0][:])
                            vBCe = producer(P, "vector", last)

                        vXR, vNBF, vRTE, vNTE = {}, {}, {}, {}
                        ptRel2 = {}
                        pfRel2 = {}
                        for qi in range(OWN_NT):
                            pp = qi % 2
                            if P.on("sync") and qi >= 2:
                                e.wait_ge(s_ve, vNBF[qi - 2])
                            vXR[qi] = dma(P, "sync", f"x1l{pp}", x1r[pp][:], x1_hbm[qi * 128:(qi + 1) * 128, :])

                            wait(P, "vector", f"x1l{pp}", vXR[qi])
                            last = None
                            if P.on("vector"):
                                e.tensor_mul(sqr[:], x1r[pp][:], x1r[pp][:])
                                last = e.reduce_sum(rsm[pp][:, 7:8], sqr[:], AX)
                            vr1 = producer(P, "vector", last)
                            wait(P, "scalar", "ve", vr1)
                            last = None
                            if P.on("scalar"):
                                e.activation(rsm[pp][:, 6:7], rsm[pp][:, 7:8], AF.Ln,
                                             bias=float(EPS2 * DIM), scale=1.0)
                                last = e.activation(rsm[pp][:, 5:6], rsm[pp][:, 6:7], AF.Exp,
                                                    bias=float(0.5 * np.log(DIM)), scale=-0.5)
                            vr2 = producer(P, "scalar", last)
                            wait(P, "vector", "sc", vr2)
                            if P.on("vector") and qi >= 2:
                                e.wait_ge(s_pe, vRTE[qi - 2])
                            last = None
                            if P.on("vector"):
                                last = e.scalar_tensor_tensor(nbf[pp][:], x1r[pp][:], rsm[pp][:, 5:6],
                                                              nwbc[:], ALU.mult, ALU.mult)
                            vNBF[qi] = producer(P, "vector", last)

                            wait(P, "tensor", "ve", vNBF[qi])
                            if P.on("tensor") and qi >= 2:
                                e.wait_ge(s_ve, ptRel2[qi - 2])
                            last = None
                            if P.on("tensor"):
                                for kd in range(8):
                                    last = e.transpose(ps_tr2[pp][:, kd * 128:(kd + 1) * 128],
                                                       nbf[pp][:, kd * 128:(kd + 1) * 128], ident[:])
                            vRTE[qi] = producer(P, "tensor", last)
                            wait(P, "vector", "pe", vRTE[qi])
                            last = None
                            if P.on("vector"):
                                for kd in range(8):
                                    last = e.tensor_copy(nrmT[kd][:, qi * 128:(qi + 1) * 128],
                                                         ps_tr2[pp][:, kd * 128:(kd + 1) * 128])
                            vNTE[qi] = producer(P, "vector", last)
                            ptRel2[qi] = vNTE[qi]

                            wait(P, "tensor", "ve", vNTE[qi])
                            if P.on("tensor") and qi >= 2:
                                e.wait_ge(s_sc, pfRel2[qi - 2])
                            last = None
                            if P.on("tensor"):
                                for kk in range(8):
                                    last = e.matmul(ps_rt[pp][:, 0:8],
                                                    nrmT[kk][:, qi * 128:(qi + 1) * 128],
                                                    rt_sb[:, kk * 8:(kk + 1) * 8],
                                                    start=(kk == 0), stop=(kk == 7))
                            vRT = producer(P, "tensor", last)
                            wait(P, "vector", "pe", vRT)
                            last = None
                            if P.on("vector"):
                                last = e.reduce_max(rsm[pp][:, 0:1], ps_rt[pp][:, 0:8], AX, negate=True)
                            vrt1 = producer(P, "vector", last)
                            wait(P, "scalar", "ve", vrt1)
                            last = None
                            if P.on("scalar"):
                                last = e.activation(rsm[pp][:, 8:16], ps_rt[pp][:, 0:8],
                                                    AF.Exp, bias=rsm[pp][:, 0:1],
                                                    accum_out=rsm[pp][:, 1:2])
                            vrt2 = producer(P, "scalar", last)
                            pfRel2[qi] = vrt2
                            wait(P, "vector", "sc", vrt2)
                            last = None
                            if P.on("vector"):
                                s = rsm[pp]
                                e.reciprocal(s[:, 2:3], s[:, 1:2])
                                e.tensor_scalar(s[:, 16:24], s[:, 8:16], s[:, 2:3], None, ALU.mult)
                                e.reduce_max(s[:, 3:4], s[:, 16:24], AX)
                                e.tensor_scalar(s[:, 24:32], s[:, 16:24], s[:, 3:4], None, ALU.is_equal)
                                e.scalar_tensor_tensor(s[:, 8:16], s[:, 24:32], -1e9, s[:, 16:24],
                                                       ALU.mult, ALU.add)
                                e.reduce_max(s[:, 4:5], s[:, 8:16], AX)
                                e.tensor_scalar(s[:, 32:40], s[:, 8:16], s[:, 4:5], None, ALU.is_equal)
                                e.tensor_add(s[:, 24:32], s[:, 24:32], s[:, 32:40])
                                e.tensor_add(s[:, 5:6], s[:, 3:4], s[:, 4:5])
                                e.reciprocal(s[:, 6:7], s[:, 5:6])
                                e.tensor_mul(s[:, 32:40], s[:, 16:24], s[:, 24:32])
                                last = e.tensor_scalar(cmb_sb[:, qi * 8:(qi + 1) * 8], s[:, 32:40],
                                                       s[:, 6:7], None, ALU.mult)
                            producer(P, "vector", last)

                        if P.on("sync"):
                            e.wait_ge(s_ve, P.cnt("ve"))
                        P.eng.sem_inc(ph, 1)

                    run_block("p3a2", emit_p3a2)

                    if dbg and ("x1" in dbg or "nrm" in dbg or "cmb" in dbg):
                        def emit_dbg3(P):
                            e = P.eng
                            phase_begin(P)
                            if "x1" in dbg:
                                dma(P, "sync", "dbg", dbg["x1"], x1_hbm)
                            if "nrm" in dbg:
                                for i in range(8):
                                    dma(P, "sync", "dbg", dbg["nrm"][i * 128:(i + 1) * 128, :], nrmT[i][:])
                            if "cmb" in dbg:
                                dma(P, "sync", "dbg", dbg["cmb"], cmb_sb[:])
                            if P.on("sync"):
                                e.wait_ge(SEM["dbg"], P.cnt("dbg"))
                            P.eng.sem_inc(ph, 1)

                        run_block("dbg3", emit_dbg3)

                if PHASES >= 4:
                    st3b = contextlib.ExitStack()
                    with st3b:
                        def sb3b(name, shape, dt):
                            return st3b.enter_context(nc.sbuf_tensor(name, shape, dt))

                        def ps3b(name, shape, dt):
                            return st3b.enter_context(nc.psum_tensor(name, shape, dt))

                        hid_sb = sb3b("hid_sb", [128, 22 * DIM], BF16)
                        dn_sb = sb3b("dn_sb", [128, 22 * DIM], BF16)
                        gu_g = [sb3b(f"gug{i}", [128, 2048], BF16) for i in range(2)]
                        gu_u = [sb3b(f"guu{i}", [128, 2048], BF16) for i in range(2)]
                        moe = sb3b("moe", [128, 8 * DIM], F32)
                        sil = [sb3b(f"sil{i}", [128, 512], F32) for i in range(2)]
                        x1l = [sb3b(f"x1l{i}", [128, DIM], F32) for i in range(2)]
                        ysc = [sb3b(f"ysc{i}", [128, DIM], F32) for i in range(2)]
                        ybuf = x1l
                        sq4 = hid_sb[:, 0:DIM]
                        sm4 = [sb3b(f"sm4{i}", [128, 8], F32) for i in range(2)]

                        ps_g = [ps3b(f"ps_g{i}", [128, 512], F32) for i in range(2)]
                        ps_u = [ps3b(f"ps_u{i}", [128, 512], F32) for i in range(2)]
                        ps_dn = [ps3b(f"ps_dn{i}", [128, 1024], F32) for i in range(2)]

                        def emit_p3b(P):
                            e = P.eng
                            phase_begin(P)
                            vGL, vUL, vDL = {}, {}, {}
                            vSIL, vHID, vDNE = {}, {}, {}
                            slabPE = {}
                            slabI = [0]
                            vDN_pe = {}
                            guU = [0]
                            dnU = [0]
                            vYST = {}
                            vFINv = {}
                            vYBv = {}

                            for hf in range(2):
                                for ee in range(NE):
                                    for hs in range(11):
                                        si = slabI[0]
                                        slabI[0] += 1
                                        gb = si % 2
                                        if P.on("gpsimd"):
                                            prev = slabPE.get(si - 2, 0)
                                            if prev > 0:
                                                e.wait_ge(s_pe, prev)
                                        for kk in range(8):
                                            vGL[(hf, ee, hs)] = dma(
                                                P, "gpsimd", f"dgu{gb}", gu_g[gb][:, kk * 256:(kk + 1) * 256],
                                                guT[ee][kk * 128:(kk + 1) * 128, hs * 256:(hs + 1) * 256])
                                        for kk in range(8):
                                            vUL[(hf, ee, hs)] = dma(
                                                P, "gpsimd", f"dgu{gb}", gu_u[gb][:, kk * 256:(kk + 1) * 256],
                                                guT[ee][kk * 128:(kk + 1) * 128,
                                                        HID + hs * 256:HID + (hs + 1) * 256])
                                        for mk in (2 * hs, 2 * hs + 1):
                                            if mk < 22:
                                                if P.on("gpsimd") and mk == 0 and (hf + ee) > 0:
                                                    e.wait_ge(s_pe, vDN_pe.get((hf, ee - 1),
                                                                               vDN_pe.get((hf - 1, NE - 1), 0)))
                                                vDL[(hf, ee)] = dma(P, "gpsimd", "dn",
                                                                    dn_sb[:, mk * DIM:(mk + 1) * DIM],
                                                                    dnT[ee][mk * 128:(mk + 1) * 128, :])
                                        for mc2 in range(2):
                                            for tc in range(2):
                                                guU[0] += 1
                                                ug = guU[0]
                                                wait(P, "tensor", f"dgu{gb}", vUL[(hf, ee, hs)])
                                                if P.on("tensor") and ug >= 3:
                                                    e.wait_ge(s_ve, vHID.get(ug - 2, 0))
                                                last = None
                                                if P.on("tensor"):
                                                    for kk in range(8):
                                                        e.matmul(ps_g[(ug - 1) % 2][:],
                                                                 gu_g[gb][:, kk * 256 + mc2 * 128:kk * 256 + (mc2 + 1) * 128],
                                                                 nrmT[kk][:, hf * 1024 + tc * 512:hf * 1024 + (tc + 1) * 512],
                                                                 start=(kk == 0), stop=(kk == 7))
                                                    for kk in range(8):
                                                        last = e.matmul(ps_u[(ug - 1) % 2][:],
                                                                        gu_u[gb][:, kk * 256 + mc2 * 128:kk * 256 + (mc2 + 1) * 128],
                                                                        nrmT[kk][:, hf * 1024 + tc * 512:hf * 1024 + (tc + 1) * 512],
                                                                        start=(kk == 0), stop=(kk == 7))
                                                vGUP = producer(P, "tensor", last)
                                                slabPE[si] = vGUP
                                                wait(P, "scalar", "pe", vGUP)
                                                if P.on("scalar") and ug >= 3:
                                                    e.wait_ge(s_ve, vHID.get(ug - 2, 0))
                                                last = None
                                                if P.on("scalar"):
                                                    last = e.activation(sil[(ug - 1) % 2][:],
                                                                        ps_g[(ug - 1) % 2][:], AF.Silu,
                                                                        bias=0.0, scale=1.0)
                                                vSIL[ug] = producer(P, "scalar", last)
                                                wait(P, "vector", "sc", vSIL[ug])
                                                if P.on("vector") and hs == 0 and mc2 == 0 and tc == 0:
                                                    pv = vDN_pe.get((hf, ee - 1),
                                                                    vDN_pe.get((hf - 1, NE - 1), 0))
                                                    if pv > 0:
                                                        e.wait_ge(s_pe, pv)
                                                mkc = 2 * hs + mc2
                                                last = None
                                                if P.on("vector"):
                                                    last = e.tensor_mul(
                                                        hid_sb[:, mkc * DIM + tc * 512:mkc * DIM + (tc + 1) * 512],
                                                        sil[(ug - 1) % 2][:], ps_u[(ug - 1) % 2][:])
                                                vHID[ug] = producer(P, "vector", last)
                                    hid_done = vHID[guU[0]]

                                    for tl in range(8):
                                        qi = hf * 8 + tl
                                        dnU[0] += 1
                                        ud = dnU[0]
                                        wait(P, "tensor", "ve", hid_done)
                                        wait(P, "tensor", "dn", vDL[(hf, ee)])
                                        if P.on("tensor") and ud >= 3:
                                            e.wait_ge(s_ve, vDNE.get(ud - 2, 0))
                                        last = None
                                        if P.on("tensor"):
                                            for half in range(2):
                                                for mk in range(22):
                                                    last = e.matmul(
                                                        ps_dn[(ud - 1) % 2][:, half * 512:(half + 1) * 512],
                                                        hid_sb[:, mk * DIM + tl * 128:mk * DIM + (tl + 1) * 128],
                                                        dn_sb[:, mk * DIM + half * 512:mk * DIM + (half + 1) * 512],
                                                        start=(mk == 0), stop=(mk == 21))
                                        vDN = producer(P, "tensor", last)
                                        if tl == 7:
                                            vDN_pe[(hf, ee)] = P.cnt("pe")
                                        wait(P, "vector", "pe", vDN)
                                        last = None
                                        if P.on("vector"):
                                            cc = cmb_sb[:, qi * 8 + ee:qi * 8 + ee + 1]
                                            if ee == 0:
                                                last = e.tensor_scalar(moe[:, tl * DIM:(tl + 1) * DIM],
                                                                       ps_dn[(ud - 1) % 2][:], cc, None, ALU.mult)
                                            else:
                                                last = e.scalar_tensor_tensor(moe[:, tl * DIM:(tl + 1) * DIM],
                                                                              ps_dn[(ud - 1) % 2][:], cc,
                                                                              moe[:, tl * DIM:(tl + 1) * DIM],
                                                                              ALU.mult, ALU.add)
                                        vDNE[ud] = producer(P, "vector", last)

                                for tl in range(8):
                                    qi = hf * 8 + tl
                                    pp = tl % 2
                                    if P.on("sync") and qi >= 2:
                                        e.wait_ge(SEM[f"yst{qi % 2}"], vYST[qi - 2])
                                    vXL = dma(P, "sync", f"x1l{pp}", x1l[pp][:], x1_hbm[qi * 128:(qi + 1) * 128, :])
                                    wait(P, "vector", f"x1l{pp}", vXL)
                                    last = None
                                    if P.on("vector"):
                                        s = sm4[pp]
                                        e.tensor_scalar(ysc[pp][:], moe[:, tl * DIM:(tl + 1) * DIM],
                                                        alpha128[:, 0:1], None, ALU.mult)
                                        e.scalar_tensor_tensor(ysc[pp][:], x1l[pp][:], 2.0, ysc[pp][:],
                                                               ALU.mult, ALU.add)
                                        e.reduce_sum(s[:, 0:1], ysc[pp][:], AX)
                                        e.tensor_mul(sq4, ysc[pp][:], ysc[pp][:])
                                        e.reduce_sum(s[:, 1:2], sq4, AX)
                                        e.tensor_scalar(s[:, 2:3], s[:, 0:1], float(-1.0 / DIM), None, ALU.mult)
                                        e.tensor_mul(s[:, 4:5], s[:, 2:3], s[:, 2:3])
                                        last = e.scalar_tensor_tensor(s[:, 3:4], s[:, 1:2], float(1.0 / DIM),
                                                                      s[:, 4:5], ALU.mult, ALU.subtract)
                                    vfs = producer(P, "vector", last)
                                    vFINv[qi] = vfs
                                    wait(P, "scalar", "ve", vfs)
                                    last = None
                                    if P.on("scalar"):
                                        s = sm4[pp]
                                        e.activation(s[:, 7:8], s[:, 3:4], AF.Ln, bias=float(EPS1), scale=1.0)
                                        last = e.activation(s[:, 5:6], s[:, 7:8], AF.Exp, bias=0.0, scale=-0.5)
                                    vfr = producer(P, "scalar", last)
                                    wait(P, "vector", "sc", vfr)
                                    last = None
                                    if P.on("vector"):
                                        last = e.tensor_scalar(ybuf[pp][:], ysc[pp][:], sm4[pp][:, 2:3],
                                                               sm4[pp][:, 5:6], ALU.add, ALU.mult)
                                    vyb = producer(P, "vector", last)
                                    vYBv[qi] = vyb
                                    wait(P, "sync", "ve", vyb)
                                    vYST[qi] = dma(P, "sync", f"yst{qi % 2}", y[qi * 128:(qi + 1) * 128, :], ybuf[pp][:])

                            if P.on("sync"):
                                e.wait_ge(SEM["yst0"], P.cnt("yst0"))
                                e.wait_ge(SEM["yst1"], P.cnt("yst1"))
                            P.eng.sem_inc(ph, 1)

                        run_block("p3b", emit_p3b)
        stOF.close()

    return nc


def kernel(**inp):
    f32 = np.float32
    x = np.asarray(inp["L0_attn_x"], f32)
    tr = lambda a: np.ascontiguousarray(np.asarray(a, f32).T)
    shared = {
        "wqT": tr(inp["L0_attn_Wq"]), "wkT": tr(inp["L0_attn_Wk"]),
        "wvT": tr(inp["L0_attn_Wv"]),
        "wqv": np.ascontiguousarray(np.asarray(inp["L0_attn_Wqv"], f32)),
        "wsT": tr(inp["L0_attn_Ws"]),
        "bqvT": np.ascontiguousarray(np.asarray(inp["L0_attn_bqv"], f32).reshape(H, QK).T),
        "wfT": tr(inp["L0_attn_Wf"]), "wr1T": tr(inp["L0_attn_Wr1"]), "wr2T": tr(inp["L0_attn_Wr2"]),
        "routerT": tr(inp["L0_ffn_router_weight"]),
        "guT": np.ascontiguousarray(np.asarray(inp["L0_ffn_gate_up_weight"], f32).transpose(0, 2, 1)),
        "dnT": np.ascontiguousarray(np.asarray(inp["L0_ffn_down_weight"], f32).transpose(0, 2, 1)),
        "bq": np.asarray(inp["L0_attn_bq"], f32), "bk": np.asarray(inp["L0_attn_bk"], f32),
        "bv": np.asarray(inp["L0_attn_bv"], f32),
        "bs": np.asarray(inp["L0_attn_bs"], f32).reshape(1),
        "bf": np.asarray(inp["L0_attn_bf"], f32),
        "br1": np.asarray(inp["L0_attn_br1"], f32), "br2": np.asarray(inp["L0_attn_br2"], f32),
        "g1": np.asarray(inp["L0_attn_g1"], f32), "b1": np.asarray(inp["L0_attn_b1"], f32),
        "g2": np.asarray(inp["L0_attn_g2"], f32), "b2": np.asarray(inp["L0_attn_b2"], f32),
        "nw": np.asarray(inp["L0_ffn_norm_weight"], f32),
        "alpha": np.asarray(inp["L0_ffn_residual_alpha"], f32).reshape(1),
    }
    in_maps = []
    for c in range(8):
        b = c // 4
        qo = (c % 4) * 2048
        xp = np.ascontiguousarray(
            np.concatenate([x[b, qo:qo + 2048], x[b, :qo], x[b, qo + 2048:]], axis=0))
        m = dict(shared)
        m["x_b"] = xp
        in_maps.append(m)

    nc = build_program()
    res = run_bass_kernel_spmd(nc, in_maps, core_ids=list(range(8)))
    out = np.empty((B, N, DIM), f32)
    for c in range(8):
        b = c // 4
        qo = (c % 4) * 2048
        out[b, qo:qo + 2048] = res.results[c]["y"]
    return out


# version 04
# speedup vs baseline: 1.0932x; 1.0932x over previous
import contextlib
import numpy as np
import concourse.bass as bass
import concourse.mybir as mybir
from concourse.bass_utils import run_bass_kernel_spmd

F32 = mybir.dt.float32
BF16 = mybir.dt.bfloat16
AF = mybir.ActivationFunctionType
ALU = mybir.AluOpType
AX = mybir.AxisListType.X

B, N, DIM = 2, 8192, 1024
H, HD, R, QK = 16, 64, 8, 32
HID = 2816
NE = 8
NT = 64
OWN_NT = 16
EPS1 = 1e-5
EPS2 = 1e-6
RS32 = float(1.0 / np.sqrt(32.0))

PHASES = 4
P0_STEPS = 99
P1_TILES = NT
DEBUG = {}

ENG = ("gpsimd", "vector", "scalar", "tensor", "sync")


CHAINED_OPS = frozenset([
    "memset", "memzero", "tensor_copy", "tensor_tensor", "tensor_add",
    "tensor_sub", "tensor_mul", "tensor_max", "tensor_scalar",
    "tensor_single_scalar", "tensor_scalar_add", "tensor_scalar_sub",
    "tensor_scalar_mul", "tensor_scalar_max", "tensor_scalar_min",
    "tensor_relu", "tensor_reduce", "reduce_max", "reduce_sum",
    "scalar_tensor_tensor", "reciprocal", "select", "copy_predicated",
    "bn_stats", "bn_aggr", "activation", "add", "mul", "copy", "square",
    "sqrt", "sign", "affine_select", "iota", "stream_shuffle",
])
_IN_KWS = ("bias", "scale", "alpha", "scalar1", "scalar2", "in_", "in0", "in1", "mask")


def _ap_name(a):
    try:
        return a.tensor.name
    except AttributeError:
        return None


class AutoChain:

    def __init__(self, eng, state, chain_sem):
        self._eng = eng
        self._st = state
        self._sem = chain_sem

    def note_update(self, inst, sem, val):
        rec = self._st["insts"].get(id(inst.ins))
        if rec is not None and rec[1] is None:
            self._st["insts"][id(inst.ins)] = (rec[0], (sem, val))

    def _edge(self, key):
        st = self._st
        rec = st["insts"].get(key)
        if rec is None:
            return
        idx, upd = rec
        if idx <= st["synced"]:
            return
        if upd is None:
            st["chain_n"] = st.get("chain_n", 0) + 1
            st["hinst"][key].then_inc(self._sem, 1)
            upd = (self._sem, st["chain_n"])
            st["insts"][key] = (idx, upd)
        self._eng.wait_ge(upd[0], upd[1])
        st["synced"] = max(st["synced"], idx)

    def __getattr__(self, name):
        fn = getattr(self._eng, name)
        if name not in CHAINED_OPS:
            return fn
        st = self._st

        def wrapped(*args, **kw):
            outs, ins = [], []
            seen_out = False
            for a in args:
                nm = _ap_name(a)
                if nm is None:
                    continue
                if not seen_out:
                    outs.append(nm)
                    seen_out = True
                else:
                    ins.append(nm)
            for k, v in kw.items():
                nm = _ap_name(v)
                if nm is None:
                    continue
                if k == "accum_out" or k == "out":
                    outs.append(nm)
                elif k in _IN_KWS:
                    ins.append(nm)
            for nm in ins:
                w = st["writer"].get(nm)
                if w is not None:
                    self._edge(w)
            for nm in outs:
                w = st["writer"].get(nm)
                if w is not None:
                    self._edge(w)
                for r in st["readers"].get(nm, ()):
                    self._edge(r)
            inst = fn(*args, **kw)
            st["idx"] = st.get("idx", 0) + 1
            key = id(inst.ins)
            st["insts"][key] = (st["idx"], None)
            st["hinst"][key] = inst
            for nm in outs:
                st["writer"][nm] = key
                st["readers"][nm] = []
            for nm in ins:
                st["readers"].setdefault(nm, []).append(key)
            return inst

        return wrapped


class Ctx:
    def __init__(self, eng, which, counts):
        self.eng = eng
        self.w = which
        self.n = counts

    def on(self, name):
        return self.w == name

    def bump(self, key, amt=1):
        self.n[key] = self.n.get(key, 0) + amt
        return self.n[key]

    def cnt(self, key):
        return self.n.get(key, 0)


def build_program():
    nc = bass.Bass("TRN2", target_bir_lowering=False)

    xb = nc.dram_tensor("x_b", [N, DIM], F32, kind="ExternalInput").ap()
    wqT = nc.dram_tensor("wqT", [DIM, DIM], F32, kind="ExternalInput").ap()
    wkT = nc.dram_tensor("wkT", [DIM, DIM], F32, kind="ExternalInput").ap()
    wvT = nc.dram_tensor("wvT", [DIM, R * HD], F32, kind="ExternalInput").ap()
    wqv = nc.dram_tensor("wqv", [QK * H, DIM], F32, kind="ExternalInput").ap()
    wsT = nc.dram_tensor("wsT", [HD + QK, 1], F32, kind="ExternalInput").ap()
    bqvT = nc.dram_tensor("bqvT", [QK, H], F32, kind="ExternalInput").ap()
    wfT = nc.dram_tensor("wfT", [DIM, DIM], F32, kind="ExternalInput").ap()
    wr1T = nc.dram_tensor("wr1T", [DIM, 2 * DIM], F32, kind="ExternalInput").ap()
    wr2T = nc.dram_tensor("wr2T", [2 * DIM, DIM], F32, kind="ExternalInput").ap()
    routerT = nc.dram_tensor("routerT", [DIM, NE], F32, kind="ExternalInput").ap()
    guT = nc.dram_tensor("guT", [NE, DIM, 2 * HID], F32, kind="ExternalInput").ap()
    dnT = nc.dram_tensor("dnT", [NE, HID, DIM], F32, kind="ExternalInput").ap()
    bq_d = nc.dram_tensor("bq", [DIM], F32, kind="ExternalInput").ap()
    bk_d = nc.dram_tensor("bk", [DIM], F32, kind="ExternalInput").ap()
    bv_d = nc.dram_tensor("bv", [R * HD], F32, kind="ExternalInput").ap()
    bs_d = nc.dram_tensor("bs", [1], F32, kind="ExternalInput").ap()
    bf_d = nc.dram_tensor("bf", [DIM], F32, kind="ExternalInput").ap()
    br1_d = nc.dram_tensor("br1", [2 * DIM], F32, kind="ExternalInput").ap()
    br2_d = nc.dram_tensor("br2", [DIM], F32, kind="ExternalInput").ap()
    g1_d = nc.dram_tensor("g1", [DIM], F32, kind="ExternalInput").ap()
    b1_d = nc.dram_tensor("b1", [DIM], F32, kind="ExternalInput").ap()
    g2_d = nc.dram_tensor("g2", [DIM], F32, kind="ExternalInput").ap()
    b2_d = nc.dram_tensor("b2", [DIM], F32, kind="ExternalInput").ap()
    nw_d = nc.dram_tensor("nw", [DIM], F32, kind="ExternalInput").ap()
    alpha_d = nc.dram_tensor("alpha", [1], F32, kind="ExternalInput").ap()

    y = nc.dram_tensor("y", [OWN_NT * 128, DIM], F32, kind="ExternalOutput").ap()
    kt_hbm = nc.dram_tensor("kt_scratch", [DIM, N], BF16, kind="Internal").ap()
    x1_hbm = nc.dram_tensor("x1_scratch", [OWN_NT * 128, DIM], F32, kind="Internal").ap()

    dbg = {}
    dbg_specs = {"kt": ([DIM, N], BF16), "v": ([128, NT * 512], BF16),
                 "qt": ([DIM, 2048], BF16), "qvd": ([128, OWN_NT * 16], F32),
                 "oft": ([DIM, 2048], BF16), "x1": ([OWN_NT * 128, DIM], F32),
                 "nrm": ([DIM, 2048], BF16), "cmb": ([128, OWN_NT * 8], F32)}
    for k_, (shape, dt) in dbg_specs.items():
        if DEBUG.get(k_):
            dbg[k_] = nc.dram_tensor("dbg_" + k_, shape, dt, kind="ExternalOutput").ap()

    CNTS = {w: {} for w in ENG}
    MS = {}

    ES = contextlib.ExitStack()
    with ES:
        def sbuf(name, shape, dt, side="left"):
            return ES.enter_context(nc.sbuf_tensor(name, shape, dt, side=side))

        s_pe = ES.enter_context(nc.semaphore("s_pe"))
        s_ve = ES.enter_context(nc.semaphore("s_ve"))
        s_sc = ES.enter_context(nc.semaphore("s_sc"))
        s_gp = ES.enter_context(nc.semaphore("s_gp"))
        ph = ES.enter_context(nc.semaphore("ph"))
        stOF = contextlib.ExitStack()
        PH = [0]
        SEM = {"pe": s_pe, "ve": s_ve, "sc": s_sc, "gp": s_gp}
        KEY = {"tensor": "pe", "vector": "ve", "scalar": "sc", "gpsimd": "gp"}
        DSTREAMS = ["dx0", "dx1", "dk0", "dk1", "dk2", "dk3", "kt0", "kt1",
                    "dx30", "dx31", "x1s0", "x1s1", "x1l0", "x1l1",
                    "yst0", "yst1", "dw", "dq0", "dq1", "dc",
                    "dgu0", "dgu1", "dn", "dbg"]
        for _s in DSTREAMS:
            SEM[_s] = ES.enter_context(nc.semaphore("sd_" + _s))

        CH_SEM = {w: ES.enter_context(nc.semaphore("ch_" + w))
                  for w in ("gpsimd", "vector", "scalar")}
        CH_ST = {w: {"idx": 0, "chain_n": 0, "synced": 0, "writer": {},
                     "readers": {}, "insts": {}, "hinst": {}}
                 for w in ("gpsimd", "vector", "scalar")}

        def mkeng(e, w):
            if w in CH_SEM:
                return AutoChain(e, CH_ST[w], CH_SEM[w])
            return e

        def run_block(name, emit):
            with nc.Block(name) as blk:
                blk.gpsimd(lambda e: emit(Ctx(mkeng(e, "gpsimd"), "gpsimd", CNTS["gpsimd"])))
                blk.vector(lambda e: emit(Ctx(mkeng(e, "vector"), "vector", CNTS["vector"])))
                blk.scalar(lambda e: emit(Ctx(mkeng(e, "scalar"), "scalar", CNTS["scalar"])))
                blk.tensor(lambda e: emit(Ctx(e, "tensor", CNTS["tensor"])))
                blk.sync(lambda e: emit(Ctx(e, "sync", CNTS["sync"])))
            PH[0] += 5

        def producer(P, engname, last_inst):
            k = KEY[engname]
            v = P.bump(k)
            if P.on(engname) and last_inst is not None:
                last_inst.then_inc(SEM[k], 1)
                if isinstance(P.eng, AutoChain):
                    P.eng.note_update(last_inst, SEM[k], v)
            return v

        def dma(P, qname, stream, dst, src):
            v = P.bump(stream, 16)
            if P.on(qname):
                P.eng.dma_start(dst, src).then_inc(SEM[stream], 16)
            return v

        def wait(P, engname, semkey, val):
            if P.on(engname) and val > 0:
                assert val <= P.cnt(semkey), f"future wait {semkey} {val}>{P.cnt(semkey)}"
                P.eng.wait_ge(SEM[semkey], val)

        def phase_begin(P):
            if PH[0] > 0:
                P.eng.wait_ge(ph, PH[0])

        def phase_end(P, extra=()):
            for semkey, v in extra:
                if v > 0:
                    P.eng.wait_ge(SEM[semkey], v)
            P.eng.sem_inc(ph, 1)

        ident = sbuf("ident", [128, 128], BF16, side="right")
        ident_f = sbuf("ident_f", [16, 16], F32, side="right")
        ones_row = sbuf("ones_row", [1, 128], BF16, side="right")
        ones_tok = sbuf("ones_tok", [1, 512], BF16, side="right")
        ones_col = sbuf("ones_col", [128, 1], BF16, side="right")
        ones_f = sbuf("ones_f", [1, 128], F32, side="right")
        wsv_bc = sbuf("wsv_bc", [128, 512], BF16, side="right")
        bs128 = sbuf("bs128", [128, 1], F32, side="right")
        alpha128 = sbuf("alpha128", [128, 1], F32, side="right")
        qvd_sb = sbuf("qvd", [128, OWN_NT * 16], F32, side="right")
        of_T = None

        ACT_CONSTS = [float(EPS1), 1.0, 0.0, -0.5, 0.125, RS32,
                      float(EPS2 * DIM), float(0.5 * np.log(DIM)), -1e9]
        consts_sb = sbuf("consts_sb", [128, len(ACT_CONSTS)], F32, side="right")
        for i_, v_ in enumerate(ACT_CONSTS):
            nc.const_aps.register(F32, v_, consts_sb[:, i_:i_ + 1])

        st12 = contextlib.ExitStack()
        with st12:
            def sb12(name, shape, dt):
                return st12.enter_context(nc.sbuf_tensor(name, shape, dt))

            v_all = sb12("v_all", [128, NT * 512], BF16)
            qt_sb = [sb12(f"qt{i}", [128, 2048], BF16) for i in range(8)]

            st01 = contextlib.ExitStack()
            with st01:
                def sb01(name, shape, dt):
                    return st01.enter_context(nc.sbuf_tensor(name, shape, dt))

                def ps01(name, shape, dt):
                    return st01.enter_context(nc.psum_tensor(name, shape, dt))

                wq_sb = sb01("wq_sb", [128, 8 * DIM], BF16)
                wk_sb = sb01("wk_sb", [128, 8 * DIM], BF16)
                wv_sb = sb01("wv_sb", [128, 8 * 512], BF16)
                wcT = sb01("wcT", [128, 8 * 16], BF16)
                bkR = sb01("bkR", [1, DIM], BF16)
                bqR = sb01("bqR", [1, DIM], BF16)
                bvR = sb01("bvR", [1, 512], BF16)
                g1bc = sb01("g1bc", [128, DIM], F32)
                b1bc = sb01("b1bc", [128, DIM], F32)
                bqvd = sb01("bqvd", [16, 1], F32)

                st00 = contextlib.ExitStack()
                st00.__enter__()

                def sb00(name, shape, dt):
                    return st00.enter_context(nc.sbuf_tensor(name, shape, dt))

                wsvrow = sb00("wsvrow", [1, 512], F32)
                wqv_blk = [sb00(f"wqv_blk{i}", [32, DIM], BF16) for i in range(2)]
                wsq_dup = sb00("wsq_dup", [64, 1], BF16)
                g1row = sb00("g1row", [1, DIM], F32)
                b1row = sb00("b1row", [1, DIM], F32)
                bsrow = sb00("bsrow", [1, 1], F32)
                alrow = sb00("alrow", [1, 1], F32)
                alclip = sb00("alclip", [1, 1], F32)
                wsvr64 = sb00("wsvr64", [1, 64], F32)
                wc_row = sb00("wc_row", [1, DIM], F32)
                wcomb16 = sb00("wcomb16", [16, DIM], BF16)
                bqvT_bf = sb00("bqvT_bf", [QK, H], BF16)
                bqv_ev = sb00("bqv_ev", [1, 16], F32)

                ps_a = st00.enter_context(nc.psum_tensor("ps_a", [128, 512], F32))
                ps_b = st00.enter_context(nc.psum_tensor("ps_b", [128, 512], F32))
                ps_c = st00.enter_context(nc.psum_tensor("ps_c", [128, 512], BF16))

                def emit_p0(P):
                    e = P.eng
                    phase_begin(P)
                    if P.on("gpsimd"):
                        e.memset(ident[:], 0.0)
                        e.memset(ident_f[:], 0.0)
                        e.affine_select(out=ident[:], in_=ident[:], compare_op=ALU.not_equal,
                                        fill=1.0, base=0, pattern=[[-1, 128]],
                                        channel_multiplier=1)
                        e.affine_select(out=ident_f[:], in_=ident_f[:], compare_op=ALU.not_equal,
                                        fill=1.0, base=0, pattern=[[-1, 16]],
                                        channel_multiplier=1)
                    if P.on("gpsimd"):
                        e.memset(ones_row[:], 1.0)
                        e.memset(ones_tok[:], 1.0)
                        e.memset(ones_col[:], 1.0)
                        for i_, v_ in enumerate(ACT_CONSTS):
                            e.memset(consts_sb[:, i_:i_ + 1], v_)
                    last = None
                    if P.on("gpsimd"):
                        last = e.memset(ones_f[:], 1.0)
                    MS["p0_const"] = producer(P, "gpsimd", last)

                    if P0_STEPS < 2:
                        P.eng.sem_inc(ph, 1)
                        return
                    dmas = []
                    for kk in range(8):
                        dmas.append((wk_sb[:, kk * DIM:(kk + 1) * DIM], wkT[kk * 128:(kk + 1) * 128, :]))
                        dmas.append((wq_sb[:, kk * DIM:(kk + 1) * DIM], wqT[kk * 128:(kk + 1) * 128, :]))
                        dmas.append((wv_sb[:, kk * 512:(kk + 1) * 512], wvT[kk * 128:(kk + 1) * 128, :]))
                    dmas += [(wsq_dup[0:32, :], wsT[0:32, :]),
                             (wsq_dup[32:64, :], wsT[0:32, :]),
                             (wsvr64[:], wsT[32:96, 0][None, :]),
                             (bqvT_bf[:], bqvT),
                             (bkR[:], bk_d[None, :]),
                             (bqR[:], bq_d[None, :]),
                             (bvR[:], bv_d[None, :]),
                             (g1row[:], g1_d[None, :]),
                             (b1row[:], b1_d[None, :]),
                             (bsrow[:], bs_d[None, :]),
                             (alrow[:], alpha_d[None, :])]
                    for dst, src in dmas:
                        MS["p0_dma"] = dma(P, "gpsimd", "dw", dst, src)

                    if P0_STEPS < 3:
                        if P.on("sync"):
                            P.eng.wait_ge(SEM["dw"], P.cnt("dw"))
                            P.eng.wait_ge(s_gp, P.cnt("gp"))
                        P.eng.sem_inc(ph, 1)
                        return
                    wait(P, "vector", "dw", MS["p0_dma"])
                    last = None
                    if P.on("vector"):
                        e.tensor_scalar(alclip[:], alrow[:], -0.5, 2.0, ALU.max, ALU.min)
                        for r_ in range(8):
                            e.tensor_copy(wsvrow[0:1, r_ * 64:(r_ + 1) * 64], wsvr64[:])
                        last = e.memset(wc_row[:], 0.0)
                    MS["p0_vsm"] = producer(P, "vector", last)

                    if P0_STEPS < 4:
                        if P.on("sync"):
                            P.eng.wait_ge(SEM["dw"], P.cnt("dw"))
                            P.eng.wait_ge(s_ve, P.cnt("ve"))
                        P.eng.sem_inc(ph, 1)
                        return
                    for half in range(2):
                        wait(P, "tensor", "dw", MS["p0_dma"])
                        wait(P, "tensor", "ve", MS["p0_vsm"] if half == 0 else MS["p0_gb"])
                        last = None
                        if P.on("tensor"):
                            e.matmul(ps_a[:], ones_f[:], g1row[0:1, half * 512:(half + 1) * 512],
                                     start=True, stop=True)
                            last = e.matmul(ps_b[:], ones_f[:], b1row[0:1, half * 512:(half + 1) * 512],
                                            start=True, stop=True)
                        vpe = producer(P, "tensor", last)
                        wait(P, "vector", "pe", vpe)
                        last = None
                        if P.on("vector"):
                            e.tensor_copy(g1bc[:, half * 512:(half + 1) * 512], ps_a[:])
                            last = e.tensor_copy(b1bc[:, half * 512:(half + 1) * 512], ps_b[:])
                        MS["p0_gb"] = producer(P, "vector", last)
                    wait(P, "tensor", "ve", MS["p0_gb"])
                    last = None
                    if P.on("tensor"):
                        e.matmul(ps_a[:], ones_f[:], wsvrow[:], start=True, stop=True)
                        e.matmul(ps_b[:, 0:1], ones_f[:], bsrow[:], start=True, stop=True)
                        last = e.matmul(ps_b[:, 2:3], ones_f[:], alclip[:], start=True, stop=True)
                    vpe = producer(P, "tensor", last)
                    wait(P, "vector", "pe", vpe)
                    last = None
                    if P.on("vector"):
                        e.tensor_copy(wsv_bc[:], ps_a[:])
                        e.tensor_copy(bs128[:], ps_b[:, 0:1])
                        last = e.tensor_copy(alpha128[:], ps_b[:, 2:3])
                    MS["p0_bc2"] = producer(P, "vector", last)

                    if P0_STEPS < 5:
                        if P.on("sync"):
                            P.eng.wait_ge(SEM["dw"], P.cnt("dw"))
                            P.eng.wait_ge(s_ve, P.cnt("ve"))
                        P.eng.sem_inc(ph, 1)
                        return
                    vWQM = {}
                    for h in range(16):
                        hb = h % 2
                        if P.on("gpsimd") and h >= 2:
                            e.wait_ge(s_pe, vWQM[h - 2])
                        vWQL = dma(P, "gpsimd", f"dq{hb}", wqv_blk[hb][:], wqv[h * 32:(h + 1) * 32, :])
                        wait(P, "tensor", "ve", P.cnt("ve"))
                        wait(P, "tensor", f"dq{hb}", vWQL)
                        last = None
                        if P.on("tensor"):
                            e.matmul(ps_b[0:1, 0:512], wsq_dup[0:32, :],
                                     wqv_blk[hb][:, 0:512], start=True, stop=True)
                            last = e.matmul(ps_a[0:1, 0:512], wsq_dup[0:32, :],
                                            wqv_blk[hb][:, 512:DIM], start=True, stop=True)
                        vpe = producer(P, "tensor", last)
                        vWQM[h] = vpe
                        wait(P, "vector", "pe", vpe)
                        if P.on("vector") and h >= 1:
                            e.wait_ge(SEM["dc"], P.cnt("dc"))
                        last = None
                        if P.on("vector"):
                            e.tensor_copy(wc_row[0:1, 0:512], ps_b[0:1, 0:512])
                            last = e.tensor_copy(wc_row[0:1, 512:DIM], ps_a[0:1, 0:512])
                        vve = producer(P, "vector", last)
                        wait(P, "gpsimd", "ve", vve)
                        MS["p0_wc"] = dma(P, "gpsimd", "dc", wcomb16[h:h + 1, :], wc_row[:])
                    if P0_STEPS < 6:
                        if P.on("sync"):
                            P.eng.wait_ge(SEM["dc"], P.cnt("dc"))
                            P.eng.wait_ge(s_ve, P.cnt("ve"))
                        P.eng.sem_inc(ph, 1)
                        return
                    for kk in range(8):
                        if kk == 0:
                            wait(P, "tensor", "dc", MS["p0_wc"])
                        else:
                            wait(P, "tensor", "ve", P.cnt("ve"))
                        last = None
                        if P.on("tensor"):
                            last = e.transpose(ps_c[:, 0:16], wcomb16[:, kk * 128:(kk + 1) * 128],
                                               ident[0:16, 0:16])
                        vpe = producer(P, "tensor", last)
                        wait(P, "vector", "pe", vpe)
                        last = None
                        if P.on("vector"):
                            last = e.tensor_copy(wcT[:, kk * 16:(kk + 1) * 16], ps_c[:, 0:16])
                        producer(P, "vector", last)
                    wait(P, "tensor", "ve", P.cnt("ve"))
                    last = None
                    if P.on("tensor"):
                        last = e.matmul(ps_a[0:1, 0:16], wsq_dup[0:32, :], bqvT_bf[0:32, :],
                                        start=True, stop=True)
                    vpe = producer(P, "tensor", last)
                    wait(P, "vector", "pe", vpe)
                    last = None
                    if P.on("vector"):
                        last = e.tensor_copy(bqv_ev[:], ps_a[0:1, 0:16])
                    vve = producer(P, "vector", last)
                    for h in range(16):
                        wait(P, "gpsimd", "ve", vve)
                        MS["p0_bqv"] = dma(P, "gpsimd", "dc", bqvd[h:h + 1, :], bqv_ev[0:1, h:h + 1])
                    wait(P, "vector", "dc", MS["p0_bqv"])
                    last = None
                    if P.on("vector"):
                        last = e.tensor_scalar(bqvd[:], bqvd[:], RS32, None, ALU.mult)
                    producer(P, "vector", last)
                    if P.on("sync"):
                        P.eng.wait_ge(s_ve, P.cnt("ve"))
                        for _s in ("dw", "dq", "dc"):
                            if P.cnt(_s) > 0:
                                P.eng.wait_ge(SEM[_s], P.cnt(_s))
                        P.eng.wait_ge(s_gp, P.cnt("gp"))
                    P.eng.sem_inc(ph, 1)

                run_block("p0", emit_p0)
                st00.__exit__(None, None, None)

                xbuf = [sb01(f"xbuf{i}", [128, DIM], F32) for i in range(2)]
                xnT = [sb01(f"xnT{k}", [128, 512], BF16) for k in range(8)]
                sq_t = sb01("sq_t", [128, DIM], F32)
                xc_t = [sb01(f"xc{i}", [128, DIM], F32) for i in range(2)]
                xg_t = sb01("xg_t", [128, DIM], F32)
                xn_bf = [sb01(f"xnbf{i}", [128, DIM], BF16) for i in range(2)]
                sm1 = [sb01(f"sm1{i}", [128, 8], F32) for i in range(2)]
                kst = [sb01(f"kst{i}", [128, 512], BF16) for i in range(4)]
                qvdT = sb01("qvdT", [16, 2048], F32)

                ps_tr = [ps01(f"ps_tr{i}", [128, 1024], BF16) for i in range(2)]
                ps_k = [ps01(f"ps_k{i}", [128, 512], F32) for i in range(2)]
                ps_v = [ps01(f"ps_v{i}", [128, 512], F32) for i in range(2)]
                ps_qv = ps01("ps_qv", [128, 512], F32)
                ps_sm = ps01("ps_sm", [128, 64], F32)

                def emit_p1(P):
                    e = P.eng
                    phase_begin(P)
                    vXN, vXC, vTR, vXNE, vVEV = {}, {}, {}, {}, {}
                    vKEV, vKSO, vQVE3, vQVS, vQVT = {}, {}, {}, {}, {}
                    pjg = {}

                    for kt in range(P1_TILES):
                        g, slot = kt // 4, kt % 4
                        own = kt < OWN_NT
                        pp = kt % 2

                        if P.on("sync") and kt >= 2:
                            e.wait_ge(s_ve, vXN[kt - 2])
                            e.wait_ge(s_sc, vXC[kt - 2])
                        vX = dma(P, "sync", f"dx{pp}", xbuf[pp][:], xb[kt * 128:(kt + 1) * 128, :])

                        wait(P, "vector", f"dx{pp}", vX)
                        last = None
                        if P.on("vector"):
                            s = sm1[pp]
                            e.reduce_sum(s[:, 0:1], xbuf[pp][:], AX)
                            e.tensor_mul(sq_t[:], xbuf[pp][:], xbuf[pp][:])
                            e.reduce_sum(s[:, 1:2], sq_t[:], AX)
                            e.tensor_scalar(s[:, 2:3], s[:, 0:1], float(-1.0 / DIM), None, ALU.mult)
                            e.tensor_mul(s[:, 4:5], s[:, 2:3], s[:, 2:3])
                            last = e.scalar_tensor_tensor(s[:, 3:4], s[:, 1:2], float(1.0 / DIM),
                                                          s[:, 4:5], ALU.mult, ALU.subtract)
                        vST = producer(P, "vector", last)
                        wait(P, "scalar", "ve", vST)
                        last = None
                        if P.on("scalar"):
                            s = sm1[pp]
                            e.activation(s[:, 7:8], s[:, 3:4], AF.Ln, bias=float(EPS1), scale=1.0)
                            last = e.activation(s[:, 5:6], s[:, 7:8], AF.Exp, bias=0.0, scale=-0.5)
                        vRS = producer(P, "scalar", last)
                        wait(P, "vector", "sc", vRS)
                        last = None
                        if P.on("vector"):
                            last = e.tensor_mul(sm1[pp][:, 6:7], sm1[pp][:, 2:3], sm1[pp][:, 5:6])
                        vNM = producer(P, "vector", last)
                        wait(P, "scalar", "ve", vNM)
                        if P.on("scalar") and kt >= 2:
                            e.wait_ge(s_ve, vXN[kt - 2])
                        last = None
                        if P.on("scalar"):
                            last = e.activation(xc_t[pp][:], xbuf[pp][:], AF.Identity,
                                                bias=sm1[pp][:, 6:7], scale=sm1[pp][:, 5:6])
                        vXC[kt] = producer(P, "scalar", last)
                        wait(P, "vector", "sc", vXC[kt])
                        if P.on("vector") and kt >= 2:
                            e.wait_ge(s_pe, vTR[kt - 2])
                        last = None
                        if P.on("vector"):
                            e.tensor_mul(xg_t[:], xc_t[pp][:], g1bc[:])
                            last = e.tensor_add(xn_bf[pp][:], xg_t[:], b1bc[:])
                        vXN[kt] = producer(P, "vector", last)

                        wait(P, "tensor", "ve", vXN[kt])
                        if P.on("tensor") and kt >= 2:
                            e.wait_ge(s_ve, vXNE[kt - 2])
                        last = None
                        if P.on("tensor"):
                            for kd in range(8):
                                last = e.transpose(ps_tr[pp][:, kd * 128:(kd + 1) * 128],
                                                   xn_bf[pp][:, kd * 128:(kd + 1) * 128], ident[:])
                        vTR[kt] = producer(P, "tensor", last)
                        wait(P, "vector", "pe", vTR[kt])
                        if P.on("vector") and g > 0 and slot == 0:
                            e.wait_ge(s_pe, pjg[g - 1])
                        last = None
                        if P.on("vector"):
                            for kd in range(8):
                                last = e.tensor_copy(xnT[kd][:, slot * 128:(slot + 1) * 128],
                                                     ps_tr[pp][:, kd * 128:(kd + 1) * 128])
                        vXNE[kt] = producer(P, "vector", last)

                        wait(P, "tensor", "ve", vXNE[kt])
                        if P.on("tensor") and kt >= 2:
                            e.wait_ge(s_ve, vVEV[kt - 2])
                        last = None
                        if P.on("tensor"):
                            for kk in range(8):
                                e.matmul(ps_v[pp][:], xnT[kk][:, slot * 128:(slot + 1) * 128],
                                         wv_sb[:, kk * 512:(kk + 1) * 512], start=(kk == 0), stop=False)
                            last = e.matmul(ps_v[pp][:], ones_row[:], bvR[:], start=False, stop=True)
                        vVP = producer(P, "tensor", last)
                        wait(P, "vector", "pe", vVP)
                        last = None
                        if P.on("vector"):
                            last = e.tensor_copy(v_all[:, kt * 512:(kt + 1) * 512], ps_v[pp][:])
                        vVEV[kt] = producer(P, "vector", last)

                        if own:
                            oi = kt
                            if P.on("tensor") and oi >= 1:
                                e.wait_ge(s_sc, vQVS[oi - 1])
                            last = None
                            if P.on("tensor"):
                                for kk in range(8):
                                    last = e.matmul(ps_qv[0:16, 0:128], wcT[:, kk * 16:(kk + 1) * 16],
                                                    xnT[kk][:, slot * 128:(slot + 1) * 128],
                                                    start=(kk == 0), stop=(kk == 7))
                            vQP = producer(P, "tensor", last)
                            wait(P, "scalar", "pe", vQP)
                            last = None
                            if P.on("scalar"):
                                last = e.activation(qvdT[:, oi * 128:(oi + 1) * 128], ps_qv[0:16, 0:128],
                                                    AF.Identity, bias=bqvd[:], scale=RS32)
                            vQVS[oi] = producer(P, "scalar", last)
                            wait(P, "tensor", "sc", vQVS[oi])
                            if P.on("tensor") and oi >= 1:
                                e.wait_ge(s_ve, vQVE3[oi - 1])
                            last = None
                            if P.on("tensor"):
                                last = e.transpose(ps_sm[:, 0:16], qvdT[0:16, oi * 128:(oi + 1) * 128],
                                                   ident_f[:])
                            vQVT[oi] = producer(P, "tensor", last)
                            wait(P, "vector", "pe", vQVT[oi])
                            last = None
                            if P.on("vector"):
                                last = e.tensor_copy(qvd_sb[:, oi * 16:(oi + 1) * 16], ps_sm[:, 0:16])
                            vQVE3[oi] = producer(P, "vector", last)

                        if slot == 3:
                            own_g = g < OWN_NT // 4
                            nchunks = 16 if own_g else 8
                            for ci in range(nchunks):
                                is_q = ci >= 8
                                mc = ci % 8
                                ck = P.bump("ckq")
                                kb = ck % 2
                                if P.on("tensor"):
                                    if ci == 0:
                                        e.wait_ge(s_ve, vXNE[kt])
                                    if ck >= 3:
                                        e.wait_ge(s_sc, vKEV[ck - 2])
                                last = None
                                if P.on("tensor"):
                                    wsrc = wq_sb if is_q else wk_sb
                                    bR = bqR if is_q else bkR
                                    for kk in range(8):
                                        e.matmul(ps_k[kb][:],
                                                 wsrc[:, kk * DIM + mc * 128:kk * DIM + (mc + 1) * 128],
                                                 xnT[kk][:], start=(kk == 0), stop=False)
                                    last = e.matmul(ps_k[kb][:], bR[0:1, mc * 128:(mc + 1) * 128],
                                                    ones_tok[:], start=False, stop=True)
                                vKP = producer(P, "tensor", last)
                                wait(P, "scalar", "pe", vKP)
                                last = None
                                if is_q:
                                    if P.on("scalar"):
                                        last = e.activation(qt_sb[mc][:, g * 512:(g + 1) * 512], ps_k[kb][:],
                                                            AF.Identity, bias=0.0, scale=0.125)
                                    vKEV[ck] = producer(P, "scalar", last)
                                else:
                                    ki = P.bump("ki") - 1
                                    if P.on("scalar") and ki >= 4:
                                        e.wait_ge(SEM[f"dk{ki % 4}"], vKSO[ki - 4])
                                    if P.on("scalar"):
                                        last = e.activation(kst[ki % 4][:], ps_k[kb][:], AF.Identity,
                                                            bias=0.0, scale=1.0)
                                    vKEV[ck] = producer(P, "scalar", last)
                                    wait(P, "sync", "sc", vKEV[ck])
                                    vKSO[ki] = dma(P, "sync", f"dk{ki % 4}",
                                                   kt_hbm[mc * 128:(mc + 1) * 128, g * 512:(g + 1) * 512],
                                                   kst[ki % 4][:])
                            pjg[g] = P.cnt("pe")

                    if P.on("sync"):
                        for _s in ("dx", "dk0", "dk1", "dk2", "dk3"):
                            if P.cnt(_s) > 0:
                                e.wait_ge(SEM[_s], P.cnt(_s))
                        e.wait_ge(s_ve, P.cnt("ve"))
                        e.wait_ge(s_sc, P.cnt("sc"))
                    P.eng.sem_inc(ph, 1)

                if PHASES >= 1:
                    run_block("p1", emit_p1)

                if dbg:
                    def emit_dbg1(P):
                        e = P.eng
                        phase_begin(P)
                        if "kt" in dbg:
                            dma(P, "sync", "dbg", dbg["kt"], kt_hbm)
                        if "v" in dbg:
                            dma(P, "sync", "dbg", dbg["v"], v_all[:])
                        if "qt" in dbg:
                            for i in range(8):
                                dma(P, "sync", "dbg", dbg["qt"][i * 128:(i + 1) * 128, :], qt_sb[i][:])
                        if "qvd" in dbg:
                            dma(P, "sync", "dbg", dbg["qvd"], qvd_sb[:])
                        if P.on("sync"):
                            e.wait_ge(SEM["dbg"], P.cnt("dbg"))
                        P.eng.sem_inc(ph, 1)

                    run_block("dbg1", emit_dbg1)

            of_T = [stOF.enter_context(nc.sbuf_tensor(f"ofT{i}", [128, 2048], BF16, side="right"))
                    for i in range(8)]

            if PHASES >= 2:
                st2 = contextlib.ExitStack()
                with st2:
                    def sb2(name, shape, dt):
                        return st2.enter_context(nc.sbuf_tensor(name, shape, dt))

                    def ps2(name, shape, dt):
                        return st2.enter_context(nc.psum_tensor(name, shape, dt))

                    ktb = [sb2(f"ktb{i}", [128, N], BF16) for i in range(2)]
                    PT = sb2("PT", [128, 2 * N], BF16)
                    on_bf = [sb2(f"onbf{i}", [128, 512], BF16) for i in range(2)]
                    tmp5 = [sb2(f"tmp5{i}", [128, 512], F32) for i in range(2)]
                    of_acc = [sb2(f"ofac{i}", [128, 64], F32) for i in range(2)]
                    of_bf = [sb2(f"ofbf{i}", [128, 64], BF16) for i in range(2)]
                    sm2 = [sb2(f"sm2{i}", [128, 32], F32) for i in range(2)]

                    spp = [ps2("spX", [128, 1024], F32), ps2("spY", [128, 1024], F32)]
                    av_ps = [ps2(f"av{i}", [128, 512], F32) for i in range(2)]
                    l_ps = [ps2(f"l{i}", [128, 8], F32) for i in range(2)]

                    NSTEP = 8 * 16

                    def emit_p2(P):
                        e = P.eng
                        phase_begin(P)
                        vKT, vSP, vEX, vAV = {}, {}, {}, {}
                        vCP, vCE, vCB, vOT, vOE = {}, {}, {}, {}, {}
                        sp_end = {}

                        def dve_tail(t):
                            for hh in range(2):
                                wait(P, "vector", "sc", vCE[(t, hh)])
                                if P.on("vector") and t >= 1:
                                    e.wait_ge(s_pe, vOT[(t - 1, hh)])
                                last = None
                                if P.on("vector"):
                                    s = sm2[hh]
                                    e.reciprocal(s[:, 3:4], s[:, 2:3])
                                    e.tensor_scalar(s[:, 24:32], s[:, 16:24], s[:, 3:4], None, ALU.mult)
                                    e.tensor_scalar(of_acc[hh][:], on_bf[hh][:, 0:64], s[:, 24:25],
                                                    None, ALU.mult)
                                    for r_ in range(1, 7):
                                        e.scalar_tensor_tensor(of_acc[hh][:],
                                                               on_bf[hh][:, r_ * 64:(r_ + 1) * 64],
                                                               s[:, 24 + r_:25 + r_], of_acc[hh][:],
                                                               ALU.mult, ALU.add)
                                    last = e.scalar_tensor_tensor(of_bf[hh][:], on_bf[hh][:, 448:512],
                                                                  s[:, 31:32], of_acc[hh][:],
                                                                  ALU.mult, ALU.add)
                                vCB[(t, hh)] = producer(P, "vector", last)

                        def act_comp(t):
                            for hh in range(2):
                                wait(P, "scalar", "ve", vCP[(t, hh)])
                                last = None
                                if P.on("scalar"):
                                    last = e.activation(sm2[hh][:, 16:24], sm2[hh][:, 8:16], AF.Exp,
                                                        bias=sm2[hh][:, 1:2], scale=1.0,
                                                        accum_out=sm2[hh][:, 2:3])
                                vCE[(t, hh)] = producer(P, "scalar", last)

                        def pe_oftr(t, hh):
                            wait(P, "tensor", "ve", vCB[(t, hh)])
                            last = None
                            if P.on("tensor"):
                                last = e.transpose(spp[hh][:].bitcast(BF16)[0:64, 0:128],
                                                   of_bf[hh][:], ident[:])
                            vOT[(t, hh)] = producer(P, "tensor", last)

                        def dve_ofevac(t):
                            c, qi = t // 16, t % 16
                            for hh in range(2):
                                wait(P, "vector", "pe", vOT[(t, hh)])
                                last = None
                                if P.on("vector"):
                                    last = e.tensor_copy(
                                        of_T[c][hh * 64:(hh + 1) * 64, qi * 128:(qi + 1) * 128],
                                        spp[hh][:].bitcast(BF16)[0:64, 0:128])
                                vOE[(t, hh)] = producer(P, "vector", last)

                        for t in range(NSTEP):
                            c, qi = t // 16, t % 16
                            if qi == 0:
                                if c == 0:
                                    vKT[0] = dma(P, "sync", "kt0", ktb[0][:], kt_hbm[0:128, :])
                                if c < 7:
                                    if P.on("sync") and c >= 1:
                                        e.wait_ge(s_pe, sp_end[c - 1])
                                    vKT[c + 1] = dma(P, "sync", f"kt{(c + 1) % 2}", ktb[(c + 1) % 2][:],
                                                     kt_hbm[(c + 1) * 128:(c + 2) * 128, :])

                            kb = c % 2
                            for st in range(16):
                                sp = spp[st % 2]
                                if P.on("tensor"):
                                    if st == 0 and qi == 0:
                                        e.wait_ge(SEM[f"kt{c % 2}"], vKT[c])
                                    if st < 2 and t >= 2:
                                        e.wait_ge(s_ve, vOE[(t - 2, st)])
                                    if (t, st - 2) in vEX or (t - 1, 15) in vEX:
                                        key = (t, st - 2) if st >= 2 else (t - 1, 14 + st)
                                        if key in vEX:
                                            e.wait_ge(s_sc, vEX[key])
                                last = None
                                if P.on("tensor"):
                                    for jj in range(4):
                                        jc = st * 4 + jj
                                        e.matmul(sp[:, jj * 128:(jj + 1) * 128],
                                                 ktb[kb][0:64, jc * 128:(jc + 1) * 128],
                                                 qt_sb[c][0:64, qi * 128:(qi + 1) * 128],
                                                 start=True, stop=True, tile_position=(0, 0))
                                        last = e.matmul(sp[:, 512 + jj * 128:512 + (jj + 1) * 128],
                                                        ktb[kb][64:128, jc * 128:(jc + 1) * 128],
                                                        qt_sb[c][64:128, qi * 128:(qi + 1) * 128],
                                                        start=True, stop=True, tile_position=(64, 0))
                                vSP[(t, st)] = producer(P, "tensor", last)
                                for hh in range(2):
                                    if P.on("scalar"):
                                        if st == 0 and hh == 0 and t >= 1:
                                            e.wait_ge(s_pe, vAV[(t - 1, 1)])
                                        e.wait_ge(s_pe, vSP[(t, st)])
                                    last = None
                                    if P.on("scalar"):
                                        last = e.activation(
                                            PT[:, hh * N + st * 512:hh * N + (st + 1) * 512],
                                            sp[:, hh * 512:(hh + 1) * 512], AF.Exp,
                                            bias=0.0, scale=1.0)
                                    vEX[(t, st)] = producer(P, "scalar", last)

                            if t >= 1:
                                act_comp(t - 1)
                                dve_tail(t - 1)

                            if P.on("tensor") and t >= 1:
                                e.wait_ge(s_ve, vCP[(t - 1, 0)])
                                e.wait_ge(s_ve, vCP[(t - 1, 1)])
                            lasts = [None, None]
                            for gg in range(16):
                                for hh in range(2):
                                    if P.on("tensor"):
                                        e.wait_ge(s_sc, vEX[(t, gg)])
                                        for jj in range(4):
                                            jc = gg * 4 + jj
                                            lhs = PT[:, hh * N + jc * 128:hh * N + (jc + 1) * 128]
                                            e.matmul(av_ps[hh][:], lhs, v_all[:, jc * 512:(jc + 1) * 512],
                                                     start=(jc == 0), stop=(jc == 63))
                                            lasts[hh] = e.matmul(l_ps[hh][:, 0:1], lhs, ones_col[:],
                                                                 start=(jc == 0), stop=(jc == 63))
                            for hh in range(2):
                                vAV[(t, hh)] = producer(P, "tensor", lasts[hh])

                            if t >= 1:
                                pe_oftr(t - 1, 0)
                                pe_oftr(t - 1, 1)
                                dve_ofevac(t - 1)

                            for hh in range(2):
                                h = 2 * c + hh
                                wait(P, "vector", "pe", vAV[(t, hh)])
                                if P.on("vector") and t >= 1:
                                    e.wait_ge(s_sc, vCE[(t - 1, hh)])
                                last = None
                                if P.on("vector"):
                                    s = sm2[hh]
                                    e.reciprocal(s[:, 0:1], l_ps[hh][:, 0:1])
                                    e.tensor_scalar(on_bf[hh][:], av_ps[hh][:], s[:, 0:1], None, ALU.mult)
                                    e.tensor_mul(tmp5[hh][:], on_bf[hh][:], wsv_bc[:])
                                    for r_ in range(8):
                                        e.reduce_sum(s[:, 8 + r_:9 + r_],
                                                     tmp5[hh][:, r_ * 64:(r_ + 1) * 64], AX)
                                    e.tensor_scalar(s[:, 8:16], s[:, 8:16],
                                                    qvd_sb[:, qi * 16 + h:qi * 16 + h + 1],
                                                    bs128[:, 0:1], ALU.add, ALU.add)
                                    last = e.reduce_max(s[:, 1:2], s[:, 8:16], AX, negate=True)
                                vCP[(t, hh)] = producer(P, "vector", last)

                            if qi == 15:
                                sp_end[c] = P.cnt("pe")

                        act_comp(NSTEP - 1)
                        dve_tail(NSTEP - 1)
                        pe_oftr(NSTEP - 1, 0)
                        pe_oftr(NSTEP - 1, 1)
                        dve_ofevac(NSTEP - 1)

                        if P.on("sync"):
                            e.wait_ge(s_ve, P.cnt("ve"))
                        P.eng.sem_inc(ph, 1)

                    run_block("p2", emit_p2)

                    if dbg and "oft" in dbg:
                        def emit_dbg2(P):
                            e = P.eng
                            phase_begin(P)
                            for i in range(8):
                                dma(P, "sync", "dbg", dbg["oft"][i * 128:(i + 1) * 128, :], of_T[i][:])
                            if P.on("sync"):
                                e.wait_ge(SEM["dbg"], P.cnt("dbg"))
                            P.eng.sem_inc(ph, 1)

                        run_block("dbg2", emit_dbg2)
        if PHASES >= 3:
            st3 = contextlib.ExitStack()
            with st3:
                def sb3(name, shape, dt):
                    return st3.enter_context(nc.sbuf_tensor(name, shape, dt))

                st3a = contextlib.ExitStack()
                with st3a:
                    def sb3a(name, shape, dt):
                        return st3a.enter_context(nc.sbuf_tensor(name, shape, dt))

                    def ps3a(name, shape, dt):
                        return st3a.enter_context(nc.psum_tensor(name, shape, dt))

                    wf_sb = sb3a("wf_sb", [128, 8 * DIM], BF16)
                    wr1_sb = sb3a("wr1_sb", [128, 8 * 2048], BF16)
                    wr2_sb = sb3a("wr2_sb", [128, 16 * DIM], BF16)
                    g2bc = sb3a("g2bc", [128, DIM], F32)
                    b2bc = sb3a("b2bc", [128, DIM], F32)
                    bfR = sb3a("bfR", [1, DIM], BF16)
                    br1R = sb3a("br1R", [1, 2 * DIM], BF16)
                    br2R = sb3a("br2R", [1, DIM], BF16)
                    x3buf = [sb3a(f"x3b{i}", [128, DIM], F32) for i in range(2)]
                    row3 = x3buf[0][0:1, :]
                    out2b = [sb3a(f"out2b{i}", [128, DIM], F32) for i in range(2)]
                    xpo2b = [sb3a(f"xpo2b{i}", [128, DIM], F32) for i in range(4)]
                    on2bf = [sb3a(f"on2bf{i}", [128, DIM], BF16) for i in range(2)]
                    onT = [sb3a(f"onT{k}", [128, 512], BF16) for k in range(8)]
                    h1T = [sb3a(f"h1T{k}", [128, 512], BF16) for k in range(16)]
                    x1b = [sb3a(f"x1b{i}", [128, DIM], F32) for i in range(2)]
                    x1t = out2b
                    sq3 = sb3a("sq3", [128, DIM], F32)
                    sm3 = [sb3a(f"sm3{i}", [128, 16], F32) for i in range(2)]

                    ps_m = [ps3a(f"ps_m{i}", [128, 1024], F32) for i in range(2)]
                    ps_f = [ps3a(f"ps_f{i}", [128, 512], F32) for i in range(2)]
                    ps_t3 = [ps3a(f"ps_t3{i}", [128, 1024], BF16) for i in range(2)]

                    def emit_p3a1(P):
                        e = P.eng
                        phase_begin(P)
                        dmas = []
                        for kk in range(8):
                            dmas.append((wf_sb[:, kk * DIM:(kk + 1) * DIM], wfT[kk * 128:(kk + 1) * 128, :]))
                            dmas.append((wr1_sb[:, kk * 2048:(kk + 1) * 2048], wr1T[kk * 128:(kk + 1) * 128, :]))
                        for kk in range(16):
                            dmas.append((wr2_sb[:, kk * DIM:(kk + 1) * DIM], wr2T[kk * 128:(kk + 1) * 128, :]))
                        dmas += [(bfR[:], bf_d[None, :]), (br1R[:], br1_d[None, :]), (br2R[:], br2_d[None, :])]
                        vW3 = 0
                        for dst, src in dmas:
                            vW3 = dma(P, "gpsimd", "dw", dst, src)
                        vBCe = 0
                        for dsrc, bc in ((g2_d, g2bc), (b2_d, b2bc)):
                            vRL = dma(P, "gpsimd", "dc", row3, dsrc[None, :])
                            for half in range(2):
                                wait(P, "tensor", "dc", vRL)
                                wait(P, "tensor", "ve", vBCe)
                                last = None
                                if P.on("tensor"):
                                    last = e.matmul(ps_m[0][:, 0:512], ones_f[:],
                                                    x3buf[0][0:1, half * 512:(half + 1) * 512],
                                                    start=True, stop=True)
                                vBC = producer(P, "tensor", last)
                                wait(P, "vector", "pe", vBC)
                                last = None
                                if P.on("vector"):
                                    last = e.tensor_copy(bc[:, half * 512:(half + 1) * 512], ps_m[0][:, 0:512])
                                vBCe = producer(P, "vector", last)
                            if P.on("gpsimd"):
                                e.wait_ge(s_pe, P.cnt("pe"))

                        vO2, vLN2F, vTRA, vTEV = {}, {}, {}, {}
                        vX1, vX1S = {}, {}
                        pmEv = {0: vBCe}
                        pmU = [0]
                        pfEv = {}
                        pfU = [0]
                        ptRel = {}
                        ptU = [0]
                        f1g = {}
                        ffn2g = {}
                        h1ready = {}

                        for g3 in range(4):
                            for tl in range(4):
                                qi = g3 * 4 + tl
                                pp = qi % 2

                                if P.on("sync"):
                                    if qi >= 2:
                                        e.wait_ge(s_ve, vO2[qi - 2])
                                    else:
                                        e.wait_ge(s_ve, vBCe)
                                vX3 = dma(P, "sync", f"dx3{pp}", x3buf[pp][:], xb[qi * 128:(qi + 1) * 128, :])

                                pmU[0] += 1
                                u = pmU[0]
                                wait(P, "tensor", "dw", vW3)
                                if P.on("tensor"):
                                    e.wait_ge(s_ve, pmEv.get(u - 2, vBCe))
                                last = None
                                if P.on("tensor"):
                                    for half in range(2):
                                        for kk in range(8):
                                            e.matmul(ps_m[(u - 1) % 2][:, half * 512:(half + 1) * 512],
                                                     of_T[kk][:, qi * 128:(qi + 1) * 128],
                                                     wf_sb[:, kk * DIM + half * 512:kk * DIM + (half + 1) * 512],
                                                     start=(kk == 0), stop=False)
                                        last = e.matmul(ps_m[(u - 1) % 2][:, half * 512:(half + 1) * 512],
                                                        ones_row[:], bfR[0:1, half * 512:(half + 1) * 512],
                                                        start=False, stop=True)
                                vWF = producer(P, "tensor", last)
                                wait(P, "vector", "pe", vWF)
                                wait(P, "vector", f"dx3{pp}", vX3)
                                last = None
                                if P.on("vector"):
                                    e.tensor_add(out2b[pp][:], ps_m[(u - 1) % 2][:], x3buf[pp][:])
                                    last = e.tensor_add(xpo2b[tl][:], out2b[pp][:], x3buf[pp][:])
                                vO2[qi] = producer(P, "vector", last)
                                pmEv[u] = vO2[qi]

                                last = None
                                if P.on("vector"):
                                    s = sm3[pp]
                                    e.reduce_sum(s[:, 0:1], out2b[pp][:], AX)
                                    e.tensor_mul(sq3[:], out2b[pp][:], out2b[pp][:])
                                    e.reduce_sum(s[:, 1:2], sq3[:], AX)
                                    e.tensor_scalar(s[:, 2:3], s[:, 0:1], float(-1.0 / DIM), None, ALU.mult)
                                    e.tensor_mul(s[:, 4:5], s[:, 2:3], s[:, 2:3])
                                    last = e.scalar_tensor_tensor(s[:, 3:4], s[:, 1:2], float(1.0 / DIM),
                                                                  s[:, 4:5], ALU.mult, ALU.subtract)
                                vst = producer(P, "vector", last)
                                wait(P, "scalar", "ve", vst)
                                last = None
                                if P.on("scalar"):
                                    s = sm3[pp]
                                    e.activation(s[:, 7:8], s[:, 3:4], AF.Ln, bias=float(EPS1), scale=1.0)
                                    last = e.activation(s[:, 5:6], s[:, 7:8], AF.Exp, bias=0.0, scale=-0.5)
                                vrs = producer(P, "scalar", last)
                                wait(P, "vector", "sc", vrs)
                                if P.on("vector") and qi >= 2:
                                    e.wait_ge(s_pe, vTRA[qi - 2])
                                last = None
                                if P.on("vector"):
                                    e.tensor_scalar(sq3[:], out2b[pp][:], sm3[pp][:, 2:3],
                                                    sm3[pp][:, 5:6], ALU.add, ALU.mult)
                                    e.tensor_mul(sq3[:], sq3[:], g2bc[:])
                                    last = e.tensor_add(on2bf[pp][:], sq3[:], b2bc[:])
                                vLN2F[qi] = producer(P, "vector", last)

                                ptU[0] += 1
                                up = ptU[0]
                                wait(P, "tensor", "ve", vLN2F[qi])
                                if P.on("tensor") and up >= 3:
                                    e.wait_ge(s_ve, ptRel[up - 2])
                                last = None
                                if P.on("tensor"):
                                    for kd in range(8):
                                        last = e.transpose(ps_t3[(up - 1) % 2][:, kd * 128:(kd + 1) * 128],
                                                           on2bf[pp][:, kd * 128:(kd + 1) * 128], ident[:])
                                vTRA[qi] = producer(P, "tensor", last)
                                wait(P, "vector", "pe", vTRA[qi])
                                if P.on("vector") and g3 > 0:
                                    e.wait_ge(s_pe, f1g[g3 - 1])
                                last = None
                                if P.on("vector"):
                                    for kd in range(8):
                                        last = e.tensor_copy(onT[kd][:, tl * 128:(tl + 1) * 128],
                                                             ps_t3[(up - 1) % 2][:, kd * 128:(kd + 1) * 128])
                                vTEV[qi] = producer(P, "vector", last)
                                ptRel[up] = vTEV[qi]

                            for mc in range(16):
                                pfU[0] += 1
                                uf = pfU[0]
                                if P.on("tensor"):
                                    if mc == 0:
                                        e.wait_ge(s_ve, vTEV[g3 * 4 + 3])
                                    if uf >= 3:
                                        e.wait_ge(s_ve, pfEv[uf - 2])
                                last = None
                                if P.on("tensor"):
                                    for kk in range(8):
                                        e.matmul(ps_f[(uf - 1) % 2][:],
                                                 wr1_sb[:, kk * 2048 + mc * 128:kk * 2048 + (mc + 1) * 128],
                                                 onT[kk][:], start=(kk == 0), stop=False)
                                    last = e.matmul(ps_f[(uf - 1) % 2][:], br1R[0:1, mc * 128:(mc + 1) * 128],
                                                    ones_tok[:], start=False, stop=True)
                                vF1 = producer(P, "tensor", last)
                                wait(P, "vector", "pe", vF1)
                                if P.on("vector") and g3 > 0:
                                    e.wait_ge(s_pe, ffn2g[g3 - 1])
                                last = None
                                if P.on("vector"):
                                    last = e.tensor_scalar(h1T[mc][:], ps_f[(uf - 1) % 2][:],
                                                           0.0, None, ALU.max)
                                vF1E = producer(P, "vector", last)
                                pfEv[uf] = vF1E
                            h1ready[g3] = pfEv[pfU[0]]
                            f1g[g3] = P.cnt("pe")

                            for tl in range(4):
                                qi = g3 * 4 + tl
                                pp = qi % 2
                                pmU[0] += 1
                                u = pmU[0]
                                if P.on("tensor"):
                                    e.wait_ge(s_ve, h1ready[g3])
                                    if u >= 3:
                                        e.wait_ge(s_ve, pmEv[u - 2])
                                last = None
                                if P.on("tensor"):
                                    for half in range(2):
                                        for mc in range(16):
                                            e.matmul(ps_m[(u - 1) % 2][:, half * 512:(half + 1) * 512],
                                                     h1T[mc][:, tl * 128:(tl + 1) * 128],
                                                     wr2_sb[:, mc * DIM + half * 512:mc * DIM + (half + 1) * 512],
                                                     start=(mc == 0), stop=False)
                                        last = e.matmul(ps_m[(u - 1) % 2][:, half * 512:(half + 1) * 512],
                                                        ones_row[:], br2R[0:1, half * 512:(half + 1) * 512],
                                                        start=False, stop=True)
                                vFF2 = producer(P, "tensor", last)
                                if tl == 3:
                                    ffn2g[g3] = P.cnt("pe")

                                wait(P, "vector", "pe", vFF2)
                                last = None
                                if P.on("vector"):
                                    s = sm3[pp]
                                    e.tensor_add(x1t[pp][:], ps_m[(u - 1) % 2][:], xpo2b[tl][:])
                                    e.reduce_sum(s[:, 8:9], x1t[pp][:], AX)
                                    e.tensor_mul(sq3[:], x1t[pp][:], x1t[pp][:])
                                    e.reduce_sum(s[:, 9:10], sq3[:], AX)
                                    e.tensor_scalar(s[:, 10:11], s[:, 8:9], float(-1.0 / DIM), None, ALU.mult)
                                    e.tensor_mul(s[:, 12:13], s[:, 10:11], s[:, 10:11])
                                    last = e.scalar_tensor_tensor(s[:, 11:12], s[:, 9:10], float(1.0 / DIM),
                                                                  s[:, 12:13], ALU.mult, ALU.subtract)
                                vst2 = producer(P, "vector", last)
                                pmEv[u] = vst2
                                wait(P, "scalar", "ve", vst2)
                                last = None
                                if P.on("scalar"):
                                    s = sm3[pp]
                                    e.activation(s[:, 15:16], s[:, 11:12], AF.Ln, bias=float(EPS1), scale=1.0)
                                    last = e.activation(s[:, 13:14], s[:, 15:16], AF.Exp, bias=0.0, scale=-0.5)
                                vrs2 = producer(P, "scalar", last)
                                wait(P, "vector", "sc", vrs2)
                                if P.on("vector") and qi >= 2:
                                    e.wait_ge(SEM[f"x1s{qi % 2}"], vX1S[qi - 2])
                                last = None
                                if P.on("vector"):
                                    last = e.tensor_scalar(x1b[pp][:], x1t[pp][:], sm3[pp][:, 10:11],
                                                           sm3[pp][:, 13:14], ALU.add, ALU.mult)
                                vX1[qi] = producer(P, "vector", last)
                                wait(P, "sync", "ve", vX1[qi])
                                vX1S[qi] = dma(P, "sync", f"x1s{qi % 2}", x1_hbm[qi * 128:(qi + 1) * 128, :], x1b[pp][:])

                        if P.on("sync"):
                            for _s in ("dx30", "dx31", "x1s0", "x1s1"):
                                if P.cnt(_s) > 0:
                                    e.wait_ge(SEM[_s], P.cnt(_s))
                            e.wait_ge(s_ve, P.cnt("ve"))
                        P.eng.sem_inc(ph, 1)

                    run_block("p3a1", emit_p3a1)
                stOF.close()

                nrmT = [sb3(f"nrmT{k}", [128, 2048], BF16) for k in range(8)]
                cmb_sb = sb3("cmb_sb", [128, OWN_NT * 8], F32)

                st3r = contextlib.ExitStack()
                with st3r:
                    def sb3r(name, shape, dt):
                        return st3r.enter_context(nc.sbuf_tensor(name, shape, dt))

                    rt_sb = sb3r("rt_sb", [128, 64], BF16)
                    nwbc = sb3r("nwbc", [128, DIM], F32)
                    rowr = sb3r("rowr", [1, DIM], F32)
                    x1r = [sb3r(f"x1r{i}", [128, DIM], F32) for i in range(2)]
                    nbf = [sb3r(f"nbf{i}", [128, DIM], BF16) for i in range(2)]
                    sqr = sb3r("sqr", [128, DIM], F32)
                    rsm = [sb3r(f"rsm{i}", [128, 40], F32) for i in range(2)]

                    ps_rt = [st3r.enter_context(nc.psum_tensor(f"ps_rt{i}", [128, 512], F32)) for i in range(2)]
                    ps_tr2 = [st3r.enter_context(nc.psum_tensor(f"ps_tr2{i}", [128, 1024], BF16)) for i in range(2)]

                    def emit_p3a2(P):
                        e = P.eng
                        phase_begin(P)
                        vRL = 0
                        for kk in range(8):
                            vRL = dma(P, "gpsimd", "dw", rt_sb[:, kk * 8:(kk + 1) * 8],
                                      routerT[kk * 128:(kk + 1) * 128, :])
                        vRL2 = dma(P, "gpsimd", "dw", rowr[:], nw_d[None, :])
                        vBCe = 0
                        for half in range(2):
                            wait(P, "tensor", "dw", vRL2)
                            wait(P, "tensor", "ve", vBCe)
                            last = None
                            if P.on("tensor"):
                                last = e.matmul(ps_rt[0][:], ones_f[:], rowr[0:1, half * 512:(half + 1) * 512],
                                                start=True, stop=True)
                            vBC = producer(P, "tensor", last)
                            wait(P, "vector", "pe", vBC)
                            last = None
                            if P.on("vector"):
                                last = e.tensor_copy(nwbc[:, half * 512:(half + 1) * 512], ps_rt[0][:])
                            vBCe = producer(P, "vector", last)

                        vXR, vNBF, vRTE, vNTE = {}, {}, {}, {}
                        ptRel2 = {}
                        pfRel2 = {}
                        for qi in range(OWN_NT):
                            pp = qi % 2
                            if P.on("sync") and qi >= 2:
                                e.wait_ge(s_ve, vNBF[qi - 2])
                            vXR[qi] = dma(P, "sync", f"x1l{pp}", x1r[pp][:], x1_hbm[qi * 128:(qi + 1) * 128, :])

                            wait(P, "vector", f"x1l{pp}", vXR[qi])
                            last = None
                            if P.on("vector"):
                                e.tensor_mul(sqr[:], x1r[pp][:], x1r[pp][:])
                                last = e.reduce_sum(rsm[pp][:, 7:8], sqr[:], AX)
                            vr1 = producer(P, "vector", last)
                            wait(P, "scalar", "ve", vr1)
                            last = None
                            if P.on("scalar"):
                                e.activation(rsm[pp][:, 6:7], rsm[pp][:, 7:8], AF.Ln,
                                             bias=float(EPS2 * DIM), scale=1.0)
                                last = e.activation(rsm[pp][:, 5:6], rsm[pp][:, 6:7], AF.Exp,
                                                    bias=float(0.5 * np.log(DIM)), scale=-0.5)
                            vr2 = producer(P, "scalar", last)
                            wait(P, "vector", "sc", vr2)
                            if P.on("vector") and qi >= 2:
                                e.wait_ge(s_pe, vRTE[qi - 2])
                            last = None
                            if P.on("vector"):
                                last = e.scalar_tensor_tensor(nbf[pp][:], x1r[pp][:], rsm[pp][:, 5:6],
                                                              nwbc[:], ALU.mult, ALU.mult)
                            vNBF[qi] = producer(P, "vector", last)

                            wait(P, "tensor", "ve", vNBF[qi])
                            if P.on("tensor") and qi >= 2:
                                e.wait_ge(s_ve, ptRel2[qi - 2])
                            last = None
                            if P.on("tensor"):
                                for kd in range(8):
                                    last = e.transpose(ps_tr2[pp][:, kd * 128:(kd + 1) * 128],
                                                       nbf[pp][:, kd * 128:(kd + 1) * 128], ident[:])
                            vRTE[qi] = producer(P, "tensor", last)
                            wait(P, "vector", "pe", vRTE[qi])
                            last = None
                            if P.on("vector"):
                                for kd in range(8):
                                    last = e.tensor_copy(nrmT[kd][:, qi * 128:(qi + 1) * 128],
                                                         ps_tr2[pp][:, kd * 128:(kd + 1) * 128])
                            vNTE[qi] = producer(P, "vector", last)
                            ptRel2[qi] = vNTE[qi]

                            wait(P, "tensor", "ve", vNTE[qi])
                            if P.on("tensor") and qi >= 2:
                                e.wait_ge(s_sc, pfRel2[qi - 2])
                            last = None
                            if P.on("tensor"):
                                for kk in range(8):
                                    last = e.matmul(ps_rt[pp][:, 0:8],
                                                    nrmT[kk][:, qi * 128:(qi + 1) * 128],
                                                    rt_sb[:, kk * 8:(kk + 1) * 8],
                                                    start=(kk == 0), stop=(kk == 7))
                            vRT = producer(P, "tensor", last)
                            wait(P, "vector", "pe", vRT)
                            last = None
                            if P.on("vector"):
                                last = e.reduce_max(rsm[pp][:, 0:1], ps_rt[pp][:, 0:8], AX, negate=True)
                            vrt1 = producer(P, "vector", last)
                            wait(P, "scalar", "ve", vrt1)
                            last = None
                            if P.on("scalar"):
                                last = e.activation(rsm[pp][:, 8:16], ps_rt[pp][:, 0:8],
                                                    AF.Exp, bias=rsm[pp][:, 0:1],
                                                    accum_out=rsm[pp][:, 1:2])
                            vrt2 = producer(P, "scalar", last)
                            pfRel2[qi] = vrt2
                            wait(P, "vector", "sc", vrt2)
                            last = None
                            if P.on("vector"):
                                s = rsm[pp]
                                e.reciprocal(s[:, 2:3], s[:, 1:2])
                                e.tensor_scalar(s[:, 16:24], s[:, 8:16], s[:, 2:3], None, ALU.mult)
                                e.reduce_max(s[:, 3:4], s[:, 16:24], AX)
                                e.tensor_scalar(s[:, 24:32], s[:, 16:24], s[:, 3:4], None, ALU.is_equal)
                                e.scalar_tensor_tensor(s[:, 8:16], s[:, 24:32], -1e9, s[:, 16:24],
                                                       ALU.mult, ALU.add)
                                e.reduce_max(s[:, 4:5], s[:, 8:16], AX)
                                e.tensor_scalar(s[:, 32:40], s[:, 8:16], s[:, 4:5], None, ALU.is_equal)
                                e.tensor_add(s[:, 24:32], s[:, 24:32], s[:, 32:40])
                                e.tensor_add(s[:, 5:6], s[:, 3:4], s[:, 4:5])
                                e.reciprocal(s[:, 6:7], s[:, 5:6])
                                e.tensor_mul(s[:, 32:40], s[:, 16:24], s[:, 24:32])
                                last = e.tensor_scalar(cmb_sb[:, qi * 8:(qi + 1) * 8], s[:, 32:40],
                                                       s[:, 6:7], None, ALU.mult)
                            producer(P, "vector", last)

                        if P.on("sync"):
                            e.wait_ge(s_ve, P.cnt("ve"))
                        P.eng.sem_inc(ph, 1)

                    run_block("p3a2", emit_p3a2)

                    if dbg and ("x1" in dbg or "nrm" in dbg or "cmb" in dbg):
                        def emit_dbg3(P):
                            e = P.eng
                            phase_begin(P)
                            if "x1" in dbg:
                                dma(P, "sync", "dbg", dbg["x1"], x1_hbm)
                            if "nrm" in dbg:
                                for i in range(8):
                                    dma(P, "sync", "dbg", dbg["nrm"][i * 128:(i + 1) * 128, :], nrmT[i][:])
                            if "cmb" in dbg:
                                dma(P, "sync", "dbg", dbg["cmb"], cmb_sb[:])
                            if P.on("sync"):
                                e.wait_ge(SEM["dbg"], P.cnt("dbg"))
                            P.eng.sem_inc(ph, 1)

                        run_block("dbg3", emit_dbg3)

                if PHASES >= 4:
                    st3b = contextlib.ExitStack()
                    with st3b:
                        def sb3b(name, shape, dt):
                            return st3b.enter_context(nc.sbuf_tensor(name, shape, dt))

                        def ps3b(name, shape, dt):
                            return st3b.enter_context(nc.psum_tensor(name, shape, dt))

                        hid_sb = sb3b("hid_sb", [128, 22 * DIM], BF16)
                        dn_sb = sb3b("dn_sb", [128, 22 * DIM], BF16)
                        gu_g = [sb3b(f"gug{i}", [128, 2048], BF16) for i in range(2)]
                        gu_u = [sb3b(f"guu{i}", [128, 2048], BF16) for i in range(2)]
                        moe = sb3b("moe", [128, 8 * DIM], F32)
                        sil = [sb3b(f"sil{i}", [128, 512], F32) for i in range(2)]
                        x1l = [sb3b(f"x1l{i}", [128, DIM], F32) for i in range(2)]
                        ysc = [sb3b(f"ysc{i}", [128, DIM], F32) for i in range(2)]
                        ybuf = x1l
                        sq4 = hid_sb[:, 0:DIM]
                        sm4 = [sb3b(f"sm4{i}", [128, 8], F32) for i in range(2)]

                        ps_g = [ps3b(f"ps_g{i}", [128, 512], F32) for i in range(2)]
                        ps_u = [ps3b(f"ps_u{i}", [128, 512], F32) for i in range(2)]
                        ps_dn = [ps3b(f"ps_dn{i}", [128, 1024], F32) for i in range(2)]

                        def emit_p3b(P):
                            e = P.eng
                            phase_begin(P)
                            vGL, vUL, vDL = {}, {}, {}
                            vSIL, vHID, vDNE = {}, {}, {}
                            slabPE = {}
                            slabI = [0]
                            vDN_pe = {}
                            guU = [0]
                            dnU = [0]
                            vYST = {}
                            vFINv = {}
                            vYBv = {}

                            for hf in range(2):
                                for ee in range(NE):
                                    for hs in range(11):
                                        si = slabI[0]
                                        slabI[0] += 1
                                        gb = si % 2
                                        if P.on("gpsimd"):
                                            prev = slabPE.get(si - 2, 0)
                                            if prev > 0:
                                                e.wait_ge(s_pe, prev)
                                        for kk in range(8):
                                            vGL[(hf, ee, hs)] = dma(
                                                P, "gpsimd", f"dgu{gb}", gu_g[gb][:, kk * 256:(kk + 1) * 256],
                                                guT[ee][kk * 128:(kk + 1) * 128, hs * 256:(hs + 1) * 256])
                                        for kk in range(8):
                                            vUL[(hf, ee, hs)] = dma(
                                                P, "gpsimd", f"dgu{gb}", gu_u[gb][:, kk * 256:(kk + 1) * 256],
                                                guT[ee][kk * 128:(kk + 1) * 128,
                                                        HID + hs * 256:HID + (hs + 1) * 256])
                                        for mk in (2 * hs, 2 * hs + 1):
                                            if mk < 22:
                                                if P.on("gpsimd") and mk == 0 and (hf + ee) > 0:
                                                    e.wait_ge(s_pe, vDN_pe.get((hf, ee - 1),
                                                                               vDN_pe.get((hf - 1, NE - 1), 0)))
                                                vDL[(hf, ee)] = dma(P, "gpsimd", "dn",
                                                                    dn_sb[:, mk * DIM:(mk + 1) * DIM],
                                                                    dnT[ee][mk * 128:(mk + 1) * 128, :])
                                        for mc2 in range(2):
                                            for tc in range(2):
                                                guU[0] += 1
                                                ug = guU[0]
                                                wait(P, "tensor", f"dgu{gb}", vUL[(hf, ee, hs)])
                                                if P.on("tensor") and ug >= 3:
                                                    e.wait_ge(s_ve, vHID.get(ug - 2, 0))
                                                last = None
                                                if P.on("tensor"):
                                                    for kk in range(8):
                                                        e.matmul(ps_g[(ug - 1) % 2][:],
                                                                 gu_g[gb][:, kk * 256 + mc2 * 128:kk * 256 + (mc2 + 1) * 128],
                                                                 nrmT[kk][:, hf * 1024 + tc * 512:hf * 1024 + (tc + 1) * 512],
                                                                 start=(kk == 0), stop=(kk == 7))
                                                    for kk in range(8):
                                                        last = e.matmul(ps_u[(ug - 1) % 2][:],
                                                                        gu_u[gb][:, kk * 256 + mc2 * 128:kk * 256 + (mc2 + 1) * 128],
                                                                        nrmT[kk][:, hf * 1024 + tc * 512:hf * 1024 + (tc + 1) * 512],
                                                                        start=(kk == 0), stop=(kk == 7))
                                                vGUP = producer(P, "tensor", last)
                                                slabPE[si] = vGUP
                                                wait(P, "scalar", "pe", vGUP)
                                                if P.on("scalar") and ug >= 3:
                                                    e.wait_ge(s_ve, vHID.get(ug - 2, 0))
                                                last = None
                                                if P.on("scalar"):
                                                    last = e.activation(sil[(ug - 1) % 2][:],
                                                                        ps_g[(ug - 1) % 2][:], AF.Silu,
                                                                        bias=0.0, scale=1.0)
                                                vSIL[ug] = producer(P, "scalar", last)
                                                wait(P, "vector", "sc", vSIL[ug])
                                                if P.on("vector") and hs == 0 and mc2 == 0 and tc == 0:
                                                    pv = vDN_pe.get((hf, ee - 1),
                                                                    vDN_pe.get((hf - 1, NE - 1), 0))
                                                    if pv > 0:
                                                        e.wait_ge(s_pe, pv)
                                                mkc = 2 * hs + mc2
                                                last = None
                                                if P.on("vector"):
                                                    last = e.tensor_mul(
                                                        hid_sb[:, mkc * DIM + tc * 512:mkc * DIM + (tc + 1) * 512],
                                                        sil[(ug - 1) % 2][:], ps_u[(ug - 1) % 2][:])
                                                vHID[ug] = producer(P, "vector", last)
                                    hid_done = vHID[guU[0]]

                                    for tl in range(8):
                                        qi = hf * 8 + tl
                                        dnU[0] += 1
                                        ud = dnU[0]
                                        wait(P, "tensor", "ve", hid_done)
                                        wait(P, "tensor", "dn", vDL[(hf, ee)])
                                        if P.on("tensor") and ud >= 3:
                                            e.wait_ge(s_ve, vDNE.get(ud - 2, 0))
                                        last = None
                                        if P.on("tensor"):
                                            for half in range(2):
                                                for mk in range(22):
                                                    last = e.matmul(
                                                        ps_dn[(ud - 1) % 2][:, half * 512:(half + 1) * 512],
                                                        hid_sb[:, mk * DIM + tl * 128:mk * DIM + (tl + 1) * 128],
                                                        dn_sb[:, mk * DIM + half * 512:mk * DIM + (half + 1) * 512],
                                                        start=(mk == 0), stop=(mk == 21))
                                        vDN = producer(P, "tensor", last)
                                        if tl == 7:
                                            vDN_pe[(hf, ee)] = P.cnt("pe")
                                        wait(P, "vector", "pe", vDN)
                                        last = None
                                        if P.on("vector"):
                                            cc = cmb_sb[:, qi * 8 + ee:qi * 8 + ee + 1]
                                            if ee == 0:
                                                last = e.tensor_scalar(moe[:, tl * DIM:(tl + 1) * DIM],
                                                                       ps_dn[(ud - 1) % 2][:], cc, None, ALU.mult)
                                            else:
                                                last = e.scalar_tensor_tensor(moe[:, tl * DIM:(tl + 1) * DIM],
                                                                              ps_dn[(ud - 1) % 2][:], cc,
                                                                              moe[:, tl * DIM:(tl + 1) * DIM],
                                                                              ALU.mult, ALU.add)
                                        vDNE[ud] = producer(P, "vector", last)

                                for tl in range(8):
                                    qi = hf * 8 + tl
                                    pp = tl % 2
                                    if P.on("sync") and qi >= 2:
                                        e.wait_ge(SEM[f"yst{qi % 2}"], vYST[qi - 2])
                                    vXL = dma(P, "sync", f"x1l{pp}", x1l[pp][:], x1_hbm[qi * 128:(qi + 1) * 128, :])
                                    wait(P, "vector", f"x1l{pp}", vXL)
                                    last = None
                                    if P.on("vector"):
                                        s = sm4[pp]
                                        e.tensor_scalar(ysc[pp][:], moe[:, tl * DIM:(tl + 1) * DIM],
                                                        alpha128[:, 0:1], None, ALU.mult)
                                        e.scalar_tensor_tensor(ysc[pp][:], x1l[pp][:], 2.0, ysc[pp][:],
                                                               ALU.mult, ALU.add)
                                        e.reduce_sum(s[:, 0:1], ysc[pp][:], AX)
                                        e.tensor_mul(sq4, ysc[pp][:], ysc[pp][:])
                                        e.reduce_sum(s[:, 1:2], sq4, AX)
                                        e.tensor_scalar(s[:, 2:3], s[:, 0:1], float(-1.0 / DIM), None, ALU.mult)
                                        e.tensor_mul(s[:, 4:5], s[:, 2:3], s[:, 2:3])
                                        last = e.scalar_tensor_tensor(s[:, 3:4], s[:, 1:2], float(1.0 / DIM),
                                                                      s[:, 4:5], ALU.mult, ALU.subtract)
                                    vfs = producer(P, "vector", last)
                                    vFINv[qi] = vfs
                                    wait(P, "scalar", "ve", vfs)
                                    last = None
                                    if P.on("scalar"):
                                        s = sm4[pp]
                                        e.activation(s[:, 7:8], s[:, 3:4], AF.Ln, bias=float(EPS1), scale=1.0)
                                        last = e.activation(s[:, 5:6], s[:, 7:8], AF.Exp, bias=0.0, scale=-0.5)
                                    vfr = producer(P, "scalar", last)
                                    wait(P, "vector", "sc", vfr)
                                    last = None
                                    if P.on("vector"):
                                        last = e.tensor_scalar(ybuf[pp][:], ysc[pp][:], sm4[pp][:, 2:3],
                                                               sm4[pp][:, 5:6], ALU.add, ALU.mult)
                                    vyb = producer(P, "vector", last)
                                    vYBv[qi] = vyb
                                    wait(P, "sync", "ve", vyb)
                                    vYST[qi] = dma(P, "sync", f"yst{qi % 2}", y[qi * 128:(qi + 1) * 128, :], ybuf[pp][:])

                            if P.on("sync"):
                                e.wait_ge(SEM["yst0"], P.cnt("yst0"))
                                e.wait_ge(SEM["yst1"], P.cnt("yst1"))
                            P.eng.sem_inc(ph, 1)

                        run_block("p3b", emit_p3b)
        stOF.close()

    return nc


def kernel(**inp):
    f32 = np.float32
    x = np.asarray(inp["L0_attn_x"], f32)
    tr = lambda a: np.ascontiguousarray(np.asarray(a, f32).T)
    shared = {
        "wqT": tr(inp["L0_attn_Wq"]), "wkT": tr(inp["L0_attn_Wk"]),
        "wvT": tr(inp["L0_attn_Wv"]),
        "wqv": np.ascontiguousarray(np.asarray(inp["L0_attn_Wqv"], f32)),
        "wsT": tr(inp["L0_attn_Ws"]),
        "bqvT": np.ascontiguousarray(np.asarray(inp["L0_attn_bqv"], f32).reshape(H, QK).T),
        "wfT": tr(inp["L0_attn_Wf"]), "wr1T": tr(inp["L0_attn_Wr1"]), "wr2T": tr(inp["L0_attn_Wr2"]),
        "routerT": tr(inp["L0_ffn_router_weight"]),
        "guT": np.ascontiguousarray(np.asarray(inp["L0_ffn_gate_up_weight"], f32).transpose(0, 2, 1)),
        "dnT": np.ascontiguousarray(np.asarray(inp["L0_ffn_down_weight"], f32).transpose(0, 2, 1)),
        "bq": np.asarray(inp["L0_attn_bq"], f32), "bk": np.asarray(inp["L0_attn_bk"], f32),
        "bv": np.asarray(inp["L0_attn_bv"], f32),
        "bs": np.asarray(inp["L0_attn_bs"], f32).reshape(1),
        "bf": np.asarray(inp["L0_attn_bf"], f32),
        "br1": np.asarray(inp["L0_attn_br1"], f32), "br2": np.asarray(inp["L0_attn_br2"], f32),
        "g1": np.asarray(inp["L0_attn_g1"], f32), "b1": np.asarray(inp["L0_attn_b1"], f32),
        "g2": np.asarray(inp["L0_attn_g2"], f32), "b2": np.asarray(inp["L0_attn_b2"], f32),
        "nw": np.asarray(inp["L0_ffn_norm_weight"], f32),
        "alpha": np.asarray(inp["L0_ffn_residual_alpha"], f32).reshape(1),
    }
    in_maps = []
    for c in range(8):
        b = c // 4
        qo = (c % 4) * 2048
        xp = np.ascontiguousarray(
            np.concatenate([x[b, qo:qo + 2048], x[b, :qo], x[b, qo + 2048:]], axis=0))
        m = dict(shared)
        m["x_b"] = xp
        in_maps.append(m)

    nc = build_program()
    res = run_bass_kernel_spmd(nc, in_maps, core_ids=list(range(8)))
    out = np.empty((B, N, DIM), f32)
    for c in range(8):
        b = c // 4
        qo = (c % 4) * 2048
        out[b, qo:qo + 2048] = res.results[c]["y"]
    return out


# version 05
# speedup vs baseline: 1.1849x; 1.0839x over previous
import contextlib
import numpy as np
import concourse.bass as bass
import concourse.mybir as mybir
from concourse.bass_utils import run_bass_kernel_spmd

F32 = mybir.dt.float32
BF16 = mybir.dt.bfloat16
AF = mybir.ActivationFunctionType
ALU = mybir.AluOpType
AX = mybir.AxisListType.X

B, N, DIM = 2, 8192, 1024
H, HD, R, QK = 16, 64, 8, 32
HID = 2816
NE = 8
NT = 64
OWN_NT = 16
EPS1 = 1e-5
EPS2 = 1e-6
RS32 = float(1.0 / np.sqrt(32.0))

PHASES = 4
P0_STEPS = 99
P1_TILES = NT
DEBUG = {}

ENG = ("gpsimd", "vector", "scalar", "tensor", "sync")


CHAINED_OPS = frozenset([
    "memset", "memzero", "tensor_copy", "tensor_tensor", "tensor_add",
    "tensor_sub", "tensor_mul", "tensor_max", "tensor_scalar",
    "tensor_single_scalar", "tensor_scalar_add", "tensor_scalar_sub",
    "tensor_scalar_mul", "tensor_scalar_max", "tensor_scalar_min",
    "tensor_relu", "tensor_reduce", "reduce_max", "reduce_sum",
    "scalar_tensor_tensor", "reciprocal", "select", "copy_predicated",
    "bn_stats", "bn_aggr", "activation", "add", "mul", "copy", "square",
    "sqrt", "sign", "affine_select", "iota", "stream_shuffle",
])
_IN_KWS = ("bias", "scale", "alpha", "scalar1", "scalar2", "in_", "in0", "in1", "mask")


def _ap_name(a):
    try:
        return a.tensor.name
    except AttributeError:
        return None


class AutoChain:

    def __init__(self, eng, state, chain_sem):
        self._eng = eng
        self._st = state
        self._sem = chain_sem

    def note_update(self, inst, sem, val):
        rec = self._st["insts"].get(id(inst.ins))
        if rec is not None and rec[1] is None:
            self._st["insts"][id(inst.ins)] = (rec[0], (sem, val))

    def _edge(self, key):
        st = self._st
        rec = st["insts"].get(key)
        if rec is None:
            return
        idx, upd = rec
        if idx <= st["synced"]:
            return
        if upd is None:
            st["chain_n"] = st.get("chain_n", 0) + 1
            st["hinst"][key].then_inc(self._sem, 1)
            upd = (self._sem, st["chain_n"])
            st["insts"][key] = (idx, upd)
        self._eng.wait_ge(upd[0], upd[1])
        st["synced"] = max(st["synced"], idx)

    def __getattr__(self, name):
        fn = getattr(self._eng, name)
        if name not in CHAINED_OPS:
            return fn
        st = self._st

        def wrapped(*args, **kw):
            outs, ins = [], []
            seen_out = False
            for a in args:
                nm = _ap_name(a)
                if nm is None:
                    continue
                if not seen_out:
                    outs.append(nm)
                    seen_out = True
                else:
                    ins.append(nm)
            for k, v in kw.items():
                nm = _ap_name(v)
                if nm is None:
                    continue
                if k == "accum_out" or k == "out":
                    outs.append(nm)
                elif k in _IN_KWS:
                    ins.append(nm)
            for nm in ins:
                w = st["writer"].get(nm)
                if w is not None:
                    self._edge(w)
            for nm in outs:
                w = st["writer"].get(nm)
                if w is not None:
                    self._edge(w)
                for r in st["readers"].get(nm, ()):
                    self._edge(r)
            inst = fn(*args, **kw)
            st["idx"] = st.get("idx", 0) + 1
            key = id(inst.ins)
            st["insts"][key] = (st["idx"], None)
            st["hinst"][key] = inst
            for nm in outs:
                st["writer"][nm] = key
                st["readers"][nm] = []
            for nm in ins:
                st["readers"].setdefault(nm, []).append(key)
            return inst

        return wrapped


class Ctx:
    def __init__(self, eng, which, counts):
        self.eng = eng
        self.w = which
        self.n = counts

    def on(self, name):
        return self.w == name

    def bump(self, key, amt=1):
        self.n[key] = self.n.get(key, 0) + amt
        return self.n[key]

    def cnt(self, key):
        return self.n.get(key, 0)


def build_program():
    nc = bass.Bass("TRN2", target_bir_lowering=False)

    xb = nc.dram_tensor("x_b", [N, DIM], F32, kind="ExternalInput").ap()
    wqT = nc.dram_tensor("wqT", [DIM, DIM], F32, kind="ExternalInput").ap()
    wkT = nc.dram_tensor("wkT", [DIM, DIM], F32, kind="ExternalInput").ap()
    wvT = nc.dram_tensor("wvT", [DIM, R * HD], F32, kind="ExternalInput").ap()
    wqv = nc.dram_tensor("wqv", [QK * H, DIM], F32, kind="ExternalInput").ap()
    wsT = nc.dram_tensor("wsT", [HD + QK, 1], F32, kind="ExternalInput").ap()
    bqvT = nc.dram_tensor("bqvT", [QK, H], F32, kind="ExternalInput").ap()
    wfT = nc.dram_tensor("wfT", [DIM, DIM], F32, kind="ExternalInput").ap()
    wr1T = nc.dram_tensor("wr1T", [DIM, 2 * DIM], F32, kind="ExternalInput").ap()
    wr2T = nc.dram_tensor("wr2T", [2 * DIM, DIM], F32, kind="ExternalInput").ap()
    routerT = nc.dram_tensor("routerT", [DIM, NE], F32, kind="ExternalInput").ap()
    guT = nc.dram_tensor("guT", [NE, DIM, 2 * HID], F32, kind="ExternalInput").ap()
    dnT = nc.dram_tensor("dnT", [NE, HID, DIM], F32, kind="ExternalInput").ap()
    bq_d = nc.dram_tensor("bq", [DIM], F32, kind="ExternalInput").ap()
    bk_d = nc.dram_tensor("bk", [DIM], F32, kind="ExternalInput").ap()
    bv_d = nc.dram_tensor("bv", [R * HD], F32, kind="ExternalInput").ap()
    bs_d = nc.dram_tensor("bs", [1], F32, kind="ExternalInput").ap()
    bf_d = nc.dram_tensor("bf", [DIM], F32, kind="ExternalInput").ap()
    br1_d = nc.dram_tensor("br1", [2 * DIM], F32, kind="ExternalInput").ap()
    br2_d = nc.dram_tensor("br2", [DIM], F32, kind="ExternalInput").ap()
    g1_d = nc.dram_tensor("g1", [DIM], F32, kind="ExternalInput").ap()
    b1_d = nc.dram_tensor("b1", [DIM], F32, kind="ExternalInput").ap()
    g2_d = nc.dram_tensor("g2", [DIM], F32, kind="ExternalInput").ap()
    b2_d = nc.dram_tensor("b2", [DIM], F32, kind="ExternalInput").ap()
    nw_d = nc.dram_tensor("nw", [DIM], F32, kind="ExternalInput").ap()
    alpha_d = nc.dram_tensor("alpha", [1], F32, kind="ExternalInput").ap()

    y = nc.dram_tensor("y", [OWN_NT * 128, DIM], F32, kind="ExternalOutput").ap()
    kt_hbm = nc.dram_tensor("kt_scratch", [DIM, N], BF16, kind="Internal").ap()
    x1_hbm = nc.dram_tensor("x1_scratch", [OWN_NT * 128, DIM], F32, kind="Internal").ap()

    dbg = {}
    dbg_specs = {"kt": ([DIM, N], BF16), "v": ([128, NT * 512], BF16),
                 "qt": ([DIM, 2048], BF16), "qvd": ([128, OWN_NT * 16], F32),
                 "oft": ([DIM, 2048], BF16), "x1": ([OWN_NT * 128, DIM], F32),
                 "nrm": ([DIM, 2048], BF16), "cmb": ([128, OWN_NT * 8], F32)}
    for k_, (shape, dt) in dbg_specs.items():
        if DEBUG.get(k_):
            dbg[k_] = nc.dram_tensor("dbg_" + k_, shape, dt, kind="ExternalOutput").ap()

    CNTS = {w: {} for w in ENG}
    MS = {}

    ES = contextlib.ExitStack()
    with ES:
        def sbuf(name, shape, dt, side="left"):
            return ES.enter_context(nc.sbuf_tensor(name, shape, dt, side=side))

        s_pe = ES.enter_context(nc.semaphore("s_pe"))
        s_ve = ES.enter_context(nc.semaphore("s_ve"))
        s_sc = ES.enter_context(nc.semaphore("s_sc"))
        s_gp = ES.enter_context(nc.semaphore("s_gp"))
        ph = ES.enter_context(nc.semaphore("ph"))
        stOF = contextlib.ExitStack()
        PH = [0]
        SEM = {"pe": s_pe, "ve": s_ve, "sc": s_sc, "gp": s_gp}
        KEY = {"tensor": "pe", "vector": "ve", "scalar": "sc", "gpsimd": "gp"}
        DSTREAMS = ["dx0", "dx1", "dk0", "dk1", "dk2", "dk3", "kt0", "kt1",
                    "dx30", "dx31", "x1s0", "x1s1", "x1l0", "x1l1",
                    "yst0", "yst1", "dw", "dq0", "dq1", "dc",
                    "dgu0", "dgu1", "dn", "dbg", "oft0", "oft1"]
        for _s in DSTREAMS:
            SEM[_s] = ES.enter_context(nc.semaphore("sd_" + _s))

        CH_SEM = {w: ES.enter_context(nc.semaphore("ch_" + w))
                  for w in ("gpsimd", "vector", "scalar")}
        CH_ST = {w: {"idx": 0, "chain_n": 0, "synced": 0, "writer": {},
                     "readers": {}, "insts": {}, "hinst": {}}
                 for w in ("gpsimd", "vector", "scalar")}

        def mkeng(e, w):
            if w in CH_SEM:
                return AutoChain(e, CH_ST[w], CH_SEM[w])
            return e

        def run_block(name, emit):
            with nc.Block(name) as blk:
                blk.gpsimd(lambda e: emit(Ctx(mkeng(e, "gpsimd"), "gpsimd", CNTS["gpsimd"])))
                blk.vector(lambda e: emit(Ctx(mkeng(e, "vector"), "vector", CNTS["vector"])))
                blk.scalar(lambda e: emit(Ctx(mkeng(e, "scalar"), "scalar", CNTS["scalar"])))
                blk.tensor(lambda e: emit(Ctx(e, "tensor", CNTS["tensor"])))
                blk.sync(lambda e: emit(Ctx(e, "sync", CNTS["sync"])))
            PH[0] += 5

        def producer(P, engname, last_inst):
            k = KEY[engname]
            v = P.bump(k)
            if P.on(engname) and last_inst is not None:
                last_inst.then_inc(SEM[k], 1)
                if isinstance(P.eng, AutoChain):
                    P.eng.note_update(last_inst, SEM[k], v)
            return v

        def dma(P, qname, stream, dst, src):
            v = P.bump(stream, 16)
            if P.on(qname):
                P.eng.dma_start(dst, src).then_inc(SEM[stream], 16)
            return v

        def wait(P, engname, semkey, val):
            if P.on(engname) and val > 0:
                assert val <= P.cnt(semkey), f"future wait {semkey} {val}>{P.cnt(semkey)}"
                P.eng.wait_ge(SEM[semkey], val)

        def phase_begin(P):
            if PH[0] > 0:
                P.eng.wait_ge(ph, PH[0])

        def phase_end(P, extra=()):
            for semkey, v in extra:
                if v > 0:
                    P.eng.wait_ge(SEM[semkey], v)
            P.eng.sem_inc(ph, 1)

        ident = sbuf("ident", [128, 128], BF16, side="right")
        ident_f = sbuf("ident_f", [16, 16], F32, side="right")
        ones_row = sbuf("ones_row", [1, 128], BF16, side="right")
        ones_tok = sbuf("ones_tok", [1, 512], BF16, side="right")
        ones_col = sbuf("ones_col", [128, 1], BF16, side="right")
        ones_f = sbuf("ones_f", [1, 128], F32, side="right")
        wsv_bc = sbuf("wsv_bc", [128, 512], BF16, side="right")
        bs128 = sbuf("bs128", [128, 1], F32, side="right")
        alpha128 = sbuf("alpha128", [128, 1], F32, side="right")
        qvd_sb = sbuf("qvd", [128, OWN_NT * 16], F32, side="right")
        of_T = None

        ACT_CONSTS = [float(EPS1), 1.0, 0.0, -0.5, 0.125, RS32,
                      float(EPS2 * DIM), float(0.5 * np.log(DIM)), -1e9]
        consts_sb = sbuf("consts_sb", [128, len(ACT_CONSTS)], F32, side="right")
        for i_, v_ in enumerate(ACT_CONSTS):
            nc.const_aps.register(F32, v_, consts_sb[:, i_:i_ + 1])

        st12 = contextlib.ExitStack()
        with st12:
            def sb12(name, shape, dt):
                return st12.enter_context(nc.sbuf_tensor(name, shape, dt))

            v_all = sb12("v_all", [128, NT * 512], BF16)
            qt_sb = [sb12(f"qt{i}", [128, 2048], BF16) for i in range(8)]

            st01 = contextlib.ExitStack()
            with st01:
                def sb01(name, shape, dt):
                    return st01.enter_context(nc.sbuf_tensor(name, shape, dt))

                def ps01(name, shape, dt):
                    return st01.enter_context(nc.psum_tensor(name, shape, dt))

                wq_sb = sb01("wq_sb", [128, 8 * DIM], BF16)
                wk_sb = sb01("wk_sb", [128, 8 * DIM], BF16)
                wv_sb = sb01("wv_sb", [128, 8 * 512], BF16)
                wcT = sb01("wcT", [128, 8 * 16], BF16)
                bkR = sb01("bkR", [1, DIM], BF16)
                bqR = sb01("bqR", [1, DIM], BF16)
                bvR = sb01("bvR", [1, 512], BF16)
                g1bc = sb01("g1bc", [128, DIM], F32)
                b1bc = sb01("b1bc", [128, DIM], F32)
                bqvd = sb01("bqvd", [16, 1], F32)

                st00 = contextlib.ExitStack()
                st00.__enter__()

                def sb00(name, shape, dt):
                    return st00.enter_context(nc.sbuf_tensor(name, shape, dt))

                wsvrow = sb00("wsvrow", [1, 512], F32)
                wqv_blk = [sb00(f"wqv_blk{i}", [32, DIM], BF16) for i in range(2)]
                wsq_dup = sb00("wsq_dup", [64, 1], BF16)
                g1row = sb00("g1row", [1, DIM], F32)
                b1row = sb00("b1row", [1, DIM], F32)
                bsrow = sb00("bsrow", [1, 1], F32)
                alrow = sb00("alrow", [1, 1], F32)
                alclip = sb00("alclip", [1, 1], F32)
                wsvr64 = sb00("wsvr64", [1, 64], F32)
                wc_row = sb00("wc_row", [1, DIM], F32)
                wcomb16 = sb00("wcomb16", [16, DIM], BF16)
                bqvT_bf = sb00("bqvT_bf", [QK, H], BF16)
                bqv_ev = sb00("bqv_ev", [1, 16], F32)

                ps_a = st00.enter_context(nc.psum_tensor("ps_a", [128, 512], F32))
                ps_b = st00.enter_context(nc.psum_tensor("ps_b", [128, 512], F32))
                ps_c = st00.enter_context(nc.psum_tensor("ps_c", [128, 512], BF16))

                def emit_p0(P):
                    e = P.eng
                    phase_begin(P)
                    if P.on("gpsimd"):
                        e.memset(ident[:], 0.0)
                        e.memset(ident_f[:], 0.0)
                        e.affine_select(out=ident[:], in_=ident[:], compare_op=ALU.not_equal,
                                        fill=1.0, base=0, pattern=[[-1, 128]],
                                        channel_multiplier=1)
                        e.affine_select(out=ident_f[:], in_=ident_f[:], compare_op=ALU.not_equal,
                                        fill=1.0, base=0, pattern=[[-1, 16]],
                                        channel_multiplier=1)
                    if P.on("gpsimd"):
                        e.memset(ones_row[:], 1.0)
                        e.memset(ones_tok[:], 1.0)
                        e.memset(ones_col[:], 1.0)
                        for i_, v_ in enumerate(ACT_CONSTS):
                            e.memset(consts_sb[:, i_:i_ + 1], v_)
                    last = None
                    if P.on("gpsimd"):
                        last = e.memset(ones_f[:], 1.0)
                    MS["p0_const"] = producer(P, "gpsimd", last)

                    if P0_STEPS < 2:
                        P.eng.sem_inc(ph, 1)
                        return
                    dmas = []
                    for kk in range(8):
                        dmas.append((wk_sb[:, kk * DIM:(kk + 1) * DIM], wkT[kk * 128:(kk + 1) * 128, :]))
                        dmas.append((wq_sb[:, kk * DIM:(kk + 1) * DIM], wqT[kk * 128:(kk + 1) * 128, :]))
                        dmas.append((wv_sb[:, kk * 512:(kk + 1) * 512], wvT[kk * 128:(kk + 1) * 128, :]))
                    dmas += [(wsq_dup[0:32, :], wsT[0:32, :]),
                             (wsq_dup[32:64, :], wsT[0:32, :]),
                             (wsvr64[:], wsT[32:96, 0][None, :]),
                             (bqvT_bf[:], bqvT),
                             (bkR[:], bk_d[None, :]),
                             (bqR[:], bq_d[None, :]),
                             (bvR[:], bv_d[None, :]),
                             (g1row[:], g1_d[None, :]),
                             (b1row[:], b1_d[None, :]),
                             (bsrow[:], bs_d[None, :]),
                             (alrow[:], alpha_d[None, :])]
                    for dst, src in dmas:
                        MS["p0_dma"] = dma(P, "gpsimd", "dw", dst, src)

                    if P0_STEPS < 3:
                        if P.on("sync"):
                            P.eng.wait_ge(SEM["dw"], P.cnt("dw"))
                            P.eng.wait_ge(s_gp, P.cnt("gp"))
                        P.eng.sem_inc(ph, 1)
                        return
                    wait(P, "vector", "dw", MS["p0_dma"])
                    last = None
                    if P.on("vector"):
                        e.tensor_scalar(alclip[:], alrow[:], -0.5, 2.0, ALU.max, ALU.min)
                        for r_ in range(8):
                            e.tensor_copy(wsvrow[0:1, r_ * 64:(r_ + 1) * 64], wsvr64[:])
                        last = e.memset(wc_row[:], 0.0)
                    MS["p0_vsm"] = producer(P, "vector", last)

                    if P0_STEPS < 4:
                        if P.on("sync"):
                            P.eng.wait_ge(SEM["dw"], P.cnt("dw"))
                            P.eng.wait_ge(s_ve, P.cnt("ve"))
                        P.eng.sem_inc(ph, 1)
                        return
                    for half in range(2):
                        wait(P, "tensor", "dw", MS["p0_dma"])
                        wait(P, "tensor", "ve", MS["p0_vsm"] if half == 0 else MS["p0_gb"])
                        last = None
                        if P.on("tensor"):
                            e.matmul(ps_a[:], ones_f[:], g1row[0:1, half * 512:(half + 1) * 512],
                                     start=True, stop=True)
                            last = e.matmul(ps_b[:], ones_f[:], b1row[0:1, half * 512:(half + 1) * 512],
                                            start=True, stop=True)
                        vpe = producer(P, "tensor", last)
                        wait(P, "vector", "pe", vpe)
                        last = None
                        if P.on("vector"):
                            e.tensor_copy(g1bc[:, half * 512:(half + 1) * 512], ps_a[:])
                            last = e.tensor_copy(b1bc[:, half * 512:(half + 1) * 512], ps_b[:])
                        MS["p0_gb"] = producer(P, "vector", last)
                    wait(P, "tensor", "ve", MS["p0_gb"])
                    last = None
                    if P.on("tensor"):
                        e.matmul(ps_a[:], ones_f[:], wsvrow[:], start=True, stop=True)
                        e.matmul(ps_b[:, 0:1], ones_f[:], bsrow[:], start=True, stop=True)
                        last = e.matmul(ps_b[:, 2:3], ones_f[:], alclip[:], start=True, stop=True)
                    vpe = producer(P, "tensor", last)
                    wait(P, "vector", "pe", vpe)
                    last = None
                    if P.on("vector"):
                        e.tensor_copy(wsv_bc[:], ps_a[:])
                        e.tensor_copy(bs128[:], ps_b[:, 0:1])
                        last = e.tensor_copy(alpha128[:], ps_b[:, 2:3])
                    MS["p0_bc2"] = producer(P, "vector", last)

                    if P0_STEPS < 5:
                        if P.on("sync"):
                            P.eng.wait_ge(SEM["dw"], P.cnt("dw"))
                            P.eng.wait_ge(s_ve, P.cnt("ve"))
                        P.eng.sem_inc(ph, 1)
                        return
                    vWQM = {}
                    for h in range(16):
                        hb = h % 2
                        if P.on("gpsimd") and h >= 2:
                            e.wait_ge(s_pe, vWQM[h - 2])
                        vWQL = dma(P, "gpsimd", f"dq{hb}", wqv_blk[hb][:], wqv[h * 32:(h + 1) * 32, :])
                        wait(P, "tensor", "ve", P.cnt("ve"))
                        wait(P, "tensor", f"dq{hb}", vWQL)
                        last = None
                        if P.on("tensor"):
                            e.matmul(ps_b[0:1, 0:512], wsq_dup[0:32, :],
                                     wqv_blk[hb][:, 0:512], start=True, stop=True)
                            last = e.matmul(ps_a[0:1, 0:512], wsq_dup[0:32, :],
                                            wqv_blk[hb][:, 512:DIM], start=True, stop=True)
                        vpe = producer(P, "tensor", last)
                        vWQM[h] = vpe
                        wait(P, "vector", "pe", vpe)
                        if P.on("vector") and h >= 1:
                            e.wait_ge(SEM["dc"], P.cnt("dc"))
                        last = None
                        if P.on("vector"):
                            e.tensor_copy(wc_row[0:1, 0:512], ps_b[0:1, 0:512])
                            last = e.tensor_copy(wc_row[0:1, 512:DIM], ps_a[0:1, 0:512])
                        vve = producer(P, "vector", last)
                        wait(P, "gpsimd", "ve", vve)
                        MS["p0_wc"] = dma(P, "gpsimd", "dc", wcomb16[h:h + 1, :], wc_row[:])
                    if P0_STEPS < 6:
                        if P.on("sync"):
                            P.eng.wait_ge(SEM["dc"], P.cnt("dc"))
                            P.eng.wait_ge(s_ve, P.cnt("ve"))
                        P.eng.sem_inc(ph, 1)
                        return
                    for kk in range(8):
                        if kk == 0:
                            wait(P, "tensor", "dc", MS["p0_wc"])
                        else:
                            wait(P, "tensor", "ve", P.cnt("ve"))
                        last = None
                        if P.on("tensor"):
                            last = e.transpose(ps_c[:, 0:16], wcomb16[:, kk * 128:(kk + 1) * 128],
                                               ident[0:16, 0:16])
                        vpe = producer(P, "tensor", last)
                        wait(P, "vector", "pe", vpe)
                        last = None
                        if P.on("vector"):
                            last = e.tensor_copy(wcT[:, kk * 16:(kk + 1) * 16], ps_c[:, 0:16])
                        producer(P, "vector", last)
                    wait(P, "tensor", "ve", P.cnt("ve"))
                    last = None
                    if P.on("tensor"):
                        last = e.matmul(ps_a[0:1, 0:16], wsq_dup[0:32, :], bqvT_bf[0:32, :],
                                        start=True, stop=True)
                    vpe = producer(P, "tensor", last)
                    wait(P, "vector", "pe", vpe)
                    last = None
                    if P.on("vector"):
                        last = e.tensor_copy(bqv_ev[:], ps_a[0:1, 0:16])
                    vve = producer(P, "vector", last)
                    for h in range(16):
                        wait(P, "gpsimd", "ve", vve)
                        MS["p0_bqv"] = dma(P, "gpsimd", "dc", bqvd[h:h + 1, :], bqv_ev[0:1, h:h + 1])
                    wait(P, "vector", "dc", MS["p0_bqv"])
                    last = None
                    if P.on("vector"):
                        last = e.tensor_scalar(bqvd[:], bqvd[:], RS32, None, ALU.mult)
                    producer(P, "vector", last)
                    if P.on("sync"):
                        P.eng.wait_ge(s_ve, P.cnt("ve"))
                        for _s in ("dw", "dq", "dc"):
                            if P.cnt(_s) > 0:
                                P.eng.wait_ge(SEM[_s], P.cnt(_s))
                        P.eng.wait_ge(s_gp, P.cnt("gp"))
                    P.eng.sem_inc(ph, 1)

                run_block("p0", emit_p0)
                st00.__exit__(None, None, None)

                xbuf = [sb01(f"xbuf{i}", [128, DIM], F32) for i in range(2)]
                xnT = [sb01(f"xnT{k}", [128, 512], BF16) for k in range(8)]
                sq_t = sb01("sq_t", [128, DIM], F32)
                xc_t = [sb01(f"xc{i}", [128, DIM], F32) for i in range(2)]
                xg_t = sb01("xg_t", [128, DIM], F32)
                xn_bf = [sb01(f"xnbf{i}", [128, DIM], BF16) for i in range(2)]
                sm1 = [sb01(f"sm1{i}", [128, 8], F32) for i in range(2)]
                kst = [sb01(f"kst{i}", [128, 512], BF16) for i in range(4)]
                qvdT = sb01("qvdT", [16, 2048], F32)

                ps_tr = [ps01(f"ps_tr{i}", [128, 1024], BF16) for i in range(2)]
                ps_k = [ps01(f"ps_k{i}", [128, 512], F32) for i in range(2)]
                ps_v = [ps01(f"ps_v{i}", [128, 512], F32) for i in range(2)]
                ps_qv = ps01("ps_qv", [128, 512], F32)
                ps_sm = ps01("ps_sm", [128, 64], F32)

                def emit_p1(P):
                    e = P.eng
                    phase_begin(P)
                    vXN, vXC, vTR, vXNE, vVEV = {}, {}, {}, {}, {}
                    vKEV, vKSO, vQVE3, vQVS, vQVT = {}, {}, {}, {}, {}
                    pjg = {}

                    for kt in range(P1_TILES):
                        g, slot = kt // 4, kt % 4
                        own = kt < OWN_NT
                        pp = kt % 2

                        if P.on("sync") and kt >= 2:
                            e.wait_ge(s_ve, vXN[kt - 2])
                            e.wait_ge(s_sc, vXC[kt - 2])
                        vX = dma(P, "sync", f"dx{pp}", xbuf[pp][:], xb[kt * 128:(kt + 1) * 128, :])

                        wait(P, "vector", f"dx{pp}", vX)
                        last = None
                        if P.on("vector"):
                            s = sm1[pp]
                            e.reduce_sum(s[:, 0:1], xbuf[pp][:], AX)
                            e.tensor_mul(sq_t[:], xbuf[pp][:], xbuf[pp][:])
                            e.reduce_sum(s[:, 1:2], sq_t[:], AX)
                            e.tensor_scalar(s[:, 2:3], s[:, 0:1], float(-1.0 / DIM), None, ALU.mult)
                            e.tensor_mul(s[:, 4:5], s[:, 2:3], s[:, 2:3])
                            last = e.scalar_tensor_tensor(s[:, 3:4], s[:, 1:2], float(1.0 / DIM),
                                                          s[:, 4:5], ALU.mult, ALU.subtract)
                        vST = producer(P, "vector", last)
                        wait(P, "scalar", "ve", vST)
                        last = None
                        if P.on("scalar"):
                            s = sm1[pp]
                            e.activation(s[:, 7:8], s[:, 3:4], AF.Ln, bias=float(EPS1), scale=1.0)
                            last = e.activation(s[:, 5:6], s[:, 7:8], AF.Exp, bias=0.0, scale=-0.5)
                        vRS = producer(P, "scalar", last)
                        wait(P, "vector", "sc", vRS)
                        last = None
                        if P.on("vector"):
                            last = e.tensor_mul(sm1[pp][:, 6:7], sm1[pp][:, 2:3], sm1[pp][:, 5:6])
                        vNM = producer(P, "vector", last)
                        wait(P, "scalar", "ve", vNM)
                        if P.on("scalar") and kt >= 2:
                            e.wait_ge(s_ve, vXN[kt - 2])
                        last = None
                        if P.on("scalar"):
                            last = e.activation(xc_t[pp][:], xbuf[pp][:], AF.Identity,
                                                bias=sm1[pp][:, 6:7], scale=sm1[pp][:, 5:6])
                        vXC[kt] = producer(P, "scalar", last)
                        wait(P, "vector", "sc", vXC[kt])
                        if P.on("vector") and kt >= 2:
                            e.wait_ge(s_pe, vTR[kt - 2])
                        last = None
                        if P.on("vector"):
                            e.tensor_mul(xg_t[:], xc_t[pp][:], g1bc[:])
                            last = e.tensor_add(xn_bf[pp][:], xg_t[:], b1bc[:])
                        vXN[kt] = producer(P, "vector", last)

                        wait(P, "tensor", "ve", vXN[kt])
                        if P.on("tensor") and kt >= 2:
                            e.wait_ge(s_ve, vXNE[kt - 2])
                        last = None
                        if P.on("tensor"):
                            for kd in range(8):
                                last = e.transpose(ps_tr[pp][:, kd * 128:(kd + 1) * 128],
                                                   xn_bf[pp][:, kd * 128:(kd + 1) * 128], ident[:])
                        vTR[kt] = producer(P, "tensor", last)
                        wait(P, "vector", "pe", vTR[kt])
                        if P.on("vector") and g > 0 and slot == 0:
                            e.wait_ge(s_pe, pjg[g - 1])
                        last = None
                        if P.on("vector"):
                            for kd in range(8):
                                last = e.tensor_copy(xnT[kd][:, slot * 128:(slot + 1) * 128],
                                                     ps_tr[pp][:, kd * 128:(kd + 1) * 128])
                        vXNE[kt] = producer(P, "vector", last)

                        wait(P, "tensor", "ve", vXNE[kt])
                        if P.on("tensor") and kt >= 2:
                            e.wait_ge(s_ve, vVEV[kt - 2])
                        last = None
                        if P.on("tensor"):
                            for kk in range(8):
                                e.matmul(ps_v[pp][:], xnT[kk][:, slot * 128:(slot + 1) * 128],
                                         wv_sb[:, kk * 512:(kk + 1) * 512], start=(kk == 0), stop=False)
                            last = e.matmul(ps_v[pp][:], ones_row[:], bvR[:], start=False, stop=True)
                        vVP = producer(P, "tensor", last)
                        wait(P, "vector", "pe", vVP)
                        last = None
                        if P.on("vector"):
                            last = e.tensor_copy(v_all[:, kt * 512:(kt + 1) * 512], ps_v[pp][:])
                        vVEV[kt] = producer(P, "vector", last)

                        if own:
                            oi = kt
                            if P.on("tensor") and oi >= 1:
                                e.wait_ge(s_sc, vQVS[oi - 1])
                            last = None
                            if P.on("tensor"):
                                for kk in range(8):
                                    last = e.matmul(ps_qv[0:16, 0:128], wcT[:, kk * 16:(kk + 1) * 16],
                                                    xnT[kk][:, slot * 128:(slot + 1) * 128],
                                                    start=(kk == 0), stop=(kk == 7))
                            vQP = producer(P, "tensor", last)
                            wait(P, "scalar", "pe", vQP)
                            last = None
                            if P.on("scalar"):
                                last = e.activation(qvdT[:, oi * 128:(oi + 1) * 128], ps_qv[0:16, 0:128],
                                                    AF.Identity, bias=bqvd[:], scale=RS32)
                            vQVS[oi] = producer(P, "scalar", last)
                            wait(P, "tensor", "sc", vQVS[oi])
                            if P.on("tensor") and oi >= 1:
                                e.wait_ge(s_ve, vQVE3[oi - 1])
                            last = None
                            if P.on("tensor"):
                                last = e.transpose(ps_sm[:, 0:16], qvdT[0:16, oi * 128:(oi + 1) * 128],
                                                   ident_f[:])
                            vQVT[oi] = producer(P, "tensor", last)
                            wait(P, "vector", "pe", vQVT[oi])
                            last = None
                            if P.on("vector"):
                                last = e.tensor_copy(qvd_sb[:, oi * 16:(oi + 1) * 16], ps_sm[:, 0:16])
                            vQVE3[oi] = producer(P, "vector", last)

                        if slot == 3:
                            own_g = g < OWN_NT // 4
                            nchunks = 16 if own_g else 8
                            for ci in range(nchunks):
                                is_q = ci >= 8
                                mc = ci % 8
                                ck = P.bump("ckq")
                                kb = ck % 2
                                if P.on("tensor"):
                                    if ci == 0:
                                        e.wait_ge(s_ve, vXNE[kt])
                                    if ck >= 3:
                                        e.wait_ge(s_sc, vKEV[ck - 2])
                                last = None
                                if P.on("tensor"):
                                    wsrc = wq_sb if is_q else wk_sb
                                    bR = bqR if is_q else bkR
                                    for kk in range(8):
                                        e.matmul(ps_k[kb][:],
                                                 wsrc[:, kk * DIM + mc * 128:kk * DIM + (mc + 1) * 128],
                                                 xnT[kk][:], start=(kk == 0), stop=False)
                                    last = e.matmul(ps_k[kb][:], bR[0:1, mc * 128:(mc + 1) * 128],
                                                    ones_tok[:], start=False, stop=True)
                                vKP = producer(P, "tensor", last)
                                wait(P, "scalar", "pe", vKP)
                                last = None
                                if is_q:
                                    if P.on("scalar"):
                                        last = e.activation(qt_sb[mc][:, g * 512:(g + 1) * 512], ps_k[kb][:],
                                                            AF.Identity, bias=0.0, scale=0.125)
                                    vKEV[ck] = producer(P, "scalar", last)
                                else:
                                    ki = P.bump("ki") - 1
                                    if P.on("scalar") and ki >= 4:
                                        e.wait_ge(SEM[f"dk{ki % 4}"], vKSO[ki - 4])
                                    if P.on("scalar"):
                                        last = e.activation(kst[ki % 4][:], ps_k[kb][:], AF.Identity,
                                                            bias=0.0, scale=1.0)
                                    vKEV[ck] = producer(P, "scalar", last)
                                    wait(P, "sync", "sc", vKEV[ck])
                                    vKSO[ki] = dma(P, "sync", f"dk{ki % 4}",
                                                   kt_hbm[mc * 128:(mc + 1) * 128, g * 512:(g + 1) * 512],
                                                   kst[ki % 4][:])
                            pjg[g] = P.cnt("pe")

                    if P.on("sync"):
                        for _s in ("dx", "dk0", "dk1", "dk2", "dk3"):
                            if P.cnt(_s) > 0:
                                e.wait_ge(SEM[_s], P.cnt(_s))
                        e.wait_ge(s_ve, P.cnt("ve"))
                        e.wait_ge(s_sc, P.cnt("sc"))
                    P.eng.sem_inc(ph, 1)

                if PHASES >= 1:
                    run_block("p1", emit_p1)

                if dbg:
                    def emit_dbg1(P):
                        e = P.eng
                        phase_begin(P)
                        if "kt" in dbg:
                            dma(P, "sync", "dbg", dbg["kt"], kt_hbm)
                        if "v" in dbg:
                            dma(P, "sync", "dbg", dbg["v"], v_all[:])
                        if "qt" in dbg:
                            for i in range(8):
                                dma(P, "sync", "dbg", dbg["qt"][i * 128:(i + 1) * 128, :], qt_sb[i][:])
                        if "qvd" in dbg:
                            dma(P, "sync", "dbg", dbg["qvd"], qvd_sb[:])
                        if P.on("sync"):
                            e.wait_ge(SEM["dbg"], P.cnt("dbg"))
                        P.eng.sem_inc(ph, 1)

                    run_block("dbg1", emit_dbg1)

            of_T = [stOF.enter_context(nc.sbuf_tensor(f"ofT{i}", [128, 2048], BF16, side="right"))
                    for i in range(8)]

            if PHASES >= 2:
                st2 = contextlib.ExitStack()
                with st2:
                    def sb2(name, shape, dt):
                        return st2.enter_context(nc.sbuf_tensor(name, shape, dt))

                    def ps2(name, shape, dt):
                        return st2.enter_context(nc.psum_tensor(name, shape, dt))

                    ktb = [sb2(f"ktb{i}", [128, N], BF16) for i in range(2)]
                    PT = sb2("PT", [128, 2 * N], BF16)
                    on_bf = [sb2(f"onbf{i}", [128, 512], BF16) for i in range(2)]
                    tmp5 = [sb2(f"tmp5{i}", [128, 512], F32) for i in range(2)]
                    of_acc = [sb2(f"ofac{i}", [128, 64], F32) for i in range(2)]
                    of_b2 = [sb2(f"ofb2_{i}", [128, 128], BF16) for i in range(2)]
                    sm2 = [sb2(f"sm2{i}", [128, 32], F32) for i in range(2)]

                    spp = [ps2("spX", [128, 1024], F32), ps2("spY", [128, 1024], F32)]
                    av_ps = [ps2(f"av{i}", [128, 512], F32) for i in range(2)]
                    l_ps = [ps2(f"l{i}", [128, 8], F32) for i in range(2)]

                    NSTEP = 8 * 16
                    def emit_p2(P):
                        e = P.eng
                        phase_begin(P)
                        vKT, vSP, vEX, vAV, vAVg = {}, {}, {}, {}, {}
                        vCP, vCE, vCB, vOFT = {}, {}, {}, {}
                        sp_end = {}

                        def sbar_stage(T, st):
                            c2, qi2 = T // 16, T % 16
                            kb2 = c2 % 2
                            sp = spp[st % 2]
                            if P.on("tensor"):
                                if st == 0 and qi2 == 0:
                                    e.wait_ge(SEM[f"kt{c2 % 2}"], vKT[c2])
                                key = (T, st - 2) if st >= 2 else (T - 1, 14 + st)
                                if key in vEX:
                                    e.wait_ge(s_sc, vEX[key])
                            last = None
                            if P.on("tensor"):
                                for jj in range(4):
                                    jc = st * 4 + jj
                                    e.matmul(sp[:, jj * 128:(jj + 1) * 128],
                                             ktb[kb2][0:64, jc * 128:(jc + 1) * 128],
                                             qt_sb[c2][0:64, qi2 * 128:(qi2 + 1) * 128],
                                             start=True, stop=True, tile_position=(0, 0))
                                    last = e.matmul(sp[:, 512 + jj * 128:512 + (jj + 1) * 128],
                                                    ktb[kb2][64:128, jc * 128:(jc + 1) * 128],
                                                    qt_sb[c2][64:128, qi2 * 128:(qi2 + 1) * 128],
                                                    start=True, stop=True, tile_position=(64, 0))
                            vSP[(T, st)] = producer(P, "tensor", last)
                            if st == 15 and qi2 == 15:
                                sp_end[c2] = P.cnt("pe")
                            for hh in range(2):
                                if P.on("scalar"):
                                    if (T - 1, st) in vAVg:
                                        e.wait_ge(s_pe, vAVg[(T - 1, st)])
                                    e.wait_ge(s_pe, vSP[(T, st)])
                                last = None
                                if P.on("scalar"):
                                    last = e.activation(
                                        PT[:, hh * N + st * 512:hh * N + (st + 1) * 512],
                                        sp[:, hh * 512:(hh + 1) * 512], AF.Exp,
                                        bias=0.0, scale=1.0)
                                vEX[(T, st)] = producer(P, "scalar", last)

                        def act_comp(t):
                            for hh in range(2):
                                wait(P, "scalar", "ve", vCP[(t, hh)])
                                last = None
                                if P.on("scalar"):
                                    last = e.activation(sm2[hh][:, 16:24], sm2[hh][:, 8:16], AF.Exp,
                                                        bias=sm2[hh][:, 1:2], scale=1.0,
                                                        accum_out=sm2[hh][:, 2:3])
                                vCE[(t, hh)] = producer(P, "scalar", last)

                        def dve_tail(t):
                            ob = of_b2[t % 2]
                            for hh in range(2):
                                wait(P, "vector", "sc", vCE[(t, hh)])
                                if P.on("vector") and hh == 0 and t >= 2:
                                    e.wait_ge(SEM[f"oft{t % 2}"], vOFT[t - 2])
                                last = None
                                if P.on("vector"):
                                    s = sm2[hh]
                                    e.reciprocal(s[:, 3:4], s[:, 2:3])
                                    e.tensor_scalar(s[:, 24:32], s[:, 16:24], s[:, 3:4], None, ALU.mult)
                                    e.tensor_scalar(of_acc[hh][:], on_bf[hh][:, 0:64], s[:, 24:25],
                                                    None, ALU.mult)
                                    for r_ in range(1, 7):
                                        e.scalar_tensor_tensor(of_acc[hh][:],
                                                               on_bf[hh][:, r_ * 64:(r_ + 1) * 64],
                                                               s[:, 24 + r_:25 + r_], of_acc[hh][:],
                                                               ALU.mult, ALU.add)
                                    last = e.scalar_tensor_tensor(ob[:, hh * 64:(hh + 1) * 64],
                                                                  on_bf[hh][:, 448:512],
                                                                  s[:, 31:32], of_acc[hh][:],
                                                                  ALU.mult, ALU.add)
                                vCB[(t, hh)] = producer(P, "vector", last)
                            c2, qi2 = t // 16, t % 16
                            if P.on("sync"):
                                e.wait_ge(s_ve, vCB[(t, 1)])
                            v = P.bump(f"oft{t % 2}", 16)
                            if P.on("sync"):
                                e.dma_start(of_T[c2][:, qi2 * 128:(qi2 + 1) * 128], ob[:],
                                            transpose=True).then_inc(SEM[f"oft{t % 2}"], 16)
                            vOFT[t] = v

                        vKT[0] = dma(P, "sync", "kt0", ktb[0][:], kt_hbm[0:128, :])
                        for st in range(16):
                            sbar_stage(0, st)

                        for t in range(NSTEP):
                            c, qi = t // 16, t % 16
                            if qi == 0 and c < 7:
                                if P.on("sync") and c >= 1:
                                    e.wait_ge(s_pe, sp_end[c - 1])
                                vKT[c + 1] = dma(P, "sync", f"kt{(c + 1) % 2}", ktb[(c + 1) % 2][:],
                                                 kt_hbm[(c + 1) * 128:(c + 2) * 128, :])

                            if t >= 1:
                                act_comp(t - 1)
                                dve_tail(t - 1)

                            if P.on("tensor") and t >= 1:
                                e.wait_ge(s_ve, vCP[(t - 1, 0)])
                                e.wait_ge(s_ve, vCP[(t - 1, 1)])
                            for gg in range(16):
                                lasts = [None, None]
                                for hh in range(2):
                                    if P.on("tensor"):
                                        e.wait_ge(s_sc, vEX[(t, gg)])
                                        for jj in range(4):
                                            jc = gg * 4 + jj
                                            lhs = PT[:, hh * N + jc * 128:hh * N + (jc + 1) * 128]
                                            e.matmul(av_ps[hh][:], lhs, v_all[:, jc * 512:(jc + 1) * 512],
                                                     start=(jc == 0), stop=(jc == 63))
                                            lasts[hh] = e.matmul(l_ps[hh][:, 0:1], lhs, ones_col[:],
                                                                 start=(jc == 0), stop=(jc == 63))
                                vAVg[(t, gg)] = producer(P, "tensor", lasts[1])
                                if gg == 15:
                                    vAV[(t, 0)] = vAV[(t, 1)] = vAVg[(t, gg)]
                                if t + 1 < NSTEP:
                                    sbar_stage(t + 1, gg)

                            for hh in range(2):
                                h = 2 * c + hh
                                wait(P, "vector", "pe", vAV[(t, hh)])
                                if P.on("vector") and t >= 1:
                                    e.wait_ge(s_sc, vCE[(t - 1, hh)])
                                last = None
                                if P.on("vector"):
                                    s = sm2[hh]
                                    e.reciprocal(s[:, 0:1], l_ps[hh][:, 0:1])
                                    e.tensor_scalar(on_bf[hh][:], av_ps[hh][:], s[:, 0:1], None, ALU.mult)
                                    e.tensor_mul(tmp5[hh][:], on_bf[hh][:], wsv_bc[:])
                                    for r_ in range(8):
                                        e.reduce_sum(s[:, 8 + r_:9 + r_],
                                                     tmp5[hh][:, r_ * 64:(r_ + 1) * 64], AX)
                                    e.tensor_scalar(s[:, 8:16], s[:, 8:16],
                                                    qvd_sb[:, qi * 16 + h:qi * 16 + h + 1],
                                                    bs128[:, 0:1], ALU.add, ALU.add)
                                    last = e.reduce_max(s[:, 1:2], s[:, 8:16], AX, negate=True)
                                vCP[(t, hh)] = producer(P, "vector", last)

                        act_comp(NSTEP - 1)
                        dve_tail(NSTEP - 1)

                        if P.on("sync"):
                            e.wait_ge(s_ve, P.cnt("ve"))
                            e.wait_ge(SEM["oft0"], P.cnt("oft0"))
                            e.wait_ge(SEM["oft1"], P.cnt("oft1"))
                        P.eng.sem_inc(ph, 1)
                    run_block("p2", emit_p2)

                    if dbg and "oft" in dbg:
                        def emit_dbg2(P):
                            e = P.eng
                            phase_begin(P)
                            for i in range(8):
                                dma(P, "sync", "dbg", dbg["oft"][i * 128:(i + 1) * 128, :], of_T[i][:])
                            if P.on("sync"):
                                e.wait_ge(SEM["dbg"], P.cnt("dbg"))
                            P.eng.sem_inc(ph, 1)

                        run_block("dbg2", emit_dbg2)
        if PHASES >= 3:
            st3 = contextlib.ExitStack()
            with st3:
                def sb3(name, shape, dt):
                    return st3.enter_context(nc.sbuf_tensor(name, shape, dt))

                st3a = contextlib.ExitStack()
                with st3a:
                    def sb3a(name, shape, dt):
                        return st3a.enter_context(nc.sbuf_tensor(name, shape, dt))

                    def ps3a(name, shape, dt):
                        return st3a.enter_context(nc.psum_tensor(name, shape, dt))

                    wf_sb = sb3a("wf_sb", [128, 8 * DIM], BF16)
                    wr1_sb = sb3a("wr1_sb", [128, 8 * 2048], BF16)
                    wr2_sb = sb3a("wr2_sb", [128, 16 * DIM], BF16)
                    g2bc = sb3a("g2bc", [128, DIM], F32)
                    b2bc = sb3a("b2bc", [128, DIM], F32)
                    bfR = sb3a("bfR", [1, DIM], BF16)
                    br1R = sb3a("br1R", [1, 2 * DIM], BF16)
                    br2R = sb3a("br2R", [1, DIM], BF16)
                    x3buf = [sb3a(f"x3b{i}", [128, DIM], F32) for i in range(2)]
                    row3 = x3buf[0][0:1, :]
                    out2b = [sb3a(f"out2b{i}", [128, DIM], F32) for i in range(2)]
                    xpo2b = [sb3a(f"xpo2b{i}", [128, DIM], F32) for i in range(4)]
                    on2bf = [sb3a(f"on2bf{i}", [128, DIM], BF16) for i in range(2)]
                    onT = [sb3a(f"onT{k}", [128, 512], BF16) for k in range(8)]
                    h1T = [sb3a(f"h1T{k}", [128, 512], BF16) for k in range(16)]
                    x1b = [sb3a(f"x1b{i}", [128, DIM], F32) for i in range(2)]
                    x1t = out2b
                    sq3 = sb3a("sq3", [128, DIM], F32)
                    sm3 = [sb3a(f"sm3{i}", [128, 16], F32) for i in range(2)]

                    ps_m = [ps3a(f"ps_m{i}", [128, 1024], F32) for i in range(2)]
                    ps_f = [ps3a(f"ps_f{i}", [128, 512], F32) for i in range(2)]
                    ps_t3 = [ps3a(f"ps_t3{i}", [128, 1024], BF16) for i in range(2)]

                    def emit_p3a1(P):
                        e = P.eng
                        phase_begin(P)
                        dmas = []
                        for kk in range(8):
                            dmas.append((wf_sb[:, kk * DIM:(kk + 1) * DIM], wfT[kk * 128:(kk + 1) * 128, :]))
                            dmas.append((wr1_sb[:, kk * 2048:(kk + 1) * 2048], wr1T[kk * 128:(kk + 1) * 128, :]))
                        for kk in range(16):
                            dmas.append((wr2_sb[:, kk * DIM:(kk + 1) * DIM], wr2T[kk * 128:(kk + 1) * 128, :]))
                        dmas += [(bfR[:], bf_d[None, :]), (br1R[:], br1_d[None, :]), (br2R[:], br2_d[None, :])]
                        vW3 = 0
                        for dst, src in dmas:
                            vW3 = dma(P, "gpsimd", "dw", dst, src)
                        vBCe = 0
                        for dsrc, bc in ((g2_d, g2bc), (b2_d, b2bc)):
                            vRL = dma(P, "gpsimd", "dc", row3, dsrc[None, :])
                            for half in range(2):
                                wait(P, "tensor", "dc", vRL)
                                wait(P, "tensor", "ve", vBCe)
                                last = None
                                if P.on("tensor"):
                                    last = e.matmul(ps_m[0][:, 0:512], ones_f[:],
                                                    x3buf[0][0:1, half * 512:(half + 1) * 512],
                                                    start=True, stop=True)
                                vBC = producer(P, "tensor", last)
                                wait(P, "vector", "pe", vBC)
                                last = None
                                if P.on("vector"):
                                    last = e.tensor_copy(bc[:, half * 512:(half + 1) * 512], ps_m[0][:, 0:512])
                                vBCe = producer(P, "vector", last)
                            if P.on("gpsimd"):
                                e.wait_ge(s_pe, P.cnt("pe"))

                        vO2, vLN2F, vTRA, vTEV = {}, {}, {}, {}
                        vX1, vX1S = {}, {}
                        pmEv = {0: vBCe}
                        pmU = [0]
                        pfEv = {}
                        pfU = [0]
                        ptRel = {}
                        ptU = [0]
                        f1g = {}
                        ffn2g = {}
                        h1ready = {}

                        for g3 in range(4):
                            for tl in range(4):
                                qi = g3 * 4 + tl
                                pp = qi % 2

                                if P.on("sync"):
                                    if qi >= 2:
                                        e.wait_ge(s_ve, vO2[qi - 2])
                                    else:
                                        e.wait_ge(s_ve, vBCe)
                                vX3 = dma(P, "sync", f"dx3{pp}", x3buf[pp][:], xb[qi * 128:(qi + 1) * 128, :])

                                pmU[0] += 1
                                u = pmU[0]
                                wait(P, "tensor", "dw", vW3)
                                if P.on("tensor"):
                                    e.wait_ge(s_ve, pmEv.get(u - 2, vBCe))
                                last = None
                                if P.on("tensor"):
                                    for half in range(2):
                                        for kk in range(8):
                                            e.matmul(ps_m[(u - 1) % 2][:, half * 512:(half + 1) * 512],
                                                     of_T[kk][:, qi * 128:(qi + 1) * 128],
                                                     wf_sb[:, kk * DIM + half * 512:kk * DIM + (half + 1) * 512],
                                                     start=(kk == 0), stop=False)
                                        last = e.matmul(ps_m[(u - 1) % 2][:, half * 512:(half + 1) * 512],
                                                        ones_row[:], bfR[0:1, half * 512:(half + 1) * 512],
                                                        start=False, stop=True)
                                vWF = producer(P, "tensor", last)
                                wait(P, "vector", "pe", vWF)
                                wait(P, "vector", f"dx3{pp}", vX3)
                                last = None
                                if P.on("vector"):
                                    e.tensor_add(out2b[pp][:], ps_m[(u - 1) % 2][:], x3buf[pp][:])
                                    last = e.tensor_add(xpo2b[tl][:], out2b[pp][:], x3buf[pp][:])
                                vO2[qi] = producer(P, "vector", last)
                                pmEv[u] = vO2[qi]

                                last = None
                                if P.on("vector"):
                                    s = sm3[pp]
                                    e.reduce_sum(s[:, 0:1], out2b[pp][:], AX)
                                    e.tensor_mul(sq3[:], out2b[pp][:], out2b[pp][:])
                                    e.reduce_sum(s[:, 1:2], sq3[:], AX)
                                    e.tensor_scalar(s[:, 2:3], s[:, 0:1], float(-1.0 / DIM), None, ALU.mult)
                                    e.tensor_mul(s[:, 4:5], s[:, 2:3], s[:, 2:3])
                                    last = e.scalar_tensor_tensor(s[:, 3:4], s[:, 1:2], float(1.0 / DIM),
                                                                  s[:, 4:5], ALU.mult, ALU.subtract)
                                vst = producer(P, "vector", last)
                                wait(P, "scalar", "ve", vst)
                                last = None
                                if P.on("scalar"):
                                    s = sm3[pp]
                                    e.activation(s[:, 7:8], s[:, 3:4], AF.Ln, bias=float(EPS1), scale=1.0)
                                    last = e.activation(s[:, 5:6], s[:, 7:8], AF.Exp, bias=0.0, scale=-0.5)
                                vrs = producer(P, "scalar", last)
                                wait(P, "vector", "sc", vrs)
                                if P.on("vector") and qi >= 2:
                                    e.wait_ge(s_pe, vTRA[qi - 2])
                                last = None
                                if P.on("vector"):
                                    e.tensor_scalar(sq3[:], out2b[pp][:], sm3[pp][:, 2:3],
                                                    sm3[pp][:, 5:6], ALU.add, ALU.mult)
                                    e.tensor_mul(sq3[:], sq3[:], g2bc[:])
                                    last = e.tensor_add(on2bf[pp][:], sq3[:], b2bc[:])
                                vLN2F[qi] = producer(P, "vector", last)

                                ptU[0] += 1
                                up = ptU[0]
                                wait(P, "tensor", "ve", vLN2F[qi])
                                if P.on("tensor") and up >= 3:
                                    e.wait_ge(s_ve, ptRel[up - 2])
                                last = None
                                if P.on("tensor"):
                                    for kd in range(8):
                                        last = e.transpose(ps_t3[(up - 1) % 2][:, kd * 128:(kd + 1) * 128],
                                                           on2bf[pp][:, kd * 128:(kd + 1) * 128], ident[:])
                                vTRA[qi] = producer(P, "tensor", last)
                                wait(P, "vector", "pe", vTRA[qi])
                                if P.on("vector") and g3 > 0:
                                    e.wait_ge(s_pe, f1g[g3 - 1])
                                last = None
                                if P.on("vector"):
                                    for kd in range(8):
                                        last = e.tensor_copy(onT[kd][:, tl * 128:(tl + 1) * 128],
                                                             ps_t3[(up - 1) % 2][:, kd * 128:(kd + 1) * 128])
                                vTEV[qi] = producer(P, "vector", last)
                                ptRel[up] = vTEV[qi]

                            for mc in range(16):
                                pfU[0] += 1
                                uf = pfU[0]
                                if P.on("tensor"):
                                    if mc == 0:
                                        e.wait_ge(s_ve, vTEV[g3 * 4 + 3])
                                    if uf >= 3:
                                        e.wait_ge(s_ve, pfEv[uf - 2])
                                last = None
                                if P.on("tensor"):
                                    for kk in range(8):
                                        e.matmul(ps_f[(uf - 1) % 2][:],
                                                 wr1_sb[:, kk * 2048 + mc * 128:kk * 2048 + (mc + 1) * 128],
                                                 onT[kk][:], start=(kk == 0), stop=False)
                                    last = e.matmul(ps_f[(uf - 1) % 2][:], br1R[0:1, mc * 128:(mc + 1) * 128],
                                                    ones_tok[:], start=False, stop=True)
                                vF1 = producer(P, "tensor", last)
                                wait(P, "vector", "pe", vF1)
                                if P.on("vector") and g3 > 0:
                                    e.wait_ge(s_pe, ffn2g[g3 - 1])
                                last = None
                                if P.on("vector"):
                                    last = e.tensor_scalar(h1T[mc][:], ps_f[(uf - 1) % 2][:],
                                                           0.0, None, ALU.max)
                                vF1E = producer(P, "vector", last)
                                pfEv[uf] = vF1E
                            h1ready[g3] = pfEv[pfU[0]]
                            f1g[g3] = P.cnt("pe")

                            for tl in range(4):
                                qi = g3 * 4 + tl
                                pp = qi % 2
                                pmU[0] += 1
                                u = pmU[0]
                                if P.on("tensor"):
                                    e.wait_ge(s_ve, h1ready[g3])
                                    if u >= 3:
                                        e.wait_ge(s_ve, pmEv[u - 2])
                                last = None
                                if P.on("tensor"):
                                    for half in range(2):
                                        for mc in range(16):
                                            e.matmul(ps_m[(u - 1) % 2][:, half * 512:(half + 1) * 512],
                                                     h1T[mc][:, tl * 128:(tl + 1) * 128],
                                                     wr2_sb[:, mc * DIM + half * 512:mc * DIM + (half + 1) * 512],
                                                     start=(mc == 0), stop=False)
                                        last = e.matmul(ps_m[(u - 1) % 2][:, half * 512:(half + 1) * 512],
                                                        ones_row[:], br2R[0:1, half * 512:(half + 1) * 512],
                                                        start=False, stop=True)
                                vFF2 = producer(P, "tensor", last)
                                if tl == 3:
                                    ffn2g[g3] = P.cnt("pe")

                                wait(P, "vector", "pe", vFF2)
                                last = None
                                if P.on("vector"):
                                    s = sm3[pp]
                                    e.tensor_add(x1t[pp][:], ps_m[(u - 1) % 2][:], xpo2b[tl][:])
                                    e.reduce_sum(s[:, 8:9], x1t[pp][:], AX)
                                    e.tensor_mul(sq3[:], x1t[pp][:], x1t[pp][:])
                                    e.reduce_sum(s[:, 9:10], sq3[:], AX)
                                    e.tensor_scalar(s[:, 10:11], s[:, 8:9], float(-1.0 / DIM), None, ALU.mult)
                                    e.tensor_mul(s[:, 12:13], s[:, 10:11], s[:, 10:11])
                                    last = e.scalar_tensor_tensor(s[:, 11:12], s[:, 9:10], float(1.0 / DIM),
                                                                  s[:, 12:13], ALU.mult, ALU.subtract)
                                vst2 = producer(P, "vector", last)
                                pmEv[u] = vst2
                                wait(P, "scalar", "ve", vst2)
                                last = None
                                if P.on("scalar"):
                                    s = sm3[pp]
                                    e.activation(s[:, 15:16], s[:, 11:12], AF.Ln, bias=float(EPS1), scale=1.0)
                                    last = e.activation(s[:, 13:14], s[:, 15:16], AF.Exp, bias=0.0, scale=-0.5)
                                vrs2 = producer(P, "scalar", last)
                                wait(P, "vector", "sc", vrs2)
                                if P.on("vector") and qi >= 2:
                                    e.wait_ge(SEM[f"x1s{qi % 2}"], vX1S[qi - 2])
                                last = None
                                if P.on("vector"):
                                    last = e.tensor_scalar(x1b[pp][:], x1t[pp][:], sm3[pp][:, 10:11],
                                                           sm3[pp][:, 13:14], ALU.add, ALU.mult)
                                vX1[qi] = producer(P, "vector", last)
                                wait(P, "sync", "ve", vX1[qi])
                                vX1S[qi] = dma(P, "sync", f"x1s{qi % 2}", x1_hbm[qi * 128:(qi + 1) * 128, :], x1b[pp][:])

                        if P.on("sync"):
                            for _s in ("dx30", "dx31", "x1s0", "x1s1"):
                                if P.cnt(_s) > 0:
                                    e.wait_ge(SEM[_s], P.cnt(_s))
                            e.wait_ge(s_ve, P.cnt("ve"))
                        P.eng.sem_inc(ph, 1)

                    run_block("p3a1", emit_p3a1)
                stOF.close()

                nrmT = [sb3(f"nrmT{k}", [128, 2048], BF16) for k in range(8)]
                cmb_sb = sb3("cmb_sb", [128, OWN_NT * 8], F32)

                st3r = contextlib.ExitStack()
                with st3r:
                    def sb3r(name, shape, dt):
                        return st3r.enter_context(nc.sbuf_tensor(name, shape, dt))

                    rt_sb = sb3r("rt_sb", [128, 64], BF16)
                    nwbc = sb3r("nwbc", [128, DIM], F32)
                    rowr = sb3r("rowr", [1, DIM], F32)
                    x1r = [sb3r(f"x1r{i}", [128, DIM], F32) for i in range(2)]
                    nbf = [sb3r(f"nbf{i}", [128, DIM], BF16) for i in range(2)]
                    sqr = sb3r("sqr", [128, DIM], F32)
                    rsm = [sb3r(f"rsm{i}", [128, 40], F32) for i in range(2)]

                    ps_rt = [st3r.enter_context(nc.psum_tensor(f"ps_rt{i}", [128, 512], F32)) for i in range(2)]
                    ps_tr2 = [st3r.enter_context(nc.psum_tensor(f"ps_tr2{i}", [128, 1024], BF16)) for i in range(2)]

                    def emit_p3a2(P):
                        e = P.eng
                        phase_begin(P)
                        vRL = 0
                        for kk in range(8):
                            vRL = dma(P, "gpsimd", "dw", rt_sb[:, kk * 8:(kk + 1) * 8],
                                      routerT[kk * 128:(kk + 1) * 128, :])
                        vRL2 = dma(P, "gpsimd", "dw", rowr[:], nw_d[None, :])
                        vBCe = 0
                        for half in range(2):
                            wait(P, "tensor", "dw", vRL2)
                            wait(P, "tensor", "ve", vBCe)
                            last = None
                            if P.on("tensor"):
                                last = e.matmul(ps_rt[0][:], ones_f[:], rowr[0:1, half * 512:(half + 1) * 512],
                                                start=True, stop=True)
                            vBC = producer(P, "tensor", last)
                            wait(P, "vector", "pe", vBC)
                            last = None
                            if P.on("vector"):
                                last = e.tensor_copy(nwbc[:, half * 512:(half + 1) * 512], ps_rt[0][:])
                            vBCe = producer(P, "vector", last)

                        vXR, vNBF, vRTE, vNTE = {}, {}, {}, {}
                        ptRel2 = {}
                        pfRel2 = {}
                        for qi in range(OWN_NT):
                            pp = qi % 2
                            if P.on("sync") and qi >= 2:
                                e.wait_ge(s_ve, vNBF[qi - 2])
                            vXR[qi] = dma(P, "sync", f"x1l{pp}", x1r[pp][:], x1_hbm[qi * 128:(qi + 1) * 128, :])

                            wait(P, "vector", f"x1l{pp}", vXR[qi])
                            last = None
                            if P.on("vector"):
                                e.tensor_mul(sqr[:], x1r[pp][:], x1r[pp][:])
                                last = e.reduce_sum(rsm[pp][:, 7:8], sqr[:], AX)
                            vr1 = producer(P, "vector", last)
                            wait(P, "scalar", "ve", vr1)
                            last = None
                            if P.on("scalar"):
                                e.activation(rsm[pp][:, 6:7], rsm[pp][:, 7:8], AF.Ln,
                                             bias=float(EPS2 * DIM), scale=1.0)
                                last = e.activation(rsm[pp][:, 5:6], rsm[pp][:, 6:7], AF.Exp,
                                                    bias=float(0.5 * np.log(DIM)), scale=-0.5)
                            vr2 = producer(P, "scalar", last)
                            wait(P, "vector", "sc", vr2)
                            if P.on("vector") and qi >= 2:
                                e.wait_ge(s_pe, vRTE[qi - 2])
                            last = None
                            if P.on("vector"):
                                last = e.scalar_tensor_tensor(nbf[pp][:], x1r[pp][:], rsm[pp][:, 5:6],
                                                              nwbc[:], ALU.mult, ALU.mult)
                            vNBF[qi] = producer(P, "vector", last)

                            wait(P, "tensor", "ve", vNBF[qi])
                            if P.on("tensor") and qi >= 2:
                                e.wait_ge(s_ve, ptRel2[qi - 2])
                            last = None
                            if P.on("tensor"):
                                for kd in range(8):
                                    last = e.transpose(ps_tr2[pp][:, kd * 128:(kd + 1) * 128],
                                                       nbf[pp][:, kd * 128:(kd + 1) * 128], ident[:])
                            vRTE[qi] = producer(P, "tensor", last)
                            wait(P, "vector", "pe", vRTE[qi])
                            last = None
                            if P.on("vector"):
                                for kd in range(8):
                                    last = e.tensor_copy(nrmT[kd][:, qi * 128:(qi + 1) * 128],
                                                         ps_tr2[pp][:, kd * 128:(kd + 1) * 128])
                            vNTE[qi] = producer(P, "vector", last)
                            ptRel2[qi] = vNTE[qi]

                            wait(P, "tensor", "ve", vNTE[qi])
                            if P.on("tensor") and qi >= 2:
                                e.wait_ge(s_sc, pfRel2[qi - 2])
                            last = None
                            if P.on("tensor"):
                                for kk in range(8):
                                    last = e.matmul(ps_rt[pp][:, 0:8],
                                                    nrmT[kk][:, qi * 128:(qi + 1) * 128],
                                                    rt_sb[:, kk * 8:(kk + 1) * 8],
                                                    start=(kk == 0), stop=(kk == 7))
                            vRT = producer(P, "tensor", last)
                            wait(P, "vector", "pe", vRT)
                            last = None
                            if P.on("vector"):
                                last = e.reduce_max(rsm[pp][:, 0:1], ps_rt[pp][:, 0:8], AX, negate=True)
                            vrt1 = producer(P, "vector", last)
                            wait(P, "scalar", "ve", vrt1)
                            last = None
                            if P.on("scalar"):
                                last = e.activation(rsm[pp][:, 8:16], ps_rt[pp][:, 0:8],
                                                    AF.Exp, bias=rsm[pp][:, 0:1],
                                                    accum_out=rsm[pp][:, 1:2])
                            vrt2 = producer(P, "scalar", last)
                            pfRel2[qi] = vrt2
                            wait(P, "vector", "sc", vrt2)
                            last = None
                            if P.on("vector"):
                                s = rsm[pp]
                                e.reciprocal(s[:, 2:3], s[:, 1:2])
                                e.tensor_scalar(s[:, 16:24], s[:, 8:16], s[:, 2:3], None, ALU.mult)
                                e.reduce_max(s[:, 3:4], s[:, 16:24], AX)
                                e.tensor_scalar(s[:, 24:32], s[:, 16:24], s[:, 3:4], None, ALU.is_equal)
                                e.scalar_tensor_tensor(s[:, 8:16], s[:, 24:32], -1e9, s[:, 16:24],
                                                       ALU.mult, ALU.add)
                                e.reduce_max(s[:, 4:5], s[:, 8:16], AX)
                                e.tensor_scalar(s[:, 32:40], s[:, 8:16], s[:, 4:5], None, ALU.is_equal)
                                e.tensor_add(s[:, 24:32], s[:, 24:32], s[:, 32:40])
                                e.tensor_add(s[:, 5:6], s[:, 3:4], s[:, 4:5])
                                e.reciprocal(s[:, 6:7], s[:, 5:6])
                                e.tensor_mul(s[:, 32:40], s[:, 16:24], s[:, 24:32])
                                last = e.tensor_scalar(cmb_sb[:, qi * 8:(qi + 1) * 8], s[:, 32:40],
                                                       s[:, 6:7], None, ALU.mult)
                            producer(P, "vector", last)

                        if P.on("sync"):
                            e.wait_ge(s_ve, P.cnt("ve"))
                        P.eng.sem_inc(ph, 1)

                    run_block("p3a2", emit_p3a2)

                    if dbg and ("x1" in dbg or "nrm" in dbg or "cmb" in dbg):
                        def emit_dbg3(P):
                            e = P.eng
                            phase_begin(P)
                            if "x1" in dbg:
                                dma(P, "sync", "dbg", dbg["x1"], x1_hbm)
                            if "nrm" in dbg:
                                for i in range(8):
                                    dma(P, "sync", "dbg", dbg["nrm"][i * 128:(i + 1) * 128, :], nrmT[i][:])
                            if "cmb" in dbg:
                                dma(P, "sync", "dbg", dbg["cmb"], cmb_sb[:])
                            if P.on("sync"):
                                e.wait_ge(SEM["dbg"], P.cnt("dbg"))
                            P.eng.sem_inc(ph, 1)

                        run_block("dbg3", emit_dbg3)

                if PHASES >= 4:
                    st3b = contextlib.ExitStack()
                    with st3b:
                        def sb3b(name, shape, dt):
                            return st3b.enter_context(nc.sbuf_tensor(name, shape, dt))

                        def ps3b(name, shape, dt):
                            return st3b.enter_context(nc.psum_tensor(name, shape, dt))

                        hid_sb = sb3b("hid_sb", [128, 22 * DIM], BF16)
                        dn_sb = sb3b("dn_sb", [128, 22 * DIM], BF16)
                        gu_g = [sb3b(f"gug{i}", [128, 2048], BF16) for i in range(2)]
                        gu_u = [sb3b(f"guu{i}", [128, 2048], BF16) for i in range(2)]
                        moe = sb3b("moe", [128, 8 * DIM], F32)
                        sil = [sb3b(f"sil{i}", [128, 512], F32) for i in range(2)]
                        x1l = [sb3b(f"x1l{i}", [128, DIM], F32) for i in range(2)]
                        ysc = [sb3b(f"ysc{i}", [128, DIM], F32) for i in range(2)]
                        ybuf = x1l
                        sq4 = hid_sb[:, 0:DIM]
                        sm4 = [sb3b(f"sm4{i}", [128, 8], F32) for i in range(2)]

                        ps_g = [ps3b(f"ps_g{i}", [128, 512], F32) for i in range(2)]
                        ps_u = [ps3b(f"ps_u{i}", [128, 512], F32) for i in range(2)]
                        ps_dn = [ps3b(f"ps_dn{i}", [128, 1024], F32) for i in range(2)]

                        def emit_p3b(P):
                            e = P.eng
                            phase_begin(P)
                            vGL, vUL, vDL = {}, {}, {}
                            vSIL, vHID, vDNE = {}, {}, {}
                            slabPE = {}
                            slabI = [0]
                            vDN_pe = {}
                            guU = [0]
                            dnU = [0]
                            vYST = {}
                            vFINv = {}
                            vYBv = {}

                            for hf in range(2):
                                for ee in range(NE):
                                    for hs in range(11):
                                        si = slabI[0]
                                        slabI[0] += 1
                                        gb = si % 2
                                        if P.on("gpsimd"):
                                            prev = slabPE.get(si - 2, 0)
                                            if prev > 0:
                                                e.wait_ge(s_pe, prev)
                                        for kk in range(8):
                                            vGL[(hf, ee, hs)] = dma(
                                                P, "gpsimd", f"dgu{gb}", gu_g[gb][:, kk * 256:(kk + 1) * 256],
                                                guT[ee][kk * 128:(kk + 1) * 128, hs * 256:(hs + 1) * 256])
                                        for kk in range(8):
                                            vUL[(hf, ee, hs)] = dma(
                                                P, "gpsimd", f"dgu{gb}", gu_u[gb][:, kk * 256:(kk + 1) * 256],
                                                guT[ee][kk * 128:(kk + 1) * 128,
                                                        HID + hs * 256:HID + (hs + 1) * 256])
                                        for mk in (2 * hs, 2 * hs + 1):
                                            if mk < 22:
                                                if P.on("gpsimd") and mk == 0 and (hf + ee) > 0:
                                                    e.wait_ge(s_pe, vDN_pe.get((hf, ee - 1),
                                                                               vDN_pe.get((hf - 1, NE - 1), 0)))
                                                vDL[(hf, ee)] = dma(P, "gpsimd", "dn",
                                                                    dn_sb[:, mk * DIM:(mk + 1) * DIM],
                                                                    dnT[ee][mk * 128:(mk + 1) * 128, :])
                                        for mc2 in range(2):
                                            for tc in range(2):
                                                guU[0] += 1
                                                ug = guU[0]
                                                wait(P, "tensor", f"dgu{gb}", vUL[(hf, ee, hs)])
                                                if P.on("tensor") and ug >= 3:
                                                    e.wait_ge(s_ve, vHID.get(ug - 2, 0))
                                                last = None
                                                if P.on("tensor"):
                                                    for kk in range(8):
                                                        e.matmul(ps_g[(ug - 1) % 2][:],
                                                                 gu_g[gb][:, kk * 256 + mc2 * 128:kk * 256 + (mc2 + 1) * 128],
                                                                 nrmT[kk][:, hf * 1024 + tc * 512:hf * 1024 + (tc + 1) * 512],
                                                                 start=(kk == 0), stop=(kk == 7))
                                                    for kk in range(8):
                                                        last = e.matmul(ps_u[(ug - 1) % 2][:],
                                                                        gu_u[gb][:, kk * 256 + mc2 * 128:kk * 256 + (mc2 + 1) * 128],
                                                                        nrmT[kk][:, hf * 1024 + tc * 512:hf * 1024 + (tc + 1) * 512],
                                                                        start=(kk == 0), stop=(kk == 7))
                                                vGUP = producer(P, "tensor", last)
                                                slabPE[si] = vGUP
                                                wait(P, "scalar", "pe", vGUP)
                                                if P.on("scalar") and ug >= 3:
                                                    e.wait_ge(s_ve, vHID.get(ug - 2, 0))
                                                last = None
                                                if P.on("scalar"):
                                                    last = e.activation(sil[(ug - 1) % 2][:],
                                                                        ps_g[(ug - 1) % 2][:], AF.Silu,
                                                                        bias=0.0, scale=1.0)
                                                vSIL[ug] = producer(P, "scalar", last)
                                                wait(P, "vector", "sc", vSIL[ug])
                                                if P.on("vector") and hs == 0 and mc2 == 0 and tc == 0:
                                                    pv = vDN_pe.get((hf, ee - 1),
                                                                    vDN_pe.get((hf - 1, NE - 1), 0))
                                                    if pv > 0:
                                                        e.wait_ge(s_pe, pv)
                                                mkc = 2 * hs + mc2
                                                last = None
                                                if P.on("vector"):
                                                    last = e.tensor_mul(
                                                        hid_sb[:, mkc * DIM + tc * 512:mkc * DIM + (tc + 1) * 512],
                                                        sil[(ug - 1) % 2][:], ps_u[(ug - 1) % 2][:])
                                                vHID[ug] = producer(P, "vector", last)
                                    hid_done = vHID[guU[0]]

                                    for tl in range(8):
                                        qi = hf * 8 + tl
                                        dnU[0] += 1
                                        ud = dnU[0]
                                        wait(P, "tensor", "ve", hid_done)
                                        wait(P, "tensor", "dn", vDL[(hf, ee)])
                                        if P.on("tensor") and ud >= 3:
                                            e.wait_ge(s_ve, vDNE.get(ud - 2, 0))
                                        last = None
                                        if P.on("tensor"):
                                            for half in range(2):
                                                for mk in range(22):
                                                    last = e.matmul(
                                                        ps_dn[(ud - 1) % 2][:, half * 512:(half + 1) * 512],
                                                        hid_sb[:, mk * DIM + tl * 128:mk * DIM + (tl + 1) * 128],
                                                        dn_sb[:, mk * DIM + half * 512:mk * DIM + (half + 1) * 512],
                                                        start=(mk == 0), stop=(mk == 21))
                                        vDN = producer(P, "tensor", last)
                                        if tl == 7:
                                            vDN_pe[(hf, ee)] = P.cnt("pe")
                                        wait(P, "vector", "pe", vDN)
                                        last = None
                                        if P.on("vector"):
                                            cc = cmb_sb[:, qi * 8 + ee:qi * 8 + ee + 1]
                                            if ee == 0:
                                                last = e.tensor_scalar(moe[:, tl * DIM:(tl + 1) * DIM],
                                                                       ps_dn[(ud - 1) % 2][:], cc, None, ALU.mult)
                                            else:
                                                last = e.scalar_tensor_tensor(moe[:, tl * DIM:(tl + 1) * DIM],
                                                                              ps_dn[(ud - 1) % 2][:], cc,
                                                                              moe[:, tl * DIM:(tl + 1) * DIM],
                                                                              ALU.mult, ALU.add)
                                        vDNE[ud] = producer(P, "vector", last)

                                for tl in range(8):
                                    qi = hf * 8 + tl
                                    pp = tl % 2
                                    if P.on("sync") and qi >= 2:
                                        e.wait_ge(SEM[f"yst{qi % 2}"], vYST[qi - 2])
                                    vXL = dma(P, "sync", f"x1l{pp}", x1l[pp][:], x1_hbm[qi * 128:(qi + 1) * 128, :])
                                    wait(P, "vector", f"x1l{pp}", vXL)
                                    last = None
                                    if P.on("vector"):
                                        s = sm4[pp]
                                        e.tensor_scalar(ysc[pp][:], moe[:, tl * DIM:(tl + 1) * DIM],
                                                        alpha128[:, 0:1], None, ALU.mult)
                                        e.scalar_tensor_tensor(ysc[pp][:], x1l[pp][:], 2.0, ysc[pp][:],
                                                               ALU.mult, ALU.add)
                                        e.reduce_sum(s[:, 0:1], ysc[pp][:], AX)
                                        e.tensor_mul(sq4, ysc[pp][:], ysc[pp][:])
                                        e.reduce_sum(s[:, 1:2], sq4, AX)
                                        e.tensor_scalar(s[:, 2:3], s[:, 0:1], float(-1.0 / DIM), None, ALU.mult)
                                        e.tensor_mul(s[:, 4:5], s[:, 2:3], s[:, 2:3])
                                        last = e.scalar_tensor_tensor(s[:, 3:4], s[:, 1:2], float(1.0 / DIM),
                                                                      s[:, 4:5], ALU.mult, ALU.subtract)
                                    vfs = producer(P, "vector", last)
                                    vFINv[qi] = vfs
                                    wait(P, "scalar", "ve", vfs)
                                    last = None
                                    if P.on("scalar"):
                                        s = sm4[pp]
                                        e.activation(s[:, 7:8], s[:, 3:4], AF.Ln, bias=float(EPS1), scale=1.0)
                                        last = e.activation(s[:, 5:6], s[:, 7:8], AF.Exp, bias=0.0, scale=-0.5)
                                    vfr = producer(P, "scalar", last)
                                    wait(P, "vector", "sc", vfr)
                                    last = None
                                    if P.on("vector"):
                                        last = e.tensor_scalar(ybuf[pp][:], ysc[pp][:], sm4[pp][:, 2:3],
                                                               sm4[pp][:, 5:6], ALU.add, ALU.mult)
                                    vyb = producer(P, "vector", last)
                                    vYBv[qi] = vyb
                                    wait(P, "sync", "ve", vyb)
                                    vYST[qi] = dma(P, "sync", f"yst{qi % 2}", y[qi * 128:(qi + 1) * 128, :], ybuf[pp][:])

                            if P.on("sync"):
                                e.wait_ge(SEM["yst0"], P.cnt("yst0"))
                                e.wait_ge(SEM["yst1"], P.cnt("yst1"))
                            P.eng.sem_inc(ph, 1)

                        run_block("p3b", emit_p3b)
        stOF.close()

    return nc


def kernel(**inp):
    f32 = np.float32
    x = np.asarray(inp["L0_attn_x"], f32)
    tr = lambda a: np.ascontiguousarray(np.asarray(a, f32).T)
    shared = {
        "wqT": tr(inp["L0_attn_Wq"]), "wkT": tr(inp["L0_attn_Wk"]),
        "wvT": tr(inp["L0_attn_Wv"]),
        "wqv": np.ascontiguousarray(np.asarray(inp["L0_attn_Wqv"], f32)),
        "wsT": tr(inp["L0_attn_Ws"]),
        "bqvT": np.ascontiguousarray(np.asarray(inp["L0_attn_bqv"], f32).reshape(H, QK).T),
        "wfT": tr(inp["L0_attn_Wf"]), "wr1T": tr(inp["L0_attn_Wr1"]), "wr2T": tr(inp["L0_attn_Wr2"]),
        "routerT": tr(inp["L0_ffn_router_weight"]),
        "guT": np.ascontiguousarray(np.asarray(inp["L0_ffn_gate_up_weight"], f32).transpose(0, 2, 1)),
        "dnT": np.ascontiguousarray(np.asarray(inp["L0_ffn_down_weight"], f32).transpose(0, 2, 1)),
        "bq": np.asarray(inp["L0_attn_bq"], f32), "bk": np.asarray(inp["L0_attn_bk"], f32),
        "bv": np.asarray(inp["L0_attn_bv"], f32),
        "bs": np.asarray(inp["L0_attn_bs"], f32).reshape(1),
        "bf": np.asarray(inp["L0_attn_bf"], f32),
        "br1": np.asarray(inp["L0_attn_br1"], f32), "br2": np.asarray(inp["L0_attn_br2"], f32),
        "g1": np.asarray(inp["L0_attn_g1"], f32), "b1": np.asarray(inp["L0_attn_b1"], f32),
        "g2": np.asarray(inp["L0_attn_g2"], f32), "b2": np.asarray(inp["L0_attn_b2"], f32),
        "nw": np.asarray(inp["L0_ffn_norm_weight"], f32),
        "alpha": np.asarray(inp["L0_ffn_residual_alpha"], f32).reshape(1),
    }
    in_maps = []
    for c in range(8):
        b = c // 4
        qo = (c % 4) * 2048
        xp = np.ascontiguousarray(
            np.concatenate([x[b, qo:qo + 2048], x[b, :qo], x[b, qo + 2048:]], axis=0))
        m = dict(shared)
        m["x_b"] = xp
        in_maps.append(m)

    nc = build_program()
    res = run_bass_kernel_spmd(nc, in_maps, core_ids=list(range(8)))
    out = np.empty((B, N, DIM), f32)
    for c in range(8):
        b = c // 4
        qo = (c % 4) * 2048
        out[b, qo:qo + 2048] = res.results[c]["y"]
    return out


# version 06
# speedup vs baseline: 1.2391x; 1.0457x over previous
import contextlib
import numpy as np
import concourse.bass as bass
import concourse.mybir as mybir
from concourse.bass_utils import run_bass_kernel_spmd

F32 = mybir.dt.float32
BF16 = mybir.dt.bfloat16
AF = mybir.ActivationFunctionType
ALU = mybir.AluOpType
AX = mybir.AxisListType.X

B, N, DIM = 2, 8192, 1024
H, HD, R, QK = 16, 64, 8, 32
HID = 2816
NE = 8
NT = 64
OWN_NT = 16
EPS1 = 1e-5
EPS2 = 1e-6
RS32 = float(1.0 / np.sqrt(32.0))

PHASES = 4
P0_STEPS = 99
P1_TILES = NT
DEBUG = {}

ENG = ("gpsimd", "vector", "scalar", "tensor", "sync")


CHAINED_OPS = frozenset([
    "memset", "memzero", "tensor_copy", "tensor_tensor", "tensor_add",
    "tensor_sub", "tensor_mul", "tensor_max", "tensor_scalar",
    "tensor_single_scalar", "tensor_scalar_add", "tensor_scalar_sub",
    "tensor_scalar_mul", "tensor_scalar_max", "tensor_scalar_min",
    "tensor_relu", "tensor_reduce", "reduce_max", "reduce_sum",
    "scalar_tensor_tensor", "reciprocal", "select", "copy_predicated",
    "bn_stats", "bn_aggr", "activation", "add", "mul", "copy", "square",
    "sqrt", "sign", "affine_select", "iota", "stream_shuffle",
])
_IN_KWS = ("bias", "scale", "alpha", "scalar1", "scalar2", "in_", "in0", "in1", "mask")


def _ap_name(a):
    try:
        return a.tensor.name
    except AttributeError:
        return None


class AutoChain:

    def __init__(self, eng, state, chain_sem):
        self._eng = eng
        self._st = state
        self._sem = chain_sem

    def note_update(self, inst, sem, val):
        rec = self._st["insts"].get(id(inst.ins))
        if rec is not None and rec[1] is None:
            self._st["insts"][id(inst.ins)] = (rec[0], (sem, val))

    def _edge(self, key):
        st = self._st
        rec = st["insts"].get(key)
        if rec is None:
            return
        idx, upd = rec
        if idx <= st["synced"]:
            return
        if upd is None:
            st["chain_n"] = st.get("chain_n", 0) + 1
            st["hinst"][key].then_inc(self._sem, 1)
            upd = (self._sem, st["chain_n"])
            st["insts"][key] = (idx, upd)
        self._eng.wait_ge(upd[0], upd[1])
        st["synced"] = max(st["synced"], idx)

    def __getattr__(self, name):
        fn = getattr(self._eng, name)
        if name not in CHAINED_OPS:
            return fn
        st = self._st

        def wrapped(*args, **kw):
            outs, ins = [], []
            seen_out = False
            for a in args:
                nm = _ap_name(a)
                if nm is None:
                    continue
                if not seen_out:
                    outs.append(nm)
                    seen_out = True
                else:
                    ins.append(nm)
            for k, v in kw.items():
                nm = _ap_name(v)
                if nm is None:
                    continue
                if k == "accum_out" or k == "out":
                    outs.append(nm)
                elif k in _IN_KWS:
                    ins.append(nm)
            for nm in ins:
                w = st["writer"].get(nm)
                if w is not None:
                    self._edge(w)
            for nm in outs:
                w = st["writer"].get(nm)
                if w is not None:
                    self._edge(w)
                for r in st["readers"].get(nm, ()):
                    self._edge(r)
            inst = fn(*args, **kw)
            st["idx"] = st.get("idx", 0) + 1
            key = id(inst.ins)
            st["insts"][key] = (st["idx"], None)
            st["hinst"][key] = inst
            for nm in outs:
                st["writer"][nm] = key
                st["readers"][nm] = []
            for nm in ins:
                st["readers"].setdefault(nm, []).append(key)
            return inst

        return wrapped


class Ctx:
    def __init__(self, eng, which, counts):
        self.eng = eng
        self.w = which
        self.n = counts

    def on(self, name):
        return self.w == name

    def bump(self, key, amt=1):
        self.n[key] = self.n.get(key, 0) + amt
        return self.n[key]

    def cnt(self, key):
        return self.n.get(key, 0)


def build_program():
    nc = bass.Bass("TRN2", target_bir_lowering=False)

    xb = nc.dram_tensor("x_b", [N, DIM], F32, kind="ExternalInput").ap()
    wqT = nc.dram_tensor("wqT", [DIM, DIM], F32, kind="ExternalInput").ap()
    wkT = nc.dram_tensor("wkT", [DIM, DIM], F32, kind="ExternalInput").ap()
    wvT = nc.dram_tensor("wvT", [DIM, R * HD], F32, kind="ExternalInput").ap()
    wqv = nc.dram_tensor("wqv", [QK * H, DIM], F32, kind="ExternalInput").ap()
    wsT = nc.dram_tensor("wsT", [HD + QK, 1], F32, kind="ExternalInput").ap()
    bqvT = nc.dram_tensor("bqvT", [QK, H], F32, kind="ExternalInput").ap()
    wfT = nc.dram_tensor("wfT", [DIM, DIM], F32, kind="ExternalInput").ap()
    wr1T = nc.dram_tensor("wr1T", [DIM, 2 * DIM], F32, kind="ExternalInput").ap()
    wr2T = nc.dram_tensor("wr2T", [2 * DIM, DIM], F32, kind="ExternalInput").ap()
    routerT = nc.dram_tensor("routerT", [DIM, NE], F32, kind="ExternalInput").ap()
    guT = nc.dram_tensor("guT", [NE, DIM, 2 * HID], F32, kind="ExternalInput").ap()
    dnT = nc.dram_tensor("dnT", [NE, HID, DIM], F32, kind="ExternalInput").ap()
    bq_d = nc.dram_tensor("bq", [DIM], F32, kind="ExternalInput").ap()
    bk_d = nc.dram_tensor("bk", [DIM], F32, kind="ExternalInput").ap()
    bv_d = nc.dram_tensor("bv", [R * HD], F32, kind="ExternalInput").ap()
    bs_d = nc.dram_tensor("bs", [1], F32, kind="ExternalInput").ap()
    bf_d = nc.dram_tensor("bf", [DIM], F32, kind="ExternalInput").ap()
    br1_d = nc.dram_tensor("br1", [2 * DIM], F32, kind="ExternalInput").ap()
    br2_d = nc.dram_tensor("br2", [DIM], F32, kind="ExternalInput").ap()
    g1_d = nc.dram_tensor("g1", [DIM], F32, kind="ExternalInput").ap()
    b1_d = nc.dram_tensor("b1", [DIM], F32, kind="ExternalInput").ap()
    g2_d = nc.dram_tensor("g2", [DIM], F32, kind="ExternalInput").ap()
    b2_d = nc.dram_tensor("b2", [DIM], F32, kind="ExternalInput").ap()
    nw_d = nc.dram_tensor("nw", [DIM], F32, kind="ExternalInput").ap()
    alpha_d = nc.dram_tensor("alpha", [1], F32, kind="ExternalInput").ap()

    y = nc.dram_tensor("y", [OWN_NT * 128, DIM], F32, kind="ExternalOutput").ap()
    kt_hbm = nc.dram_tensor("kt_scratch", [DIM, N], BF16, kind="Internal").ap()
    x1_hbm = nc.dram_tensor("x1_scratch", [OWN_NT * 128, DIM], F32, kind="Internal").ap()

    dbg = {}
    dbg_specs = {"kt": ([DIM, N], BF16), "v": ([128, NT * 512], BF16),
                 "qt": ([DIM, 2048], BF16), "qvd": ([128, OWN_NT * 16], F32),
                 "oft": ([DIM, 2048], BF16), "x1": ([OWN_NT * 128, DIM], F32),
                 "nrm": ([DIM, 2048], BF16), "cmb": ([128, OWN_NT * 8], F32)}
    for k_, (shape, dt) in dbg_specs.items():
        if DEBUG.get(k_):
            dbg[k_] = nc.dram_tensor("dbg_" + k_, shape, dt, kind="ExternalOutput").ap()

    CNTS = {w: {} for w in ENG}
    MS = {}

    ES = contextlib.ExitStack()
    with ES:
        def sbuf(name, shape, dt, side="left"):
            return ES.enter_context(nc.sbuf_tensor(name, shape, dt, side=side))

        s_pe = ES.enter_context(nc.semaphore("s_pe"))
        s_ve = ES.enter_context(nc.semaphore("s_ve"))
        s_sc = ES.enter_context(nc.semaphore("s_sc"))
        s_gp = ES.enter_context(nc.semaphore("s_gp"))
        ph = ES.enter_context(nc.semaphore("ph"))
        stOF = contextlib.ExitStack()
        PH = [0]
        SEM = {"pe": s_pe, "ve": s_ve, "sc": s_sc, "gp": s_gp}
        KEY = {"tensor": "pe", "vector": "ve", "scalar": "sc", "gpsimd": "gp"}
        DSTREAMS = ["dx0", "dx1", "dk0", "dk1", "dk2", "dk3", "kt0", "kt1",
                    "dx30", "dx31", "x1s0", "x1s1", "x1l0", "x1l1",
                    "yst0", "yst1", "dw", "dq0", "dq1", "dc",
                    "dgu0", "dgu1", "dn", "dbg", "oft0", "oft1"]
        for _s in DSTREAMS:
            SEM[_s] = ES.enter_context(nc.semaphore("sd_" + _s))

        CH_SEM = {w: ES.enter_context(nc.semaphore("ch_" + w))
                  for w in ("gpsimd", "vector", "scalar")}
        CH_ST = {w: {"idx": 0, "chain_n": 0, "synced": 0, "writer": {},
                     "readers": {}, "insts": {}, "hinst": {}}
                 for w in ("gpsimd", "vector", "scalar")}

        def mkeng(e, w):
            if w in CH_SEM:
                return AutoChain(e, CH_ST[w], CH_SEM[w])
            return e

        def run_block(name, emit):
            with nc.Block(name) as blk:
                blk.gpsimd(lambda e: emit(Ctx(mkeng(e, "gpsimd"), "gpsimd", CNTS["gpsimd"])))
                blk.vector(lambda e: emit(Ctx(mkeng(e, "vector"), "vector", CNTS["vector"])))
                blk.scalar(lambda e: emit(Ctx(mkeng(e, "scalar"), "scalar", CNTS["scalar"])))
                blk.tensor(lambda e: emit(Ctx(e, "tensor", CNTS["tensor"])))
                blk.sync(lambda e: emit(Ctx(e, "sync", CNTS["sync"])))
            PH[0] += 5

        def producer(P, engname, last_inst):
            k = KEY[engname]
            v = P.bump(k)
            if P.on(engname) and last_inst is not None:
                last_inst.then_inc(SEM[k], 1)
                if isinstance(P.eng, AutoChain):
                    P.eng.note_update(last_inst, SEM[k], v)
            return v

        def dma(P, qname, stream, dst, src):
            v = P.bump(stream, 16)
            if P.on(qname):
                P.eng.dma_start(dst, src).then_inc(SEM[stream], 16)
            return v

        def wait(P, engname, semkey, val):
            if P.on(engname) and val > 0:
                assert val <= P.cnt(semkey), f"future wait {semkey} {val}>{P.cnt(semkey)}"
                P.eng.wait_ge(SEM[semkey], val)

        def phase_begin(P):
            if PH[0] > 0:
                P.eng.wait_ge(ph, PH[0])

        def phase_end(P, extra=()):
            for semkey, v in extra:
                if v > 0:
                    P.eng.wait_ge(SEM[semkey], v)
            P.eng.sem_inc(ph, 1)

        ident = sbuf("ident", [128, 128], BF16, side="right")
        ident_f = sbuf("ident_f", [16, 16], F32, side="right")
        ones_row = sbuf("ones_row", [1, 128], BF16, side="right")
        ones_tok = sbuf("ones_tok", [1, 512], BF16, side="right")
        ones_col = sbuf("ones_col", [128, 1], BF16, side="right")
        ones_f = sbuf("ones_f", [1, 128], F32, side="right")
        wsv_bc = sbuf("wsv_bc", [128, 512], BF16, side="right")
        bs128 = sbuf("bs128", [128, 1], F32, side="right")
        alpha128 = sbuf("alpha128", [128, 1], F32, side="right")
        qvd_sb = sbuf("qvd", [128, OWN_NT * 16], F32, side="right")
        of_T = None

        ACT_CONSTS = [float(EPS1), 1.0, 0.0, -0.5, 0.125, RS32,
                      float(EPS2 * DIM), float(0.5 * np.log(DIM)), -1e9]
        consts_sb = sbuf("consts_sb", [128, len(ACT_CONSTS)], F32, side="right")
        for i_, v_ in enumerate(ACT_CONSTS):
            nc.const_aps.register(F32, v_, consts_sb[:, i_:i_ + 1])

        st12 = contextlib.ExitStack()
        with st12:
            def sb12(name, shape, dt):
                return st12.enter_context(nc.sbuf_tensor(name, shape, dt))

            v_all = sb12("v_all", [128, NT * 512], BF16)
            qt_sb = [sb12(f"qt{i}", [128, 2048], BF16) for i in range(8)]

            st01 = contextlib.ExitStack()
            with st01:
                def sb01(name, shape, dt):
                    return st01.enter_context(nc.sbuf_tensor(name, shape, dt))

                def ps01(name, shape, dt):
                    return st01.enter_context(nc.psum_tensor(name, shape, dt))

                wq_sb = sb01("wq_sb", [128, 8 * DIM], BF16)
                wk_sb = sb01("wk_sb", [128, 8 * DIM], BF16)
                wv_sb = sb01("wv_sb", [128, 8 * 512], BF16)
                wcT = sb01("wcT", [128, 8 * 16], BF16)
                bkR = sb01("bkR", [1, DIM], BF16)
                bqR = sb01("bqR", [1, DIM], BF16)
                bvR = sb01("bvR", [1, 512], BF16)
                g1bc = sb01("g1bc", [128, DIM], F32)
                b1bc = sb01("b1bc", [128, DIM], F32)
                bqvd = sb01("bqvd", [16, 1], F32)

                st00 = contextlib.ExitStack()
                st00.__enter__()

                def sb00(name, shape, dt):
                    return st00.enter_context(nc.sbuf_tensor(name, shape, dt))

                wsvrow = sb00("wsvrow", [1, 512], F32)
                wqv_blk = [sb00(f"wqv_blk{i}", [32, DIM], BF16) for i in range(2)]
                wsq_dup = sb00("wsq_dup", [64, 1], BF16)
                g1row = sb00("g1row", [1, DIM], F32)
                b1row = sb00("b1row", [1, DIM], F32)
                bsrow = sb00("bsrow", [1, 1], F32)
                alrow = sb00("alrow", [1, 1], F32)
                alclip = sb00("alclip", [1, 1], F32)
                wsvr64 = sb00("wsvr64", [1, 64], F32)
                wc_row = sb00("wc_row", [1, DIM], F32)
                wcomb16 = sb00("wcomb16", [16, DIM], BF16)
                bqvT_bf = sb00("bqvT_bf", [QK, H], BF16)
                bqv_ev = sb00("bqv_ev", [1, 16], F32)

                ps_a = st00.enter_context(nc.psum_tensor("ps_a", [128, 512], F32))
                ps_b = st00.enter_context(nc.psum_tensor("ps_b", [128, 512], F32))
                ps_c = st00.enter_context(nc.psum_tensor("ps_c", [128, 512], BF16))

                def emit_p0(P):
                    e = P.eng
                    phase_begin(P)
                    if P.on("gpsimd"):
                        e.memset(ident[:], 0.0)
                        e.memset(ident_f[:], 0.0)
                        e.affine_select(out=ident[:], in_=ident[:], compare_op=ALU.not_equal,
                                        fill=1.0, base=0, pattern=[[-1, 128]],
                                        channel_multiplier=1)
                        e.affine_select(out=ident_f[:], in_=ident_f[:], compare_op=ALU.not_equal,
                                        fill=1.0, base=0, pattern=[[-1, 16]],
                                        channel_multiplier=1)
                    if P.on("gpsimd"):
                        e.memset(ones_row[:], 1.0)
                        e.memset(ones_tok[:], 1.0)
                        e.memset(ones_col[:], 1.0)
                        for i_, v_ in enumerate(ACT_CONSTS):
                            e.memset(consts_sb[:, i_:i_ + 1], v_)
                    last = None
                    if P.on("gpsimd"):
                        last = e.memset(ones_f[:], 1.0)
                    MS["p0_const"] = producer(P, "gpsimd", last)

                    if P0_STEPS < 2:
                        P.eng.sem_inc(ph, 1)
                        return
                    dmas = []
                    for kk in range(8):
                        dmas.append((wk_sb[:, kk * DIM:(kk + 1) * DIM], wkT[kk * 128:(kk + 1) * 128, :]))
                        dmas.append((wq_sb[:, kk * DIM:(kk + 1) * DIM], wqT[kk * 128:(kk + 1) * 128, :]))
                        dmas.append((wv_sb[:, kk * 512:(kk + 1) * 512], wvT[kk * 128:(kk + 1) * 128, :]))
                    dmas += [(wsq_dup[0:32, :], wsT[0:32, :]),
                             (wsq_dup[32:64, :], wsT[0:32, :]),
                             (wsvr64[:], wsT[32:96, 0][None, :]),
                             (bqvT_bf[:], bqvT),
                             (bkR[:], bk_d[None, :]),
                             (bqR[:], bq_d[None, :]),
                             (bvR[:], bv_d[None, :]),
                             (g1row[:], g1_d[None, :]),
                             (b1row[:], b1_d[None, :]),
                             (bsrow[:], bs_d[None, :]),
                             (alrow[:], alpha_d[None, :])]
                    for dst, src in dmas:
                        MS["p0_dma"] = dma(P, "gpsimd", "dw", dst, src)

                    if P0_STEPS < 3:
                        if P.on("sync"):
                            P.eng.wait_ge(SEM["dw"], P.cnt("dw"))
                            P.eng.wait_ge(s_gp, P.cnt("gp"))
                        P.eng.sem_inc(ph, 1)
                        return
                    wait(P, "vector", "dw", MS["p0_dma"])
                    last = None
                    if P.on("vector"):
                        e.tensor_scalar(alclip[:], alrow[:], -0.5, 2.0, ALU.max, ALU.min)
                        for r_ in range(8):
                            e.tensor_copy(wsvrow[0:1, r_ * 64:(r_ + 1) * 64], wsvr64[:])
                        last = e.memset(wc_row[:], 0.0)
                    MS["p0_vsm"] = producer(P, "vector", last)

                    if P0_STEPS < 4:
                        if P.on("sync"):
                            P.eng.wait_ge(SEM["dw"], P.cnt("dw"))
                            P.eng.wait_ge(s_ve, P.cnt("ve"))
                        P.eng.sem_inc(ph, 1)
                        return
                    for half in range(2):
                        wait(P, "tensor", "dw", MS["p0_dma"])
                        wait(P, "tensor", "ve", MS["p0_vsm"] if half == 0 else MS["p0_gb"])
                        last = None
                        if P.on("tensor"):
                            e.matmul(ps_a[:], ones_f[:], g1row[0:1, half * 512:(half + 1) * 512],
                                     start=True, stop=True)
                            last = e.matmul(ps_b[:], ones_f[:], b1row[0:1, half * 512:(half + 1) * 512],
                                            start=True, stop=True)
                        vpe = producer(P, "tensor", last)
                        wait(P, "vector", "pe", vpe)
                        last = None
                        if P.on("vector"):
                            e.tensor_copy(g1bc[:, half * 512:(half + 1) * 512], ps_a[:])
                            last = e.tensor_copy(b1bc[:, half * 512:(half + 1) * 512], ps_b[:])
                        MS["p0_gb"] = producer(P, "vector", last)
                    wait(P, "tensor", "ve", MS["p0_gb"])
                    last = None
                    if P.on("tensor"):
                        e.matmul(ps_a[:], ones_f[:], wsvrow[:], start=True, stop=True)
                        e.matmul(ps_b[:, 0:1], ones_f[:], bsrow[:], start=True, stop=True)
                        last = e.matmul(ps_b[:, 2:3], ones_f[:], alclip[:], start=True, stop=True)
                    vpe = producer(P, "tensor", last)
                    wait(P, "vector", "pe", vpe)
                    last = None
                    if P.on("vector"):
                        e.tensor_copy(wsv_bc[:], ps_a[:])
                        e.tensor_copy(bs128[:], ps_b[:, 0:1])
                        last = e.tensor_copy(alpha128[:], ps_b[:, 2:3])
                    MS["p0_bc2"] = producer(P, "vector", last)

                    if P0_STEPS < 5:
                        if P.on("sync"):
                            P.eng.wait_ge(SEM["dw"], P.cnt("dw"))
                            P.eng.wait_ge(s_ve, P.cnt("ve"))
                        P.eng.sem_inc(ph, 1)
                        return
                    vWQM = {}
                    for h in range(16):
                        hb = h % 2
                        if P.on("gpsimd") and h >= 2:
                            e.wait_ge(s_pe, vWQM[h - 2])
                        vWQL = dma(P, "gpsimd", f"dq{hb}", wqv_blk[hb][:], wqv[h * 32:(h + 1) * 32, :])
                        wait(P, "tensor", "ve", P.cnt("ve"))
                        wait(P, "tensor", f"dq{hb}", vWQL)
                        last = None
                        if P.on("tensor"):
                            e.matmul(ps_b[0:1, 0:512], wsq_dup[0:32, :],
                                     wqv_blk[hb][:, 0:512], start=True, stop=True)
                            last = e.matmul(ps_a[0:1, 0:512], wsq_dup[0:32, :],
                                            wqv_blk[hb][:, 512:DIM], start=True, stop=True)
                        vpe = producer(P, "tensor", last)
                        vWQM[h] = vpe
                        wait(P, "vector", "pe", vpe)
                        if P.on("vector") and h >= 1:
                            e.wait_ge(SEM["dc"], P.cnt("dc"))
                        last = None
                        if P.on("vector"):
                            e.tensor_copy(wc_row[0:1, 0:512], ps_b[0:1, 0:512])
                            last = e.tensor_copy(wc_row[0:1, 512:DIM], ps_a[0:1, 0:512])
                        vve = producer(P, "vector", last)
                        wait(P, "gpsimd", "ve", vve)
                        MS["p0_wc"] = dma(P, "gpsimd", "dc", wcomb16[h:h + 1, :], wc_row[:])
                    if P0_STEPS < 6:
                        if P.on("sync"):
                            P.eng.wait_ge(SEM["dc"], P.cnt("dc"))
                            P.eng.wait_ge(s_ve, P.cnt("ve"))
                        P.eng.sem_inc(ph, 1)
                        return
                    for kk in range(8):
                        if kk == 0:
                            wait(P, "tensor", "dc", MS["p0_wc"])
                        else:
                            wait(P, "tensor", "ve", P.cnt("ve"))
                        last = None
                        if P.on("tensor"):
                            last = e.transpose(ps_c[:, 0:16], wcomb16[:, kk * 128:(kk + 1) * 128],
                                               ident[0:16, 0:16])
                        vpe = producer(P, "tensor", last)
                        wait(P, "vector", "pe", vpe)
                        last = None
                        if P.on("vector"):
                            last = e.tensor_copy(wcT[:, kk * 16:(kk + 1) * 16], ps_c[:, 0:16])
                        producer(P, "vector", last)
                    wait(P, "tensor", "ve", P.cnt("ve"))
                    last = None
                    if P.on("tensor"):
                        last = e.matmul(ps_a[0:1, 0:16], wsq_dup[0:32, :], bqvT_bf[0:32, :],
                                        start=True, stop=True)
                    vpe = producer(P, "tensor", last)
                    wait(P, "vector", "pe", vpe)
                    last = None
                    if P.on("vector"):
                        last = e.tensor_copy(bqv_ev[:], ps_a[0:1, 0:16])
                    vve = producer(P, "vector", last)
                    for h in range(16):
                        wait(P, "gpsimd", "ve", vve)
                        MS["p0_bqv"] = dma(P, "gpsimd", "dc", bqvd[h:h + 1, :], bqv_ev[0:1, h:h + 1])
                    wait(P, "vector", "dc", MS["p0_bqv"])
                    last = None
                    if P.on("vector"):
                        last = e.tensor_scalar(bqvd[:], bqvd[:], RS32, None, ALU.mult)
                    producer(P, "vector", last)
                    if P.on("sync"):
                        P.eng.wait_ge(s_ve, P.cnt("ve"))
                        for _s in ("dw", "dq", "dc"):
                            if P.cnt(_s) > 0:
                                P.eng.wait_ge(SEM[_s], P.cnt(_s))
                        P.eng.wait_ge(s_gp, P.cnt("gp"))
                    P.eng.sem_inc(ph, 1)

                run_block("p0", emit_p0)
                st00.__exit__(None, None, None)

                xbuf = [sb01(f"xbuf{i}", [128, DIM], F32) for i in range(2)]
                xnT = [sb01(f"xnT{k}", [128, 512], BF16) for k in range(8)]
                sq_t = sb01("sq_t", [128, DIM], F32)
                xc_t = [sb01(f"xc{i}", [128, DIM], F32) for i in range(2)]
                xg_t = sb01("xg_t", [128, DIM], F32)
                xn_bf = [sb01(f"xnbf{i}", [128, DIM], BF16) for i in range(2)]
                sm1 = [sb01(f"sm1{i}", [128, 8], F32) for i in range(2)]
                kst = [sb01(f"kst{i}", [128, 512], BF16) for i in range(4)]
                qvdT = sb01("qvdT", [16, 2048], F32)

                ps_tr = [ps01(f"ps_tr{i}", [128, 1024], BF16) for i in range(2)]
                ps_k = [ps01(f"ps_k{i}", [128, 512], F32) for i in range(2)]
                ps_v = [ps01(f"ps_v{i}", [128, 512], F32) for i in range(2)]
                ps_qv = ps01("ps_qv", [128, 512], F32)
                ps_sm = ps01("ps_sm", [128, 64], F32)

                def emit_p1(P):
                    e = P.eng
                    phase_begin(P)
                    vXN, vXC, vTR, vXNE, vVEV = {}, {}, {}, {}, {}
                    vKEV, vKSO, vQVE3, vQVS, vQVT = {}, {}, {}, {}, {}
                    pjg = {}

                    for kt in range(P1_TILES):
                        g, slot = kt // 4, kt % 4
                        own = kt < OWN_NT
                        pp = kt % 2

                        if P.on("sync") and kt >= 2:
                            e.wait_ge(s_ve, vXN[kt - 2])
                            e.wait_ge(s_sc, vXC[kt - 2])
                        vX = dma(P, "sync", f"dx{pp}", xbuf[pp][:], xb[kt * 128:(kt + 1) * 128, :])

                        wait(P, "vector", f"dx{pp}", vX)
                        last = None
                        if P.on("vector"):
                            s = sm1[pp]
                            e.reduce_sum(s[:, 0:1], xbuf[pp][:], AX)
                            e.tensor_mul(sq_t[:], xbuf[pp][:], xbuf[pp][:])
                            e.reduce_sum(s[:, 1:2], sq_t[:], AX)
                            e.tensor_scalar(s[:, 2:3], s[:, 0:1], float(-1.0 / DIM), None, ALU.mult)
                            e.tensor_mul(s[:, 4:5], s[:, 2:3], s[:, 2:3])
                            last = e.scalar_tensor_tensor(s[:, 3:4], s[:, 1:2], float(1.0 / DIM),
                                                          s[:, 4:5], ALU.mult, ALU.subtract)
                        vST = producer(P, "vector", last)
                        wait(P, "scalar", "ve", vST)
                        last = None
                        if P.on("scalar"):
                            s = sm1[pp]
                            e.activation(s[:, 7:8], s[:, 3:4], AF.Ln, bias=float(EPS1), scale=1.0)
                            last = e.activation(s[:, 5:6], s[:, 7:8], AF.Exp, bias=0.0, scale=-0.5)
                        vRS = producer(P, "scalar", last)
                        wait(P, "vector", "sc", vRS)
                        last = None
                        if P.on("vector"):
                            last = e.tensor_mul(sm1[pp][:, 6:7], sm1[pp][:, 2:3], sm1[pp][:, 5:6])
                        vNM = producer(P, "vector", last)
                        wait(P, "scalar", "ve", vNM)
                        if P.on("scalar") and kt >= 2:
                            e.wait_ge(s_ve, vXN[kt - 2])
                        last = None
                        if P.on("scalar"):
                            last = e.activation(xc_t[pp][:], xbuf[pp][:], AF.Identity,
                                                bias=sm1[pp][:, 6:7], scale=sm1[pp][:, 5:6])
                        vXC[kt] = producer(P, "scalar", last)
                        wait(P, "vector", "sc", vXC[kt])
                        if P.on("vector") and kt >= 2:
                            e.wait_ge(s_pe, vTR[kt - 2])
                        last = None
                        if P.on("vector"):
                            e.tensor_mul(xg_t[:], xc_t[pp][:], g1bc[:])
                            last = e.tensor_add(xn_bf[pp][:], xg_t[:], b1bc[:])
                        vXN[kt] = producer(P, "vector", last)

                        wait(P, "tensor", "ve", vXN[kt])
                        if P.on("tensor") and kt >= 2:
                            e.wait_ge(s_ve, vXNE[kt - 2])
                        last = None
                        if P.on("tensor"):
                            for kd in range(8):
                                last = e.transpose(ps_tr[pp][:, kd * 128:(kd + 1) * 128],
                                                   xn_bf[pp][:, kd * 128:(kd + 1) * 128], ident[:])
                        vTR[kt] = producer(P, "tensor", last)
                        wait(P, "vector", "pe", vTR[kt])
                        if P.on("vector") and g > 0 and slot == 0:
                            e.wait_ge(s_pe, pjg[g - 1])
                        last = None
                        if P.on("vector"):
                            for kd in range(8):
                                last = e.tensor_copy(xnT[kd][:, slot * 128:(slot + 1) * 128],
                                                     ps_tr[pp][:, kd * 128:(kd + 1) * 128])
                        vXNE[kt] = producer(P, "vector", last)

                        wait(P, "tensor", "ve", vXNE[kt])
                        if P.on("tensor") and kt >= 2:
                            e.wait_ge(s_ve, vVEV[kt - 2])
                        last = None
                        if P.on("tensor"):
                            for kk in range(8):
                                e.matmul(ps_v[pp][:], xnT[kk][:, slot * 128:(slot + 1) * 128],
                                         wv_sb[:, kk * 512:(kk + 1) * 512], start=(kk == 0), stop=False)
                            last = e.matmul(ps_v[pp][:], ones_row[:], bvR[:], start=False, stop=True)
                        vVP = producer(P, "tensor", last)
                        wait(P, "vector", "pe", vVP)
                        last = None
                        if P.on("vector"):
                            last = e.tensor_copy(v_all[:, kt * 512:(kt + 1) * 512], ps_v[pp][:])
                        vVEV[kt] = producer(P, "vector", last)

                        if own:
                            oi = kt
                            if P.on("tensor") and oi >= 1:
                                e.wait_ge(s_sc, vQVS[oi - 1])
                            last = None
                            if P.on("tensor"):
                                for kk in range(8):
                                    last = e.matmul(ps_qv[0:16, 0:128], wcT[:, kk * 16:(kk + 1) * 16],
                                                    xnT[kk][:, slot * 128:(slot + 1) * 128],
                                                    start=(kk == 0), stop=(kk == 7))
                            vQP = producer(P, "tensor", last)
                            wait(P, "scalar", "pe", vQP)
                            last = None
                            if P.on("scalar"):
                                last = e.activation(qvdT[:, oi * 128:(oi + 1) * 128], ps_qv[0:16, 0:128],
                                                    AF.Identity, bias=bqvd[:], scale=RS32)
                            vQVS[oi] = producer(P, "scalar", last)
                            wait(P, "tensor", "sc", vQVS[oi])
                            if P.on("tensor") and oi >= 1:
                                e.wait_ge(s_ve, vQVE3[oi - 1])
                            last = None
                            if P.on("tensor"):
                                last = e.transpose(ps_sm[:, 0:16], qvdT[0:16, oi * 128:(oi + 1) * 128],
                                                   ident_f[:])
                            vQVT[oi] = producer(P, "tensor", last)
                            wait(P, "vector", "pe", vQVT[oi])
                            last = None
                            if P.on("vector"):
                                last = e.tensor_copy(qvd_sb[:, oi * 16:(oi + 1) * 16], ps_sm[:, 0:16])
                            vQVE3[oi] = producer(P, "vector", last)

                        if slot == 3:
                            own_g = g < OWN_NT // 4
                            nchunks = 16 if own_g else 8
                            for ci in range(nchunks):
                                is_q = ci >= 8
                                mc = ci % 8
                                ck = P.bump("ckq")
                                kb = ck % 2
                                if P.on("tensor"):
                                    if ci == 0:
                                        e.wait_ge(s_ve, vXNE[kt])
                                    if ck >= 3:
                                        e.wait_ge(s_sc, vKEV[ck - 2])
                                last = None
                                if P.on("tensor"):
                                    wsrc = wq_sb if is_q else wk_sb
                                    bR = bqR if is_q else bkR
                                    for kk in range(8):
                                        e.matmul(ps_k[kb][:],
                                                 wsrc[:, kk * DIM + mc * 128:kk * DIM + (mc + 1) * 128],
                                                 xnT[kk][:], start=(kk == 0), stop=False)
                                    last = e.matmul(ps_k[kb][:], bR[0:1, mc * 128:(mc + 1) * 128],
                                                    ones_tok[:], start=False, stop=True)
                                vKP = producer(P, "tensor", last)
                                wait(P, "scalar", "pe", vKP)
                                last = None
                                if is_q:
                                    if P.on("scalar"):
                                        last = e.activation(qt_sb[mc][:, g * 512:(g + 1) * 512], ps_k[kb][:],
                                                            AF.Identity, bias=0.0, scale=0.125)
                                    vKEV[ck] = producer(P, "scalar", last)
                                else:
                                    ki = P.bump("ki") - 1
                                    if P.on("scalar") and ki >= 4:
                                        e.wait_ge(SEM[f"dk{ki % 4}"], vKSO[ki - 4])
                                    if P.on("scalar"):
                                        last = e.activation(kst[ki % 4][:], ps_k[kb][:], AF.Identity,
                                                            bias=0.0, scale=1.0)
                                    vKEV[ck] = producer(P, "scalar", last)
                                    wait(P, "sync", "sc", vKEV[ck])
                                    vKSO[ki] = dma(P, "sync", f"dk{ki % 4}",
                                                   kt_hbm[mc * 128:(mc + 1) * 128, g * 512:(g + 1) * 512],
                                                   kst[ki % 4][:])
                            pjg[g] = P.cnt("pe")

                    if P.on("sync"):
                        for _s in ("dx", "dk0", "dk1", "dk2", "dk3"):
                            if P.cnt(_s) > 0:
                                e.wait_ge(SEM[_s], P.cnt(_s))
                        e.wait_ge(s_ve, P.cnt("ve"))
                        e.wait_ge(s_sc, P.cnt("sc"))
                    P.eng.sem_inc(ph, 1)

                if PHASES >= 1:
                    run_block("p1", emit_p1)

                if dbg:
                    def emit_dbg1(P):
                        e = P.eng
                        phase_begin(P)
                        if "kt" in dbg:
                            dma(P, "sync", "dbg", dbg["kt"], kt_hbm)
                        if "v" in dbg:
                            dma(P, "sync", "dbg", dbg["v"], v_all[:])
                        if "qt" in dbg:
                            for i in range(8):
                                dma(P, "sync", "dbg", dbg["qt"][i * 128:(i + 1) * 128, :], qt_sb[i][:])
                        if "qvd" in dbg:
                            dma(P, "sync", "dbg", dbg["qvd"], qvd_sb[:])
                        if P.on("sync"):
                            e.wait_ge(SEM["dbg"], P.cnt("dbg"))
                        P.eng.sem_inc(ph, 1)

                    run_block("dbg1", emit_dbg1)

            of_T = [stOF.enter_context(nc.sbuf_tensor(f"ofT{i}", [128, 2048], BF16, side="right"))
                    for i in range(8)]

            if PHASES >= 2:
                st2 = contextlib.ExitStack()
                with st2:
                    def sb2(name, shape, dt):
                        return st2.enter_context(nc.sbuf_tensor(name, shape, dt))

                    def ps2(name, shape, dt):
                        return st2.enter_context(nc.psum_tensor(name, shape, dt))

                    ktb = [sb2(f"ktb{i}", [128, N], BF16) for i in range(2)]
                    PT = sb2("PT", [128, 2 * N], BF16)
                    on_bf = [sb2(f"onbf{i}", [128, 512], BF16) for i in range(2)]
                    tmp5 = [sb2(f"tmp5{i}", [128, 512], F32) for i in range(2)]
                    of_acc = [sb2(f"ofac{i}", [128, 64], F32) for i in range(2)]
                    of_b2 = [sb2(f"ofb2_{i}", [128, 128], BF16) for i in range(2)]
                    sm2 = [sb2(f"sm2{i}", [128, 32], F32) for i in range(2)]

                    spp = [ps2("spX", [128, 1024], F32), ps2("spY", [128, 1024], F32)]
                    av_ps = [ps2(f"av{i}", [128, 512], F32) for i in range(2)]
                    l_ps = [ps2(f"l{i}", [128, 8], F32) for i in range(2)]

                    NSTEP = 8 * 16
                    def emit_p2(P):
                        e = P.eng
                        phase_begin(P)
                        vKT, vSP, vEX, vAV, vAVg = {}, {}, {}, {}, {}
                        vCP, vCE, vCB, vOFT, vM1 = {}, {}, {}, {}, {}
                        sp_end = {}

                        def sbar_stage(T, st):
                            c2, qi2 = T // 16, T % 16
                            kb2 = c2 % 2
                            sp = spp[st % 2]
                            if P.on("tensor"):
                                if st == 0 and qi2 == 0:
                                    e.wait_ge(SEM[f"kt{c2 % 2}"], vKT[c2])
                                key = (T, st - 2) if st >= 2 else (T - 1, 14 + st)
                                if key in vEX:
                                    e.wait_ge(s_sc, vEX[key])
                            last = None
                            if P.on("tensor"):
                                for jj in range(4):
                                    jc = st * 4 + jj
                                    e.matmul(sp[:, jj * 128:(jj + 1) * 128],
                                             ktb[kb2][0:64, jc * 128:(jc + 1) * 128],
                                             qt_sb[c2][0:64, qi2 * 128:(qi2 + 1) * 128],
                                             start=True, stop=True, tile_position=(0, 0))
                                    last = e.matmul(sp[:, 512 + jj * 128:512 + (jj + 1) * 128],
                                                    ktb[kb2][64:128, jc * 128:(jc + 1) * 128],
                                                    qt_sb[c2][64:128, qi2 * 128:(qi2 + 1) * 128],
                                                    start=True, stop=True, tile_position=(64, 0))
                            vSP[(T, st)] = producer(P, "tensor", last)
                            if st == 15 and qi2 == 15:
                                sp_end[c2] = P.cnt("pe")
                            for hh in range(2):
                                if P.on("scalar"):
                                    if (T - 1, st) in vAVg:
                                        e.wait_ge(s_pe, vAVg[(T - 1, st)])
                                    e.wait_ge(s_pe, vSP[(T, st)])
                                last = None
                                if P.on("scalar"):
                                    last = e.activation(
                                        PT[:, hh * N + st * 512:hh * N + (st + 1) * 512],
                                        sp[:, hh * 512:(hh + 1) * 512], AF.Exp,
                                        bias=0.0, scale=1.0)
                                vEX[(T, st)] = producer(P, "scalar", last)

                        def act_comp(t):
                            for hh in range(2):
                                wait(P, "scalar", "ve", vCP[(t, hh)])
                                last = None
                                if P.on("scalar"):
                                    last = e.activation(sm2[hh][:, 16:24], sm2[hh][:, 8:16], AF.Exp,
                                                        bias=sm2[hh][:, 1:2], scale=1.0,
                                                        accum_out=sm2[hh][:, 2:3])
                                vCE[(t, hh)] = producer(P, "scalar", last)

                        def dve_tail(t):
                            ob = of_b2[t % 2]
                            for hh in range(2):
                                wait(P, "vector", "sc", vCE[(t, hh)])
                                if P.on("vector") and hh == 0 and t >= 2:
                                    e.wait_ge(SEM[f"oft{t % 2}"], vOFT[t - 2])
                                last = None
                                if P.on("vector"):
                                    s = sm2[hh]
                                    e.reciprocal(s[:, 3:4], s[:, 2:3])
                                    e.tensor_scalar(s[:, 24:32], s[:, 16:24], s[:, 3:4], None, ALU.mult)
                                    e.tensor_scalar(of_acc[hh][:], on_bf[hh][:, 0:64], s[:, 24:25],
                                                    None, ALU.mult)
                                    for r_ in range(1, 7):
                                        e.scalar_tensor_tensor(of_acc[hh][:],
                                                               on_bf[hh][:, r_ * 64:(r_ + 1) * 64],
                                                               s[:, 24 + r_:25 + r_], of_acc[hh][:],
                                                               ALU.mult, ALU.add)
                                    last = e.scalar_tensor_tensor(ob[:, hh * 64:(hh + 1) * 64],
                                                                  on_bf[hh][:, 448:512],
                                                                  s[:, 31:32], of_acc[hh][:],
                                                                  ALU.mult, ALU.add)
                                vCB[(t, hh)] = producer(P, "vector", last)
                            c2, qi2 = t // 16, t % 16
                            if P.on("sync"):
                                e.wait_ge(s_ve, vCB[(t, 1)])
                            v = P.bump(f"oft{t % 2}", 16)
                            if P.on("sync"):
                                e.dma_start(of_T[c2][:, qi2 * 128:(qi2 + 1) * 128], ob[:],
                                            transpose=True).then_inc(SEM[f"oft{t % 2}"], 16)
                            vOFT[t] = v

                        vKT[0] = dma(P, "sync", "kt0", ktb[0][:], kt_hbm[0:128, :])
                        for st in range(16):
                            sbar_stage(0, st)

                        for t in range(NSTEP):
                            c, qi = t // 16, t % 16
                            if qi == 0 and c < 7:
                                if P.on("sync") and c >= 1:
                                    e.wait_ge(s_pe, sp_end[c - 1])
                                vKT[c + 1] = dma(P, "sync", f"kt{(c + 1) % 2}", ktb[(c + 1) % 2][:],
                                                 kt_hbm[(c + 1) * 128:(c + 2) * 128, :])

                            if t >= 1:
                                act_comp(t - 1)
                                dve_tail(t - 1)

                            if P.on("tensor") and t >= 1:
                                e.wait_ge(s_ve, vM1[(t - 1, 1)])
                            for gg in range(16):
                                lasts = [None, None]
                                for hh in range(2):
                                    if P.on("tensor"):
                                        e.wait_ge(s_sc, vEX[(t, gg)])
                                        for jj in range(4):
                                            jc = gg * 4 + jj
                                            lhs = PT[:, hh * N + jc * 128:hh * N + (jc + 1) * 128]
                                            e.matmul(av_ps[hh][:], lhs, v_all[:, jc * 512:(jc + 1) * 512],
                                                     start=(jc == 0), stop=(jc == 63))
                                            lasts[hh] = e.matmul(l_ps[hh][:, 0:1], lhs, ones_col[:],
                                                                 start=(jc == 0), stop=(jc == 63))
                                vAVg[(t, gg)] = producer(P, "tensor", lasts[1])
                                if gg == 15:
                                    vAV[(t, 0)] = vAV[(t, 1)] = vAVg[(t, gg)]
                                if t + 1 < NSTEP:
                                    sbar_stage(t + 1, gg)

                            for hh in range(2):
                                wait(P, "vector", "pe", vAV[(t, hh)])
                                last = None
                                if P.on("vector"):
                                    s = sm2[hh]
                                    e.reciprocal(s[:, 0:1], l_ps[hh][:, 0:1])
                                    last = e.tensor_scalar(on_bf[hh][:], av_ps[hh][:], s[:, 0:1],
                                                           None, ALU.mult)
                                vM1[(t, hh)] = producer(P, "vector", last)
                            for hh in range(2):
                                h = 2 * c + hh
                                if P.on("vector") and t >= 1:
                                    e.wait_ge(s_sc, vCE[(t - 1, hh)])
                                last = None
                                if P.on("vector"):
                                    s = sm2[hh]
                                    e.tensor_mul(tmp5[hh][:], on_bf[hh][:], wsv_bc[:])
                                    for r_ in range(8):
                                        e.reduce_sum(s[:, 8 + r_:9 + r_],
                                                     tmp5[hh][:, r_ * 64:(r_ + 1) * 64], AX)
                                    e.tensor_scalar(s[:, 8:16], s[:, 8:16],
                                                    qvd_sb[:, qi * 16 + h:qi * 16 + h + 1],
                                                    bs128[:, 0:1], ALU.add, ALU.add)
                                    last = e.reduce_max(s[:, 1:2], s[:, 8:16], AX, negate=True)
                                vCP[(t, hh)] = producer(P, "vector", last)

                        act_comp(NSTEP - 1)
                        dve_tail(NSTEP - 1)

                        if P.on("sync"):
                            e.wait_ge(s_ve, P.cnt("ve"))
                            e.wait_ge(SEM["oft0"], P.cnt("oft0"))
                            e.wait_ge(SEM["oft1"], P.cnt("oft1"))
                        P.eng.sem_inc(ph, 1)
                    run_block("p2", emit_p2)

                    if dbg and "oft" in dbg:
                        def emit_dbg2(P):
                            e = P.eng
                            phase_begin(P)
                            for i in range(8):
                                dma(P, "sync", "dbg", dbg["oft"][i * 128:(i + 1) * 128, :], of_T[i][:])
                            if P.on("sync"):
                                e.wait_ge(SEM["dbg"], P.cnt("dbg"))
                            P.eng.sem_inc(ph, 1)

                        run_block("dbg2", emit_dbg2)
        if PHASES >= 3:
            st3 = contextlib.ExitStack()
            with st3:
                def sb3(name, shape, dt):
                    return st3.enter_context(nc.sbuf_tensor(name, shape, dt))

                st3a = contextlib.ExitStack()
                with st3a:
                    def sb3a(name, shape, dt):
                        return st3a.enter_context(nc.sbuf_tensor(name, shape, dt))

                    def ps3a(name, shape, dt):
                        return st3a.enter_context(nc.psum_tensor(name, shape, dt))

                    wf_sb = sb3a("wf_sb", [128, 8 * DIM], BF16)
                    wr1_sb = sb3a("wr1_sb", [128, 8 * 2048], BF16)
                    wr2_sb = sb3a("wr2_sb", [128, 16 * DIM], BF16)
                    g2bc = sb3a("g2bc", [128, DIM], F32)
                    b2bc = sb3a("b2bc", [128, DIM], F32)
                    bfR = sb3a("bfR", [1, DIM], BF16)
                    br1R = sb3a("br1R", [1, 2 * DIM], BF16)
                    br2R = sb3a("br2R", [1, DIM], BF16)
                    x3buf = [sb3a(f"x3b{i}", [128, DIM], F32) for i in range(2)]
                    row3 = x3buf[0][0:1, :]
                    out2b = [sb3a(f"out2b{i}", [128, DIM], F32) for i in range(2)]
                    xpo2b = [sb3a(f"xpo2b{i}", [128, DIM], F32) for i in range(4)]
                    on2bf = [sb3a(f"on2bf{i}", [128, DIM], BF16) for i in range(2)]
                    onT = [sb3a(f"onT{k}", [128, 512], BF16) for k in range(8)]
                    h1T = [sb3a(f"h1T{k}", [128, 512], BF16) for k in range(16)]
                    x1b = [sb3a(f"x1b{i}", [128, DIM], F32) for i in range(2)]
                    x1t = out2b
                    sq3 = sb3a("sq3", [128, DIM], F32)
                    sm3 = [sb3a(f"sm3{i}", [128, 16], F32) for i in range(2)]

                    ps_m = [ps3a(f"ps_m{i}", [128, 1024], F32) for i in range(2)]
                    ps_f = [ps3a(f"ps_f{i}", [128, 512], F32) for i in range(2)]
                    ps_t3 = [ps3a(f"ps_t3{i}", [128, 1024], BF16) for i in range(2)]

                    def emit_p3a1(P):
                        e = P.eng
                        phase_begin(P)
                        dmas = []
                        for kk in range(8):
                            dmas.append((wf_sb[:, kk * DIM:(kk + 1) * DIM], wfT[kk * 128:(kk + 1) * 128, :]))
                            dmas.append((wr1_sb[:, kk * 2048:(kk + 1) * 2048], wr1T[kk * 128:(kk + 1) * 128, :]))
                        for kk in range(16):
                            dmas.append((wr2_sb[:, kk * DIM:(kk + 1) * DIM], wr2T[kk * 128:(kk + 1) * 128, :]))
                        dmas += [(bfR[:], bf_d[None, :]), (br1R[:], br1_d[None, :]), (br2R[:], br2_d[None, :])]
                        vW3 = 0
                        for dst, src in dmas:
                            vW3 = dma(P, "gpsimd", "dw", dst, src)
                        vBCe = 0
                        for dsrc, bc in ((g2_d, g2bc), (b2_d, b2bc)):
                            vRL = dma(P, "gpsimd", "dc", row3, dsrc[None, :])
                            for half in range(2):
                                wait(P, "tensor", "dc", vRL)
                                wait(P, "tensor", "ve", vBCe)
                                last = None
                                if P.on("tensor"):
                                    last = e.matmul(ps_m[0][:, 0:512], ones_f[:],
                                                    x3buf[0][0:1, half * 512:(half + 1) * 512],
                                                    start=True, stop=True)
                                vBC = producer(P, "tensor", last)
                                wait(P, "vector", "pe", vBC)
                                last = None
                                if P.on("vector"):
                                    last = e.tensor_copy(bc[:, half * 512:(half + 1) * 512], ps_m[0][:, 0:512])
                                vBCe = producer(P, "vector", last)
                            if P.on("gpsimd"):
                                e.wait_ge(s_pe, P.cnt("pe"))

                        vO2, vLN2F, vTRA, vTEV = {}, {}, {}, {}
                        vX1, vX1S = {}, {}
                        pmEv = {0: vBCe}
                        pmU = [0]
                        pfEv = {}
                        pfU = [0]
                        ptRel = {}
                        ptU = [0]
                        f1g = {}
                        ffn2g = {}
                        h1ready = {}

                        for g3 in range(4):
                            for tl in range(4):
                                qi = g3 * 4 + tl
                                pp = qi % 2

                                if P.on("sync"):
                                    if qi >= 2:
                                        e.wait_ge(s_ve, vO2[qi - 2])
                                    else:
                                        e.wait_ge(s_ve, vBCe)
                                vX3 = dma(P, "sync", f"dx3{pp}", x3buf[pp][:], xb[qi * 128:(qi + 1) * 128, :])

                                pmU[0] += 1
                                u = pmU[0]
                                wait(P, "tensor", "dw", vW3)
                                if P.on("tensor"):
                                    e.wait_ge(s_ve, pmEv.get(u - 2, vBCe))
                                last = None
                                if P.on("tensor"):
                                    for half in range(2):
                                        for kk in range(8):
                                            e.matmul(ps_m[(u - 1) % 2][:, half * 512:(half + 1) * 512],
                                                     of_T[kk][:, qi * 128:(qi + 1) * 128],
                                                     wf_sb[:, kk * DIM + half * 512:kk * DIM + (half + 1) * 512],
                                                     start=(kk == 0), stop=False)
                                        last = e.matmul(ps_m[(u - 1) % 2][:, half * 512:(half + 1) * 512],
                                                        ones_row[:], bfR[0:1, half * 512:(half + 1) * 512],
                                                        start=False, stop=True)
                                vWF = producer(P, "tensor", last)
                                wait(P, "vector", "pe", vWF)
                                wait(P, "vector", f"dx3{pp}", vX3)
                                last = None
                                if P.on("vector"):
                                    e.tensor_add(out2b[pp][:], ps_m[(u - 1) % 2][:], x3buf[pp][:])
                                    last = e.tensor_add(xpo2b[tl][:], out2b[pp][:], x3buf[pp][:])
                                vO2[qi] = producer(P, "vector", last)
                                pmEv[u] = vO2[qi]

                                last = None
                                if P.on("vector"):
                                    s = sm3[pp]
                                    e.reduce_sum(s[:, 0:1], out2b[pp][:], AX)
                                    e.tensor_mul(sq3[:], out2b[pp][:], out2b[pp][:])
                                    e.reduce_sum(s[:, 1:2], sq3[:], AX)
                                    e.tensor_scalar(s[:, 2:3], s[:, 0:1], float(-1.0 / DIM), None, ALU.mult)
                                    e.tensor_mul(s[:, 4:5], s[:, 2:3], s[:, 2:3])
                                    last = e.scalar_tensor_tensor(s[:, 3:4], s[:, 1:2], float(1.0 / DIM),
                                                                  s[:, 4:5], ALU.mult, ALU.subtract)
                                vst = producer(P, "vector", last)
                                wait(P, "scalar", "ve", vst)
                                last = None
                                if P.on("scalar"):
                                    s = sm3[pp]
                                    e.activation(s[:, 7:8], s[:, 3:4], AF.Ln, bias=float(EPS1), scale=1.0)
                                    last = e.activation(s[:, 5:6], s[:, 7:8], AF.Exp, bias=0.0, scale=-0.5)
                                vrs = producer(P, "scalar", last)
                                wait(P, "vector", "sc", vrs)
                                if P.on("vector") and qi >= 2:
                                    e.wait_ge(s_pe, vTRA[qi - 2])
                                last = None
                                if P.on("vector"):
                                    e.tensor_scalar(sq3[:], out2b[pp][:], sm3[pp][:, 2:3],
                                                    sm3[pp][:, 5:6], ALU.add, ALU.mult)
                                    e.tensor_mul(sq3[:], sq3[:], g2bc[:])
                                    last = e.tensor_add(on2bf[pp][:], sq3[:], b2bc[:])
                                vLN2F[qi] = producer(P, "vector", last)

                                ptU[0] += 1
                                up = ptU[0]
                                wait(P, "tensor", "ve", vLN2F[qi])
                                if P.on("tensor") and up >= 3:
                                    e.wait_ge(s_ve, ptRel[up - 2])
                                last = None
                                if P.on("tensor"):
                                    for kd in range(8):
                                        last = e.transpose(ps_t3[(up - 1) % 2][:, kd * 128:(kd + 1) * 128],
                                                           on2bf[pp][:, kd * 128:(kd + 1) * 128], ident[:])
                                vTRA[qi] = producer(P, "tensor", last)
                                wait(P, "vector", "pe", vTRA[qi])
                                if P.on("vector") and g3 > 0:
                                    e.wait_ge(s_pe, f1g[g3 - 1])
                                last = None
                                if P.on("vector"):
                                    for kd in range(8):
                                        last = e.tensor_copy(onT[kd][:, tl * 128:(tl + 1) * 128],
                                                             ps_t3[(up - 1) % 2][:, kd * 128:(kd + 1) * 128])
                                vTEV[qi] = producer(P, "vector", last)
                                ptRel[up] = vTEV[qi]

                            for mc in range(16):
                                pfU[0] += 1
                                uf = pfU[0]
                                if P.on("tensor"):
                                    if mc == 0:
                                        e.wait_ge(s_ve, vTEV[g3 * 4 + 3])
                                    if uf >= 3:
                                        e.wait_ge(s_ve, pfEv[uf - 2])
                                last = None
                                if P.on("tensor"):
                                    for kk in range(8):
                                        e.matmul(ps_f[(uf - 1) % 2][:],
                                                 wr1_sb[:, kk * 2048 + mc * 128:kk * 2048 + (mc + 1) * 128],
                                                 onT[kk][:], start=(kk == 0), stop=False)
                                    last = e.matmul(ps_f[(uf - 1) % 2][:], br1R[0:1, mc * 128:(mc + 1) * 128],
                                                    ones_tok[:], start=False, stop=True)
                                vF1 = producer(P, "tensor", last)
                                wait(P, "vector", "pe", vF1)
                                if P.on("vector") and g3 > 0:
                                    e.wait_ge(s_pe, ffn2g[g3 - 1])
                                last = None
                                if P.on("vector"):
                                    last = e.tensor_scalar(h1T[mc][:], ps_f[(uf - 1) % 2][:],
                                                           0.0, None, ALU.max)
                                vF1E = producer(P, "vector", last)
                                pfEv[uf] = vF1E
                            h1ready[g3] = pfEv[pfU[0]]
                            f1g[g3] = P.cnt("pe")

                            for tl in range(4):
                                qi = g3 * 4 + tl
                                pp = qi % 2
                                pmU[0] += 1
                                u = pmU[0]
                                if P.on("tensor"):
                                    e.wait_ge(s_ve, h1ready[g3])
                                    if u >= 3:
                                        e.wait_ge(s_ve, pmEv[u - 2])
                                last = None
                                if P.on("tensor"):
                                    for half in range(2):
                                        for mc in range(16):
                                            e.matmul(ps_m[(u - 1) % 2][:, half * 512:(half + 1) * 512],
                                                     h1T[mc][:, tl * 128:(tl + 1) * 128],
                                                     wr2_sb[:, mc * DIM + half * 512:mc * DIM + (half + 1) * 512],
                                                     start=(mc == 0), stop=False)
                                        last = e.matmul(ps_m[(u - 1) % 2][:, half * 512:(half + 1) * 512],
                                                        ones_row[:], br2R[0:1, half * 512:(half + 1) * 512],
                                                        start=False, stop=True)
                                vFF2 = producer(P, "tensor", last)
                                if tl == 3:
                                    ffn2g[g3] = P.cnt("pe")

                                wait(P, "vector", "pe", vFF2)
                                last = None
                                if P.on("vector"):
                                    s = sm3[pp]
                                    e.tensor_add(x1t[pp][:], ps_m[(u - 1) % 2][:], xpo2b[tl][:])
                                    e.reduce_sum(s[:, 8:9], x1t[pp][:], AX)
                                    e.tensor_mul(sq3[:], x1t[pp][:], x1t[pp][:])
                                    e.reduce_sum(s[:, 9:10], sq3[:], AX)
                                    e.tensor_scalar(s[:, 10:11], s[:, 8:9], float(-1.0 / DIM), None, ALU.mult)
                                    e.tensor_mul(s[:, 12:13], s[:, 10:11], s[:, 10:11])
                                    last = e.scalar_tensor_tensor(s[:, 11:12], s[:, 9:10], float(1.0 / DIM),
                                                                  s[:, 12:13], ALU.mult, ALU.subtract)
                                vst2 = producer(P, "vector", last)
                                pmEv[u] = vst2
                                wait(P, "scalar", "ve", vst2)
                                last = None
                                if P.on("scalar"):
                                    s = sm3[pp]
                                    e.activation(s[:, 15:16], s[:, 11:12], AF.Ln, bias=float(EPS1), scale=1.0)
                                    last = e.activation(s[:, 13:14], s[:, 15:16], AF.Exp, bias=0.0, scale=-0.5)
                                vrs2 = producer(P, "scalar", last)
                                wait(P, "vector", "sc", vrs2)
                                if P.on("vector") and qi >= 2:
                                    e.wait_ge(SEM[f"x1s{qi % 2}"], vX1S[qi - 2])
                                last = None
                                if P.on("vector"):
                                    last = e.tensor_scalar(x1b[pp][:], x1t[pp][:], sm3[pp][:, 10:11],
                                                           sm3[pp][:, 13:14], ALU.add, ALU.mult)
                                vX1[qi] = producer(P, "vector", last)
                                wait(P, "sync", "ve", vX1[qi])
                                vX1S[qi] = dma(P, "sync", f"x1s{qi % 2}", x1_hbm[qi * 128:(qi + 1) * 128, :], x1b[pp][:])

                        if P.on("sync"):
                            for _s in ("dx30", "dx31", "x1s0", "x1s1"):
                                if P.cnt(_s) > 0:
                                    e.wait_ge(SEM[_s], P.cnt(_s))
                            e.wait_ge(s_ve, P.cnt("ve"))
                        P.eng.sem_inc(ph, 1)

                    run_block("p3a1", emit_p3a1)
                stOF.close()

                nrmT = [sb3(f"nrmT{k}", [128, 2048], BF16) for k in range(8)]
                cmb_sb = sb3("cmb_sb", [128, OWN_NT * 8], F32)

                st3r = contextlib.ExitStack()
                with st3r:
                    def sb3r(name, shape, dt):
                        return st3r.enter_context(nc.sbuf_tensor(name, shape, dt))

                    rt_sb = sb3r("rt_sb", [128, 64], BF16)
                    nwbc = sb3r("nwbc", [128, DIM], F32)
                    rowr = sb3r("rowr", [1, DIM], F32)
                    x1r = [sb3r(f"x1r{i}", [128, DIM], F32) for i in range(2)]
                    nbf = [sb3r(f"nbf{i}", [128, DIM], BF16) for i in range(2)]
                    sqr = sb3r("sqr", [128, DIM], F32)
                    rsm = [sb3r(f"rsm{i}", [128, 40], F32) for i in range(2)]

                    ps_rt = [st3r.enter_context(nc.psum_tensor(f"ps_rt{i}", [128, 512], F32)) for i in range(2)]
                    ps_tr2 = [st3r.enter_context(nc.psum_tensor(f"ps_tr2{i}", [128, 1024], BF16)) for i in range(2)]

                    def emit_p3a2(P):
                        e = P.eng
                        phase_begin(P)
                        vRL = 0
                        for kk in range(8):
                            vRL = dma(P, "gpsimd", "dw", rt_sb[:, kk * 8:(kk + 1) * 8],
                                      routerT[kk * 128:(kk + 1) * 128, :])
                        vRL2 = dma(P, "gpsimd", "dw", rowr[:], nw_d[None, :])
                        vBCe = 0
                        for half in range(2):
                            wait(P, "tensor", "dw", vRL2)
                            wait(P, "tensor", "ve", vBCe)
                            last = None
                            if P.on("tensor"):
                                last = e.matmul(ps_rt[0][:], ones_f[:], rowr[0:1, half * 512:(half + 1) * 512],
                                                start=True, stop=True)
                            vBC = producer(P, "tensor", last)
                            wait(P, "vector", "pe", vBC)
                            last = None
                            if P.on("vector"):
                                last = e.tensor_copy(nwbc[:, half * 512:(half + 1) * 512], ps_rt[0][:])
                            vBCe = producer(P, "vector", last)

                        vXR, vNBF, vRTE, vNTE = {}, {}, {}, {}
                        ptRel2 = {}
                        pfRel2 = {}
                        for qi in range(OWN_NT):
                            pp = qi % 2
                            if P.on("sync") and qi >= 2:
                                e.wait_ge(s_ve, vNBF[qi - 2])
                            vXR[qi] = dma(P, "sync", f"x1l{pp}", x1r[pp][:], x1_hbm[qi * 128:(qi + 1) * 128, :])

                            wait(P, "vector", f"x1l{pp}", vXR[qi])
                            last = None
                            if P.on("vector"):
                                e.tensor_mul(sqr[:], x1r[pp][:], x1r[pp][:])
                                last = e.reduce_sum(rsm[pp][:, 7:8], sqr[:], AX)
                            vr1 = producer(P, "vector", last)
                            wait(P, "scalar", "ve", vr1)
                            last = None
                            if P.on("scalar"):
                                e.activation(rsm[pp][:, 6:7], rsm[pp][:, 7:8], AF.Ln,
                                             bias=float(EPS2 * DIM), scale=1.0)
                                last = e.activation(rsm[pp][:, 5:6], rsm[pp][:, 6:7], AF.Exp,
                                                    bias=float(0.5 * np.log(DIM)), scale=-0.5)
                            vr2 = producer(P, "scalar", last)
                            wait(P, "vector", "sc", vr2)
                            if P.on("vector") and qi >= 2:
                                e.wait_ge(s_pe, vRTE[qi - 2])
                            last = None
                            if P.on("vector"):
                                last = e.scalar_tensor_tensor(nbf[pp][:], x1r[pp][:], rsm[pp][:, 5:6],
                                                              nwbc[:], ALU.mult, ALU.mult)
                            vNBF[qi] = producer(P, "vector", last)

                            wait(P, "tensor", "ve", vNBF[qi])
                            if P.on("tensor") and qi >= 2:
                                e.wait_ge(s_ve, ptRel2[qi - 2])
                            last = None
                            if P.on("tensor"):
                                for kd in range(8):
                                    last = e.transpose(ps_tr2[pp][:, kd * 128:(kd + 1) * 128],
                                                       nbf[pp][:, kd * 128:(kd + 1) * 128], ident[:])
                            vRTE[qi] = producer(P, "tensor", last)
                            wait(P, "vector", "pe", vRTE[qi])
                            last = None
                            if P.on("vector"):
                                for kd in range(8):
                                    last = e.tensor_copy(nrmT[kd][:, qi * 128:(qi + 1) * 128],
                                                         ps_tr2[pp][:, kd * 128:(kd + 1) * 128])
                            vNTE[qi] = producer(P, "vector", last)
                            ptRel2[qi] = vNTE[qi]

                            wait(P, "tensor", "ve", vNTE[qi])
                            if P.on("tensor") and qi >= 2:
                                e.wait_ge(s_sc, pfRel2[qi - 2])
                            last = None
                            if P.on("tensor"):
                                for kk in range(8):
                                    last = e.matmul(ps_rt[pp][:, 0:8],
                                                    nrmT[kk][:, qi * 128:(qi + 1) * 128],
                                                    rt_sb[:, kk * 8:(kk + 1) * 8],
                                                    start=(kk == 0), stop=(kk == 7))
                            vRT = producer(P, "tensor", last)
                            wait(P, "vector", "pe", vRT)
                            last = None
                            if P.on("vector"):
                                last = e.reduce_max(rsm[pp][:, 0:1], ps_rt[pp][:, 0:8], AX, negate=True)
                            vrt1 = producer(P, "vector", last)
                            wait(P, "scalar", "ve", vrt1)
                            last = None
                            if P.on("scalar"):
                                last = e.activation(rsm[pp][:, 8:16], ps_rt[pp][:, 0:8],
                                                    AF.Exp, bias=rsm[pp][:, 0:1],
                                                    accum_out=rsm[pp][:, 1:2])
                            vrt2 = producer(P, "scalar", last)
                            pfRel2[qi] = vrt2
                            wait(P, "vector", "sc", vrt2)
                            last = None
                            if P.on("vector"):
                                s = rsm[pp]
                                e.reciprocal(s[:, 2:3], s[:, 1:2])
                                e.tensor_scalar(s[:, 16:24], s[:, 8:16], s[:, 2:3], None, ALU.mult)
                                e.reduce_max(s[:, 3:4], s[:, 16:24], AX)
                                e.tensor_scalar(s[:, 24:32], s[:, 16:24], s[:, 3:4], None, ALU.is_equal)
                                e.scalar_tensor_tensor(s[:, 8:16], s[:, 24:32], -1e9, s[:, 16:24],
                                                       ALU.mult, ALU.add)
                                e.reduce_max(s[:, 4:5], s[:, 8:16], AX)
                                e.tensor_scalar(s[:, 32:40], s[:, 8:16], s[:, 4:5], None, ALU.is_equal)
                                e.tensor_add(s[:, 24:32], s[:, 24:32], s[:, 32:40])
                                e.tensor_add(s[:, 5:6], s[:, 3:4], s[:, 4:5])
                                e.reciprocal(s[:, 6:7], s[:, 5:6])
                                e.tensor_mul(s[:, 32:40], s[:, 16:24], s[:, 24:32])
                                last = e.tensor_scalar(cmb_sb[:, qi * 8:(qi + 1) * 8], s[:, 32:40],
                                                       s[:, 6:7], None, ALU.mult)
                            producer(P, "vector", last)

                        if P.on("sync"):
                            e.wait_ge(s_ve, P.cnt("ve"))
                        P.eng.sem_inc(ph, 1)

                    run_block("p3a2", emit_p3a2)

                    if dbg and ("x1" in dbg or "nrm" in dbg or "cmb" in dbg):
                        def emit_dbg3(P):
                            e = P.eng
                            phase_begin(P)
                            if "x1" in dbg:
                                dma(P, "sync", "dbg", dbg["x1"], x1_hbm)
                            if "nrm" in dbg:
                                for i in range(8):
                                    dma(P, "sync", "dbg", dbg["nrm"][i * 128:(i + 1) * 128, :], nrmT[i][:])
                            if "cmb" in dbg:
                                dma(P, "sync", "dbg", dbg["cmb"], cmb_sb[:])
                            if P.on("sync"):
                                e.wait_ge(SEM["dbg"], P.cnt("dbg"))
                            P.eng.sem_inc(ph, 1)

                        run_block("dbg3", emit_dbg3)

                if PHASES >= 4:
                    st3b = contextlib.ExitStack()
                    with st3b:
                        def sb3b(name, shape, dt):
                            return st3b.enter_context(nc.sbuf_tensor(name, shape, dt))

                        def ps3b(name, shape, dt):
                            return st3b.enter_context(nc.psum_tensor(name, shape, dt))

                        hid_sb = sb3b("hid_sb", [128, 22 * DIM], BF16)
                        dn_sb = sb3b("dn_sb", [128, 22 * DIM], BF16)
                        gu_g = [sb3b(f"gug{i}", [128, 2048], BF16) for i in range(2)]
                        gu_u = [sb3b(f"guu{i}", [128, 2048], BF16) for i in range(2)]
                        moe = sb3b("moe", [128, 8 * DIM], F32)
                        sil = [sb3b(f"sil{i}", [128, 512], F32) for i in range(2)]
                        x1l = [sb3b(f"x1l{i}", [128, DIM], F32) for i in range(2)]
                        ysc = [sb3b(f"ysc{i}", [128, DIM], F32) for i in range(2)]
                        ybuf = x1l
                        sq4 = hid_sb[:, 0:DIM]
                        sm4 = [sb3b(f"sm4{i}", [128, 8], F32) for i in range(2)]

                        ps_g = [ps3b(f"ps_g{i}", [128, 512], F32) for i in range(2)]
                        ps_u = [ps3b(f"ps_u{i}", [128, 512], F32) for i in range(2)]
                        ps_dn = [ps3b(f"ps_dn{i}", [128, 1024], F32) for i in range(2)]

                        def emit_p3b(P):
                            e = P.eng
                            phase_begin(P)
                            vGL, vUL, vDL = {}, {}, {}
                            vSIL, vHID, vDNE = {}, {}, {}
                            slabPE = {}
                            slabI = [0]
                            vDN_pe = {}
                            guU = [0]
                            dnU = [0]
                            vYST = {}
                            vFINv = {}
                            vYBv = {}

                            for hf in range(2):
                                for ee in range(NE):
                                    for hs in range(11):
                                        si = slabI[0]
                                        slabI[0] += 1
                                        gb = si % 2
                                        if P.on("gpsimd"):
                                            prev = slabPE.get(si - 2, 0)
                                            if prev > 0:
                                                e.wait_ge(s_pe, prev)
                                        for kk in range(8):
                                            vGL[(hf, ee, hs)] = dma(
                                                P, "gpsimd", f"dgu{gb}", gu_g[gb][:, kk * 256:(kk + 1) * 256],
                                                guT[ee][kk * 128:(kk + 1) * 128, hs * 256:(hs + 1) * 256])
                                        for kk in range(8):
                                            vUL[(hf, ee, hs)] = dma(
                                                P, "gpsimd", f"dgu{gb}", gu_u[gb][:, kk * 256:(kk + 1) * 256],
                                                guT[ee][kk * 128:(kk + 1) * 128,
                                                        HID + hs * 256:HID + (hs + 1) * 256])
                                        for mk in (2 * hs, 2 * hs + 1):
                                            if mk < 22:
                                                if P.on("gpsimd") and mk == 0 and (hf + ee) > 0:
                                                    e.wait_ge(s_pe, vDN_pe.get((hf, ee - 1),
                                                                               vDN_pe.get((hf - 1, NE - 1), 0)))
                                                vDL[(hf, ee)] = dma(P, "gpsimd", "dn",
                                                                    dn_sb[:, mk * DIM:(mk + 1) * DIM],
                                                                    dnT[ee][mk * 128:(mk + 1) * 128, :])
                                        for mc2 in range(2):
                                            for tc in range(2):
                                                guU[0] += 1
                                                ug = guU[0]
                                                wait(P, "tensor", f"dgu{gb}", vUL[(hf, ee, hs)])
                                                if P.on("tensor") and ug >= 3:
                                                    e.wait_ge(s_ve, vHID.get(ug - 2, 0))
                                                last = None
                                                if P.on("tensor"):
                                                    for kk in range(8):
                                                        e.matmul(ps_g[(ug - 1) % 2][:],
                                                                 gu_g[gb][:, kk * 256 + mc2 * 128:kk * 256 + (mc2 + 1) * 128],
                                                                 nrmT[kk][:, hf * 1024 + tc * 512:hf * 1024 + (tc + 1) * 512],
                                                                 start=(kk == 0), stop=(kk == 7))
                                                    for kk in range(8):
                                                        last = e.matmul(ps_u[(ug - 1) % 2][:],
                                                                        gu_u[gb][:, kk * 256 + mc2 * 128:kk * 256 + (mc2 + 1) * 128],
                                                                        nrmT[kk][:, hf * 1024 + tc * 512:hf * 1024 + (tc + 1) * 512],
                                                                        start=(kk == 0), stop=(kk == 7))
                                                vGUP = producer(P, "tensor", last)
                                                slabPE[si] = vGUP
                                                wait(P, "scalar", "pe", vGUP)
                                                if P.on("scalar") and ug >= 3:
                                                    e.wait_ge(s_ve, vHID.get(ug - 2, 0))
                                                last = None
                                                if P.on("scalar"):
                                                    last = e.activation(sil[(ug - 1) % 2][:],
                                                                        ps_g[(ug - 1) % 2][:], AF.Silu,
                                                                        bias=0.0, scale=1.0)
                                                vSIL[ug] = producer(P, "scalar", last)
                                                wait(P, "vector", "sc", vSIL[ug])
                                                if P.on("vector") and hs == 0 and mc2 == 0 and tc == 0:
                                                    pv = vDN_pe.get((hf, ee - 1),
                                                                    vDN_pe.get((hf - 1, NE - 1), 0))
                                                    if pv > 0:
                                                        e.wait_ge(s_pe, pv)
                                                mkc = 2 * hs + mc2
                                                last = None
                                                if P.on("vector"):
                                                    last = e.tensor_mul(
                                                        hid_sb[:, mkc * DIM + tc * 512:mkc * DIM + (tc + 1) * 512],
                                                        sil[(ug - 1) % 2][:], ps_u[(ug - 1) % 2][:])
                                                vHID[ug] = producer(P, "vector", last)
                                    hid_done = vHID[guU[0]]

                                    for tl in range(8):
                                        qi = hf * 8 + tl
                                        dnU[0] += 1
                                        ud = dnU[0]
                                        wait(P, "tensor", "ve", hid_done)
                                        wait(P, "tensor", "dn", vDL[(hf, ee)])
                                        if P.on("tensor") and ud >= 3:
                                            e.wait_ge(s_ve, vDNE.get(ud - 2, 0))
                                        last = None
                                        if P.on("tensor"):
                                            for half in range(2):
                                                for mk in range(22):
                                                    last = e.matmul(
                                                        ps_dn[(ud - 1) % 2][:, half * 512:(half + 1) * 512],
                                                        hid_sb[:, mk * DIM + tl * 128:mk * DIM + (tl + 1) * 128],
                                                        dn_sb[:, mk * DIM + half * 512:mk * DIM + (half + 1) * 512],
                                                        start=(mk == 0), stop=(mk == 21))
                                        vDN = producer(P, "tensor", last)
                                        if tl == 7:
                                            vDN_pe[(hf, ee)] = P.cnt("pe")
                                        wait(P, "vector", "pe", vDN)
                                        last = None
                                        if P.on("vector"):
                                            cc = cmb_sb[:, qi * 8 + ee:qi * 8 + ee + 1]
                                            if ee == 0:
                                                last = e.tensor_scalar(moe[:, tl * DIM:(tl + 1) * DIM],
                                                                       ps_dn[(ud - 1) % 2][:], cc, None, ALU.mult)
                                            else:
                                                last = e.scalar_tensor_tensor(moe[:, tl * DIM:(tl + 1) * DIM],
                                                                              ps_dn[(ud - 1) % 2][:], cc,
                                                                              moe[:, tl * DIM:(tl + 1) * DIM],
                                                                              ALU.mult, ALU.add)
                                        vDNE[ud] = producer(P, "vector", last)

                                for tl in range(8):
                                    qi = hf * 8 + tl
                                    pp = tl % 2
                                    if P.on("sync") and qi >= 2:
                                        e.wait_ge(SEM[f"yst{qi % 2}"], vYST[qi - 2])
                                    vXL = dma(P, "sync", f"x1l{pp}", x1l[pp][:], x1_hbm[qi * 128:(qi + 1) * 128, :])
                                    wait(P, "vector", f"x1l{pp}", vXL)
                                    last = None
                                    if P.on("vector"):
                                        s = sm4[pp]
                                        e.tensor_scalar(ysc[pp][:], moe[:, tl * DIM:(tl + 1) * DIM],
                                                        alpha128[:, 0:1], None, ALU.mult)
                                        e.scalar_tensor_tensor(ysc[pp][:], x1l[pp][:], 2.0, ysc[pp][:],
                                                               ALU.mult, ALU.add)
                                        e.reduce_sum(s[:, 0:1], ysc[pp][:], AX)
                                        e.tensor_mul(sq4, ysc[pp][:], ysc[pp][:])
                                        e.reduce_sum(s[:, 1:2], sq4, AX)
                                        e.tensor_scalar(s[:, 2:3], s[:, 0:1], float(-1.0 / DIM), None, ALU.mult)
                                        e.tensor_mul(s[:, 4:5], s[:, 2:3], s[:, 2:3])
                                        last = e.scalar_tensor_tensor(s[:, 3:4], s[:, 1:2], float(1.0 / DIM),
                                                                      s[:, 4:5], ALU.mult, ALU.subtract)
                                    vfs = producer(P, "vector", last)
                                    vFINv[qi] = vfs
                                    wait(P, "scalar", "ve", vfs)
                                    last = None
                                    if P.on("scalar"):
                                        s = sm4[pp]
                                        e.activation(s[:, 7:8], s[:, 3:4], AF.Ln, bias=float(EPS1), scale=1.0)
                                        last = e.activation(s[:, 5:6], s[:, 7:8], AF.Exp, bias=0.0, scale=-0.5)
                                    vfr = producer(P, "scalar", last)
                                    wait(P, "vector", "sc", vfr)
                                    last = None
                                    if P.on("vector"):
                                        last = e.tensor_scalar(ybuf[pp][:], ysc[pp][:], sm4[pp][:, 2:3],
                                                               sm4[pp][:, 5:6], ALU.add, ALU.mult)
                                    vyb = producer(P, "vector", last)
                                    vYBv[qi] = vyb
                                    wait(P, "sync", "ve", vyb)
                                    vYST[qi] = dma(P, "sync", f"yst{qi % 2}", y[qi * 128:(qi + 1) * 128, :], ybuf[pp][:])

                            if P.on("sync"):
                                e.wait_ge(SEM["yst0"], P.cnt("yst0"))
                                e.wait_ge(SEM["yst1"], P.cnt("yst1"))
                            P.eng.sem_inc(ph, 1)

                        run_block("p3b", emit_p3b)
        stOF.close()

    return nc


def kernel(**inp):
    f32 = np.float32
    x = np.asarray(inp["L0_attn_x"], f32)
    tr = lambda a: np.ascontiguousarray(np.asarray(a, f32).T)
    shared = {
        "wqT": tr(inp["L0_attn_Wq"]), "wkT": tr(inp["L0_attn_Wk"]),
        "wvT": tr(inp["L0_attn_Wv"]),
        "wqv": np.ascontiguousarray(np.asarray(inp["L0_attn_Wqv"], f32)),
        "wsT": tr(inp["L0_attn_Ws"]),
        "bqvT": np.ascontiguousarray(np.asarray(inp["L0_attn_bqv"], f32).reshape(H, QK).T),
        "wfT": tr(inp["L0_attn_Wf"]), "wr1T": tr(inp["L0_attn_Wr1"]), "wr2T": tr(inp["L0_attn_Wr2"]),
        "routerT": tr(inp["L0_ffn_router_weight"]),
        "guT": np.ascontiguousarray(np.asarray(inp["L0_ffn_gate_up_weight"], f32).transpose(0, 2, 1)),
        "dnT": np.ascontiguousarray(np.asarray(inp["L0_ffn_down_weight"], f32).transpose(0, 2, 1)),
        "bq": np.asarray(inp["L0_attn_bq"], f32), "bk": np.asarray(inp["L0_attn_bk"], f32),
        "bv": np.asarray(inp["L0_attn_bv"], f32),
        "bs": np.asarray(inp["L0_attn_bs"], f32).reshape(1),
        "bf": np.asarray(inp["L0_attn_bf"], f32),
        "br1": np.asarray(inp["L0_attn_br1"], f32), "br2": np.asarray(inp["L0_attn_br2"], f32),
        "g1": np.asarray(inp["L0_attn_g1"], f32), "b1": np.asarray(inp["L0_attn_b1"], f32),
        "g2": np.asarray(inp["L0_attn_g2"], f32), "b2": np.asarray(inp["L0_attn_b2"], f32),
        "nw": np.asarray(inp["L0_ffn_norm_weight"], f32),
        "alpha": np.asarray(inp["L0_ffn_residual_alpha"], f32).reshape(1),
    }
    in_maps = []
    for c in range(8):
        b = c // 4
        qo = (c % 4) * 2048
        xp = np.ascontiguousarray(
            np.concatenate([x[b, qo:qo + 2048], x[b, :qo], x[b, qo + 2048:]], axis=0))
        m = dict(shared)
        m["x_b"] = xp
        in_maps.append(m)

    nc = build_program()
    res = run_bass_kernel_spmd(nc, in_maps, core_ids=list(range(8)))
    out = np.empty((B, N, DIM), f32)
    for c in range(8):
        b = c // 4
        qo = (c % 4) * 2048
        out[b, qo:qo + 2048] = res.results[c]["y"]
    return out
